# version 04
# baseline (speedup 1.0000x reference)
import math
import numpy as np
import ml_dtypes

import concourse.bass as bass
import concourse.mybir as mybir
from concourse.ap import AP
from concourse.bass_isa import ReduceOp
from concourse import library_config
from concourse.bass_utils import run_bass_kernel_spmd

F32 = mybir.dt.float32
BF16 = mybir.dt.bfloat16
I8 = mybir.dt.int8
AF = mybir.ActivationFunctionType
ALU = mybir.AluOpType
AXL = mybir.AxisListType

B, L, C = 8, 2048, 2048
NH, HD, MB = 32, 64, 16
DH = 1024
IF = 8192
EPS = 1e-5
ROPE_THETA = 10000.0
CH4 = 256
NCH4 = L // CH4
SUB = 128

ENGINES = ("sync", "vector", "scalar", "tensor", "gpsimd")


def _np_bf16(x):
    return np.asarray(x).astype(ml_dtypes.bfloat16)


class Ctr:
    def __init__(self, h):
        self.h = h
        self.n = 0

    def inc(self, k):
        self.n += k
        return self.n


class KB:

    def __init__(self, nc):
        self.nc = nc
        self.ops = {e: [] for e in ENGINES}
        self._sems = {}

    def sem(self, name):
        if name not in self._sems:
            self._sems[name] = Ctr(self.nc.alloc_semaphore(name))
        return self._sems[name]

    def on(self, eng, fn):
        self.ops[eng].append(fn)

    def wait(self, eng, sem, val=None):
        v = sem.n if val is None else val
        self.on(eng, lambda e, s=sem.h, v=v: e.wait_ge(s, v))

    def dma(self, eng, out, in_, sem, slow=False):
        def fn(e, out=out, in_=in_, s=sem.h, slow=slow):
            e.dma_start(out=out, in_=in_,
                        allow_slow_non_contiguous=slow).then_inc(s, 16)
        self.on(eng, fn)
        return sem.inc(16)

    def run(self, eng, fn, sem=None):
        if sem is None:
            if eng in ("vector", "gpsimd"):
                def plain(e, fn=fn):
                    fn(e)
                    e.drain()
                self.on(eng, plain)
            else:
                self.on(eng, fn)
            return None

        def wrapped(e, fn=fn, s=sem.h):
            fn(e)
            e.drain().then_inc(s, 1)
        self.on(eng, wrapped)
        return sem.inc(1)

    def finalize(self):
        with self.nc.Block() as blk:
            for e in ENGINES:
                getattr(blk, e)(lambda eng, e=e: [f(eng) for f in self.ops[e]])


ARENA_TOP = 229344
ARENA_SZ = 98304
ARENA_BASE = ARENA_TOP - ARENA_SZ


class Arena:

    _DT = {F32: 4, BF16: 2, I8: 1}

    def __init__(self, nc, base=ARENA_BASE, top=ARENA_TOP):
        self.nc = nc
        self.base = base
        self.top = top
        self.off = base
        self.n = 0

    def reset(self):
        self.off = self.base

    def mark(self):
        return self.off

    def restore(self, m):
        self.off = m

    def alloc(self, name, shape, dtype):
        import functools, operator
        free_elems = functools.reduce(operator.mul, shape[1:], 1)
        sz = free_elems * self._DT[dtype]
        sz = (sz + 31) // 32 * 32
        assert self.off + sz <= self.top, f"arena overflow at {name}"
        self.n += 1
        t = self.nc.alloc_sbuf_tensor_at(f"{name}_{self.n}", shape, dtype,
                                         offset=self.off)
        self.off += sz
        return t



def build_program(stages=("p0", "p1", "p2", "p3", "p4"), debug=()):
    nc = bass.Bass("TRN2", target_bir_lowering=False, detect_race_conditions=False)
    kb = KB(nc)
    ar = Arena(nc)
    st = set(stages)
    dbg_names = set(debug)

    def barrier():
        for e in ENGINES:
            for s in list(kb._sems.values()):
                if s.n > 0:
                    kb.wait(e, s, s.n)

    dr = {}
    def din(name, shape):
        dr[name] = nc.dram_tensor(name, list(shape), F32, kind="ExternalInput")
        return dr[name]

    din("x", (L, C))
    din("q_w", (C, C)); din("k_w", (C, C)); din("v_w", (C, C)); din("o_w", (C, C))
    din("ln1_w", (C,)); din("ln1_b", (C,))
    din("ttt_lr_w", (NH, 1, C)); din("ttt_lr_b", (NH,))
    din("token_idx_off", (MB,))
    din("ttt_norm_w", (NH, HD)); din("ttt_norm_b", (NH, HD))
    din("W1", (NH, HD, HD)); din("b1", (NH, 1, HD))
    din("post_w", (C,)); din("post_b", (C,))
    din("ffn_ln_w", (C,)); din("ffn_ln_b", (C,))
    din("w1f", (C, DH)); din("b1f", (DH,))
    din("w2f", (DH, C)); din("b2f", (C,))
    din("w_gate", (IF, C)); din("w_up", (IF, C)); din("w_down", (C, IF))

    out_d = nc.dram_tensor("out", [L, C], F32, kind="ExternalOutput")

    hfm_d = nc.dram_tensor("hfm", [IF, L], F32, kind="Internal")
    x1c_st_d = nc.dram_tensor("x1c_st", [16, 128, C], BF16, kind="Internal")
    g_st_d = nc.dram_tensor("g_st", [16, 128, C], BF16, kind="Internal")
    wdT_st_d = nc.dram_tensor("wdT_st", [IF // 128, C // 128, 128, 128], I8, kind="Internal")
    qw_st_d = nc.dram_tensor("qw_st", [C, C], BF16, kind="Internal")
    kw_st_d = nc.dram_tensor("kw_st", [C, C], BF16, kind="Internal")
    vw_st_d = nc.dram_tensor("vw_st", [C, C], BF16, kind="Internal")
    ow_st_d = nc.dram_tensor("ow_st", [C, C], BF16, kind="Internal")
    w1f_st_d = nc.dram_tensor("w1f_st", [C, DH], BF16, kind="Internal")
    w2f_st_d = nc.dram_tensor("w2f_st", [DH, C], BF16, kind="Internal")

    dbg = {}

    def dbg_dram(name, shape, dtype=F32):
        if name in dbg_names:
            dbg[name] = nc.dram_tensor("dbg_" + name, list(shape), dtype,
                                       kind="ExternalOutput")
            return dbg[name]
        return None

    idx = np.arange(128)
    a_and = np.bitwise_and.outer(idx, idx)
    pc = np.zeros((128, 128), np.int64)
    for k in range(7):
        pc += (a_and >> k) & 1
    H128_np = np.where(pc % 2 == 0, 1.0, -1.0).astype(np.float32)
    H128_d = nc.inline_tensor(_np_bf16(H128_np), name="cH128")
    identbf_d = nc.inline_tensor(np.eye(128, dtype=ml_dtypes.bfloat16), name="cIdbf")
    identf_d = nc.inline_tensor(np.eye(128, dtype=np.float32), name="cIdf")
    onesbf_d = nc.inline_tensor(np.ones((128, 128), ml_dtypes.bfloat16), name="cOnesbf")
    onescolbf_d = nc.inline_tensor(np.ones((128, 1), ml_dtypes.bfloat16), name="cOnescolbf")
    onesrow_d = nc.inline_tensor(np.ones((1, 128), ml_dtypes.bfloat16), name="cOnesrowbf")
    onesf_d = nc.inline_tensor(np.ones((128, 128), np.float32), name="cOnesf")

    pos = (np.arange(128) % MB).astype(np.float32)
    invf = 1.0 / ROPE_THETA ** (np.arange(0, HD, 2, dtype=np.float32) / HD)
    fr = pos[:, None] * invf[None, :]
    emb = np.concatenate([fr, fr], axis=1)
    cos_d = nc.inline_tensor(np.cos(emb).astype(np.float32), name="cCos")
    sinp_d = nc.inline_tensor(
        np.concatenate([-np.sin(emb[:, :32]), np.sin(emb[:, 32:])], axis=1)
        .astype(np.float32), name="cSinp")

    mT = (np.arange(16)[:, None] <= np.arange(16)[None, :]).astype(np.float32)
    negmaskT_np = -np.tile(np.tile(mT, (4, 1)), (2, 16 // 16))
    nm = np.zeros((128, 256), np.float32)
    for blk in range(4):
        for j in range(16):
            for g in range(4):
                for ab in range(2):
                    nm[blk * 32 + j, g * 64 + ab * 16: g * 64 + ab * 16 + 16] = -mT[j]
    negmaskT_d = nc.inline_tensor(nm, name="cNegMaskT")
    invn_d = nc.inline_tensor((1.0 / np.arange(1, MB + 1, dtype=np.float32))
                              .reshape(1, MB), name="cInvN")
    J16 = np.zeros((16, 128), np.float32)
    for p in range(128):
        J16[p % 16, p] = 1.0
    J16_d = nc.inline_tensor(J16, name="cJ16")
    S16 = np.zeros((128, 16), np.float32)
    for p in range(128):
        if p % 32 < 16:
            S16[p, p % 32] = 1.0
    S16_d = nc.inline_tensor(S16, name="cS16")
    I8t = np.zeros((128, 8), np.float32)
    for p in range(128):
        if p % 32 < 8:
            I8t[p, p % 32] = 1.0
    I8s_d = nc.inline_tensor(I8t, name="cI8s")
    bandm_np = np.zeros((128, 8), np.float32)
    for p in range(128):
        bandm_np[p, p // 16] = 1.0
    bandm_d = nc.inline_tensor(bandm_np, name="cBandM")
    halfm_np = np.zeros((128, 2), np.float32)
    halfm_np[0:64, 0] = 1.0
    halfm_np[64:128, 1] = 1.0
    halfm_d = nc.inline_tensor(halfm_np, name="cHalfM")
    blkm_np = np.zeros((128, 4), np.float32)
    for p in range(128):
        if p % 32 < 16:
            blkm_np[p, p // 32] = 1.0
    blkm_d = nc.inline_tensor(blkm_np, name="cBlkM")
    I8s4_np = np.zeros((128, 32), np.float32)
    for bq in range(4):
        for cc in range(8):
            I8s4_np[bq * 32 + cc, bq * 8 + cc] = 1.0
    I8s4_d = nc.inline_tensor(I8s4_np, name="cI8s4")

    sD0 = kb.sem("d0")
    sC0 = kb.sem("c0")

    wlnB = nc.alloc_sbuf_tensor("wlnB", [128, C], F32)
    kb.dma("sync", wlnB[:], AP(dr["ln1_w"], 0, [[0, 128], [1, C]]), sD0)
    blnB = nc.alloc_sbuf_tensor("blnB", [128, C], F32)
    kb.dma("sync", blnB[:], AP(dr["ln1_b"], 0, [[0, 128], [1, C]]), sD0)

    wacc = nc.alloc_sbuf_tensor("wacc", [128, 4], F32)
    swv = nc.alloc_sbuf_tensor("swv", [128, 12], F32)
    onesq_f = nc.alloc_sbuf_tensor("onesq_f", [128, 128], F32)
    kb.dma("sync", onesq_f[:], onesf_d.ap(), sD0)
    pbS = nc.alloc_psum_tensor("pbS", [128, 512], F32)
    pbX = nc.alloc_psum_tensor("pbX", [128, 512], F32)
    t0 = kb.run("vector", lambda e: e.memset(wacc[:], 0.0), sC0)

    if "p0" in st:
        sWD = kb.sem("wsd")
        sWR = kb.sem("wsr")
        wsp = [ar.alloc(f"wsp{i}", [128, 2048], F32) for i in range(2)]
        wtmp = ar.alloc("wtmp", [128, 1], F32)
        kb.wait("vector", sC0, t0)
        panels = []
        for name, slot in (("w_gate", 0), ("w_up", 1), ("w_down", 2)):
            md = dr[name]
            rows, cols = md.shape
            for pi in range(rows // 128):
                for si in range(cols // 2048):
                    panels.append((md, slot, pi, si, cols))
        for i, (md, slot, pi, si, cols) in enumerate(panels):
            buf = wsp[i % 2]
            src = AP(md, pi * 128 * cols + si * 2048, [[cols, 128], [1, 2048]])
            if i >= 2:
                kb.wait("scalar", sWR, 2 * (i - 1))
            tl_ = kb.dma("scalar", buf[:], src, sWD)
            kb.wait("vector", sWD, tl_)
            kb.run("vector", lambda e, b_=buf: e.reduce_sum(
                wtmp[:], b_[:], axis=AXL.X, apply_absolute_value=True), sWR)
            kb.run("vector", lambda e, s_=slot: e.tensor_add(
                wacc[:, s_:s_ + 1], wacc[:, s_:s_ + 1], wtmp[:]), sWR)
        sM0 = kb.sem("m0")
        kb.wait("tensor", sWR, sWR.n)
        tm = kb.run("tensor", lambda e: e.matmul(
            pbS[:, 0:4], onesq_f[:], wacc[:], start=True, stop=True), sM0)
        kb.wait("vector", sM0, tm)
        for slot, nel in ((0, IF * C), (1, IF * C), (2, C * IF)):
            kb.run("vector", lambda e, s_=slot, n_=nel: e.tensor_scalar(
                swv[:, s_:s_ + 1], pbS[:, s_:s_ + 1], 1.0 / n_, EPS,
                ALU.mult, ALU.max), sC0)
            kb.run("vector", lambda e, s_=slot: e.tensor_scalar_mul(
                swv[:, 3 + s_:4 + s_], swv[:, s_:s_ + 1], -0.5), sC0)
            kb.run("vector", lambda e, s_=slot: e.tensor_scalar_mul(
                swv[:, 6 + s_:7 + s_], swv[:, s_:s_ + 1], 0.5), sC0)
            kb.run("vector", lambda e, s_=slot: e.tensor_scalar_mul(
                swv[:, 9 + s_:10 + s_], swv[:, s_:s_ + 1], 0.5), sC0)
        d = dbg_dram("swv", (128, 12))
        if d is not None:
            sDB = kb.sem("dbsw")
            kb.wait("gpsimd", sC0, sC0.n)
            kb.dma("gpsimd", d.ap(), swv[:], sDB)
            kb.wait("gpsimd", sDB, sDB.n)

    ctx = dict(locals())
    if "p1" in st:
        _emit_p1(ctx)
    if "p2" in st:
        _emit_p2(ctx)
    if "p3" in st:
        _emit_p3(ctx)
    if "p4" in st:
        _emit_p4(ctx)

    kb.finalize()
    return nc, dbg



def _emit_p1(ctx):
    nc, kb, dr = ctx["nc"], ctx["kb"], ctx["dr"]
    ar = ctx["ar"]
    pbS = ctx["pbS"]
    H128_d, identbf_d = ctx["H128_d"], ctx["identbf_d"]
    wlnB = ctx["wlnB"]
    blnB = ctx["blnB"]
    x1c_st_d = ctx["x1c_st_d"]
    dbg_dram = ctx["dbg_dram"]

    q1fm = nc.alloc_sbuf_tensor("q1fm", [128, 16, 2048], BF16)
    lam1 = nc.alloc_sbuf_tensor("lam1", [128, L], F32)
    mrstd = nc.alloc_sbuf_tensor("mrstd", [128, 16, 4], F32)
    ctx["q1fm"] = q1fm; ctx["lam1"] = lam1; ctx["mrstd"] = mrstd

    xch = [ar.alloc(f"xch{i}", [128, C], F32) for i in range(2)]
    xnb = ar.alloc("xnb", [128, C], F32)
    xbb = ar.alloc("xbb", [128, C], F32)
    x1cb = ar.alloc("x1cb", [128, C], BF16)
    xbh = ar.alloc("xbh", [128, C], BF16)
    xbl = ar.alloc("xbl", [128, C], BF16)
    hfmt = ar.alloc("hfmt", [128, 16, 128], BF16)
    lfmt = ar.alloc("lfmt", [128, 16, 128], BF16)
    x1cfm = ar.alloc("x1cfm", [128, 16, 128], BF16)
    f1sb = ar.alloc("f1sb", [128, 16, 128], F32)
    rmaxb = ar.alloc("rmaxb", [128, 128], F32)
    amaxb = ar.alloc("amaxb", [128, 128], F32)
    sc1b = ar.alloc("sc1b", [128, 128], F32)
    yq = ar.alloc("yq", [128, 4, 128], F32)
    mq = ar.alloc("mq", [128, 4, 128], F32)
    tin = ar.alloc("tinyP1", [128, 8], F32)

    H128s = nc.alloc_sbuf_tensor("H128s", [128, 128], BF16)
    idbfs = nc.alloc_sbuf_tensor("idbfs", [128, 128], BF16)
    sD0 = ctx["sD0"]
    kb.dma("sync", H128s[:], H128_d.ap(), sD0)
    kb.dma("sync", idbfs[:], identbf_d.ap(), sD0)

    pbT = nc.alloc_psum_tensor("pbT", [128, 512], BF16)
    pbF = nc.alloc_psum_tensor("pbF", [128, 512], F32)

    sX = kb.sem("p1x")
    sV = kb.sem("p1v")
    sA = kb.sem("p1a")
    sG = kb.sem("p1g")
    sT = kb.sem("p1t")
    sS = kb.sem("p1s")

    onesf_row = nc.alloc_sbuf_tensor("onesf_row", [1, 128], F32)
    amax_row = nc.alloc_sbuf_tensor("amax_row", [1, 128], F32)
    kb.dma("sync", onesf_row[:], AP(ctx["onesf_d"], 0, [[128, 1], [1, 128]]), ctx["sD0"])

    P1CUT = getattr(_emit_p1, "CUT", 99)
    for k in range(1 if P1CUT < 99 else 16):
        xb = xch[k % 2]
        if k >= 2:
            kb.wait("sync", sV, ctx[f"p1_xn_done_{k-2}"])
        tx = kb.dma("sync", xb[:], AP(dr["x"], k * 128 * C, [[C, 128], [1, C]]), sX)

        kb.wait("vector", sX, tx)
        kb.run("vector", lambda e, xb=xb: e.reduce_sum(
            tin[:, 5:6], xb[:], axis=AXL.X))
        kb.run("vector", lambda e, xb=xb: e.scalar_tensor_tensor(
            xbb[:], xb[:], 1.0, xb[:], ALU.mult, ALU.mult,
            accum_out=tin[:, 0:1]))
        kb.run("vector", lambda e: e.tensor_scalar_mul(
            tin[:, 6:7], tin[:, 5:6], 1.0 / C))
        kb.run("vector", lambda e: e.tensor_mul(tin[:, 2:3], tin[:, 6:7], tin[:, 6:7]))
        kb.run("vector", lambda e: e.scalar_tensor_tensor(
            tin[:, 7:8], tin[:, 0:1], 1.0 / C, tin[:, 2:3],
            ALU.mult, ALU.subtract))
        tvs1 = kb.run("vector", lambda e: e.tensor_scalar_add(
            tin[:, 3:4], tin[:, 7:8], 1e-5), sV)
        kb.wait("scalar", sV, tvs1)
        tas1 = kb.run("scalar", lambda e: e.activation(
            tin[:, 0:1], tin[:, 3:4], AF.Sqrt), sA)
        kb.wait("vector", sA, tas1)
        kb.run("vector", lambda e: e.reciprocal(tin[:, 1:2], tin[:, 0:1]))
        kb.run("vector", lambda e: e.tensor_mul(tin[:, 2:3], tin[:, 1:2], tin[:, 1:2]))
        kb.run("vector", lambda e: e.tensor_mul(tin[:, 2:3], tin[:, 2:3], tin[:, 7:8]))
        tvs2 = kb.run("vector", lambda e: e.tensor_scalar_add(
            tin[:, 2:3], tin[:, 2:3], 1e-5), sV)
        kb.wait("scalar", sV, tvs2)
        tas2 = kb.run("scalar", lambda e: e.activation(
            tin[:, 3:4], tin[:, 2:3], AF.Sqrt), sA)
        kb.wait("vector", sA, tas2)
        kb.run("vector", lambda e: e.reciprocal(tin[:, 4:5], tin[:, 3:4]))
        kb.run("vector", lambda e, k=k: e.tensor_copy(mrstd[:, k, 0:1], tin[:, 6:7]))
        kb.run("vector", lambda e, k=k: e.tensor_copy(mrstd[:, k, 1:2], tin[:, 1:2]))
        kb.run("vector", lambda e, k=k: e.tensor_copy(mrstd[:, k, 2:3], tin[:, 4:5]))
        kb.run("vector", lambda e, xb=xb: e.tensor_scalar(
            xnb[:], xb[:], tin[:, 6:7], tin[:, 1:2], ALU.subtract, ALU.mult))
        kb.run("vector", lambda e: e.scalar_tensor_tensor(
            xbb[:], xnb[:], tin[:, 4:5], wlnB[:], ALU.mult, ALU.mult))
        tv = kb.run("vector", lambda e: e.tensor_add(
            x1cb[:], xbb[:], blnB[:]), sV)
        ctx[f"p1_xn_done_{k}"] = tv
        if k == getattr(_emit_p1, "DBGK", -1):
            _p4tap(ctx, "p1xb", xb[:], (128, 2048), eng="vector")
            _p4tap(ctx, "p1tin", tin[:], (128, 8), eng="vector")
            _p4tap(ctx, "p1xnb", xnb[:], (128, 2048), eng="vector")
        if P1CUT <= 1:
            return

        cur, oth = xnb, xbb
        tglast = None
        for si, hh in enumerate((1, 2, 4, 8)):
            eng = "vector" if si % 2 == 0 else "gpsimd"
            if si == 1:
                kb.wait("gpsimd", sV, sV.n)
            if si == 2:
                kb.wait("vector", sG, sG.n)
            if si == 3:
                kb.wait("gpsimd", sV, sV.n)
            na = 16 // (2 * hh)
            ap_a = AP(cur, 0, [[C, 128], [2 * hh * 128, na], [128, hh], [1, 128]])
            ap_b = AP(cur, hh * 128, [[C, 128], [2 * hh * 128, na], [128, hh], [1, 128]])
            ap_pl = AP(oth, 0, [[C, 128], [2 * hh * 128, na], [128, hh], [1, 128]])
            ap_mi = AP(oth, hh * 128, [[C, 128], [2 * hh * 128, na], [128, hh], [1, 128]])
            ssem = sV if eng == "vector" else sG
            kb.run(eng, lambda e, a=ap_a, b_=ap_b, o=ap_pl: e.tensor_tensor(
                o, a, b_, ALU.add), ssem)
            tglast = kb.run(eng, lambda e, a=ap_a, b_=ap_b, o=ap_mi: e.tensor_tensor(
                o, a, b_, ALU.subtract), ssem)
            cur, oth = oth, cur
        kb.wait("vector", sG, sG.n)
        kb.run("vector", lambda e, c_=cur: e.tensor_copy(xbh[:], c_[:]))
        tv2 = kb.run("vector", lambda e, c_=cur: e.tensor_sub(xbl[:], c_[:], xbh[:]), sV)
        if k == getattr(_emit_p1, "DBGK", -1):
            _p4tap(ctx, "p1bfly", cur[:], (128, 2048), eng="vector")
            _p4tap(ctx, "p1x1cb", x1cb[:], (128, 2048), eng="vector")
        if P1CUT <= 2:
            return

        def emit_T_group(src_bf, dst_fm, wait_sem, wait_val, tag):
            for grp in range(4):
                for q in range(4):
                    ct = grp * 4 + q
                    if q == 0:
                        kb.wait("tensor", wait_sem, wait_val)
                        if (grp > 0 or tag > 0):
                            kb.wait("tensor", sA, sA.n)
                    kb.run("tensor", lambda e, ct=ct, q=q, s_=src_bf: e.transpose(
                        pbT[:, q * 128:(q + 1) * 128], s_[:, ct * 128:(ct + 1) * 128],
                        idbfs[:]), sT if q == 3 else None)
                kb.wait("scalar", sT, sT.n)
                kb.run("scalar", lambda e, grp=grp, d_=dst_fm: e.copy(
                    d_[:, grp * 4:(grp + 1) * 4, :], pbT[:]), sA)

        if k >= 1:
            kb.wait("tensor", sS, sS.n)
        emit_T_group(x1cb, x1cfm, sV, tv, 0)
        v_x1 = sA.n
        emit_T_group(xbh, hfmt, sV, tv2, 1)
        emit_T_group(xbl, lfmt, sV, tv2, 2)

        kb.wait("sync", sA, v_x1)
        kb.dma("sync", AP(x1c_st_d, k * 128 * C, [[C, 128], [1, C]]),
               x1cfm[:], sS)

        if P1CUT <= 3:
            return
        kb.wait("tensor", sA, sA.n)
        if k >= 1:
            kb.wait("scalar", sV, ctx.get("p1_q1_done", 0))
        for grp in range(4):
            for q in range(4):
                ct = grp * 4 + q
                kb.run("tensor", lambda e, ct=ct, q=q: e.matmul(
                    pbF[:, q * 128:(q + 1) * 128], H128s[:], hfmt[:, ct, :],
                    start=True, stop=False))
                kb.run("tensor", lambda e, ct=ct, q=q: e.matmul(
                    pbF[:, q * 128:(q + 1) * 128], H128s[:], lfmt[:, ct, :],
                    start=False, stop=True), sT if q == 3 else None)
            kb.wait("scalar", sT, sT.n)
            kb.run("scalar", lambda e, grp=grp: e.copy(
                f1sb[:, grp * 4:(grp + 1) * 4, :], pbF[:]), sA)
            if grp < 3:
                kb.wait("tensor", sA, sA.n)

        if k == getattr(_emit_p1, "DBGK", -1):
            _p4tap(ctx, "p1hfmt",
                   hfmt[:].rearrange("p a b -> p (a b)"), (128, 2048))
            _p4tap(ctx, "p1f1sb",
                   f1sb[:].rearrange("p a b -> p (a b)"), (128, 2048))
        if P1CUT <= 4:
            return
        kb.wait("vector", sA, sA.n)
        tv_r = kb.run("vector", lambda e: e.tensor_reduce(
            rmaxb[:], AP(f1sb, 0, [[2048, 128], [1, 128], [128, 16]]),
            axis=AXL.X, op=ALU.max, apply_absolute_value=True), sV)
        kb.wait("gpsimd", sV, tv_r)
        tg = kb.run("gpsimd", lambda e: e.tensor_reduce(
            amax_row[:], rmaxb[:], axis=AXL.C, op=ALU.max), sG)
        kb.wait("tensor", sG, tg)
        tm_r = kb.run("tensor", lambda e: e.matmul(
            pbS[:, 128:256], onesf_row[:], amax_row[:],
            start=True, stop=True), sT)
        kb.wait("vector", sT, tm_r)
        if P1CUT <= 5:
            return
        c8 = 1.0 / math.sqrt(2048.0)
        kb.run("vector", lambda e: e.tensor_scalar(
            amaxb[:], pbS[:, 128:256], c8, EPS, ALU.mult, ALU.max))
        kb.run("vector", lambda e, k=k: e.tensor_scalar_mul(
            lam1[:, k * 128:(k + 1) * 128], amaxb[:], 1.0 / 127.0))
        kb.run("vector", lambda e: e.reciprocal(sc1b[:], amaxb[:]))
        tv3 = kb.run("vector", lambda e: e.tensor_scalar_mul(
            sc1b[:], sc1b[:], 127.0 * c8), sV)

        if P1CUT <= 6:
            return
        for grp in range(4):
            sl = slice(grp * 4, (grp + 1) * 4)
            sc_ap = AP(sc1b, 0, [[128, 128], [0, 4], [1, 128]])
            kb.run("vector", lambda e, sl=sl, sc_ap=sc_ap: e.tensor_tensor(
                yq[:], f1sb[:, sl, :], sc_ap, ALU.mult))
            kb.run("vector", lambda e, k=k, sl=sl: e.tensor_scalar(
                q1fm[:, sl, k * 128:(k + 1) * 128],
                yq[:], 8388736.0, 8388608.0, ALU.add, ALU.subtract), sV)
        ctx["p1_q1_done"] = sV.n

    kb.wait("sync", sS, sS.n)
    ctx["sV"] = sV; ctx["sA"] = sA; ctx["sT"] = sT; ctx["sG"] = sG
    ctx["pbT"] = pbT; ctx["pbF"] = pbF
    ctx["idbfs"] = idbfs
    ctx["H128s"] = H128s
    ctx["onesf_row"] = onesf_row; ctx["amax_row"] = amax_row
    identf_s = nc.alloc_sbuf_tensor("identf_s", [128, 128], F32)
    kb.dma("sync", identf_s[:], ctx["identf_d"].ap(), ctx["sD0"])
    ctx["identf_s"] = identf_s

    d = dbg_dram("q1fm", (128, 16, 2048), BF16)
    if d is not None:
        sDB = kb.sem("dbq1")
        kb.wait("gpsimd", sV, sV.n)
        kb.dma("gpsimd", d.ap(), q1fm[:], sDB)
        kb.wait("gpsimd", sDB, sDB.n)
    d = dbg_dram("lam1", (128, L))
    if d is not None:
        sDB2 = kb.sem("dblam")
        kb.wait("gpsimd", sV, sV.n)
        kb.dma("gpsimd", d.ap(), lam1[:], sDB2)
        kb.wait("gpsimd", sDB2, sDB2.n)


def _emit_p2(ctx):
    nc, kb, dr = ctx["nc"], ctx["kb"], ctx["dr"]
    ar = ctx["ar"]; ar.reset()
    ctx["barrier"]()
    pbS, pbT, pbF = ctx["pbS"], ctx["pbT"], ctx["pbF"]
    q1fm, lam1 = ctx["q1fm"], ctx["lam1"]
    swv = ctx["swv"]
    hfm_d = ctx["hfm_d"]
    idbfs = ctx["idbfs"]
    onesbf_d = ctx["onesbf_d"]
    dbg_dram = ctx["dbg_dram"]

    pbGU = nc.alloc_psum_tensor("pbGU", [128, 2048], F32)
    ctx["pbGU"] = pbGU

    wpan2 = [ar.alloc(f"wpan2_{i}", [128, 2048], F32) for i in range(2)]
    wqT = ar.alloc("wqT", [128, 16, 128], BF16)
    sgA = ar.alloc("sgA", [128, 512], BF16)
    sgB = ar.alloc("sgB", [128, 512], BF16)
    silu_g = ar.alloc("silu_g", [128, 2048], F32)
    sg1 = ar.alloc("sg1", [128, 512], F32)
    hprime = ar.alloc("hprime", [128, 2048], F32)
    csg = ar.alloc("csg", [128, 2], F32)
    onescol_bf = nc.alloc_sbuf_tensor("onescol_bf", [128, 1], BF16)
    ctx["onescol_bf"] = onescol_bf
    kb.dma("sync", onescol_bf[:], ctx["onescolbf_d"].ap(), ctx["sD0"])

    sW2 = kb.sem("p2w")
    sT2 = kb.sem("p2t")
    sA2 = kb.sem("p2a")
    sV2 = kb.sem("p2v")
    sH2 = kb.sem("p2h")

    kb.wait("tensor", ctx["sV"], ctx["sV"].n)
    kb.wait("tensor", ctx["sA"], ctx["sA"].n)
    kb.wait("tensor", ctx["sC0"], ctx["sC0"].n)

    NP = getattr(_emit_p2, 'NP', IF // 128)
    for p in range(NP):
        for m, wname in ((0, "w_gate"), (1, "w_up")):
            buf = wpan2[(2 * p + m) % 2]
            if 2 * p + m >= 2:
                kb.wait("sync", sV2, ctx[f"p2_q_{2 * p + m - 2}"])
            md = dr[wname]
            tw = kb.dma("sync", buf[:], AP(md, p * 128 * C, [[C, 128], [1, C]]), sW2)
            kb.wait("tensor", sW2, tw)
            for grp in range(4):
                if not (p == 0 and m == 0 and grp == 0):
                    kb.wait("tensor", sA2, sA2.n)
                for q in range(4):
                    it = grp * 4 + q
                    kb.run("tensor", lambda e, it=it, q=q, b_=buf: e.transpose(
                        pbF[:, q * 128:(q + 1) * 128],
                        b_[:, it * 128:(it + 1) * 128], ctx["identf_s"][:]),
                        sT2 if q == 3 else None)
                kb.wait("scalar", sT2, sT2.n)
                kb.wait("scalar", sV2, sV2.n)
                kb.run("scalar", lambda e, m=m: e.activation(
                    sgA[:], pbF[:], AF.Sign, bias=swv[:, 3 + m:4 + m]))
                kb.run("scalar", lambda e, m=m: e.activation(
                    sgB[:], pbF[:], AF.Sign, bias=swv[:, 6 + m:7 + m]), sA2)
                kb.wait("vector", sA2, sA2.n)
                tq = kb.run("vector", lambda e, grp=grp: e.tensor_add(
                    wqT[:, grp * 4:(grp + 1) * 4, :], sgA[:], sgB[:]), sV2)
            ctx[f"p2_q_{2 * p + m}"] = tq
            kb.wait("tensor", sV2, tq)
            if m == 0 and p > 0:
                kb.wait("tensor", sH2, ctx[f"p2_h_{p - 1}"])
            for it in range(16):
                for tg in range(4):
                    kb.run("tensor", lambda e, it=it, tg=tg: e.matmul(
                        pbGU[:, tg * 512:(tg + 1) * 512], wqT[:, it, :],
                        q1fm[:, it, tg * 512:(tg + 1) * 512],
                        start=(it == 0), stop=(it == 15)))
                kb.run("tensor", lambda e, it=it, m=m: e.matmul(
                    pbS[:, 260 + m:261 + m], wqT[:, it, :], onescol_bf[:],
                    start=(it == 0), stop=(it == 15)),
                    sT2 if it == 15 else None)
            kb.wait("vector", sT2, sT2.n)
            kb.run("vector", lambda e, m=m: e.tensor_scalar_mul(
                csg[:, m:m + 1], pbS[:, 260 + m:261 + m], 128.0))
            if m == 0:
                for tg in range(4):
                    if tg > 0:
                        kb.wait("vector", sA2, sA2.n)
                    kb.run("vector", lambda e, tg=tg: e.scalar_tensor_tensor(
                        sg1[:], pbGU[:, tg * 512:(tg + 1) * 512], csg[:, 0:1],
                        lam1[:, tg * 512:(tg + 1) * 512],
                        ALU.subtract, ALU.mult), sV2)
                    kb.wait("scalar", sV2, sV2.n)
                    kb.run("scalar", lambda e, tg=tg: e.activation(
                        silu_g[:, tg * 512:(tg + 1) * 512], sg1[:], AF.Silu,
                        scale=swv[:, 9:10]), sA2)
                kb.wait("tensor", sA2, sA2.n)
            else:
                if p > 0:
                    kb.wait("vector", sH2, ctx[f"p2_h_{p - 1}"])
                for tg in range(4):
                    kb.run("vector", lambda e, tg=tg: e.scalar_tensor_tensor(
                        hprime[:, tg * 512:(tg + 1) * 512],
                        pbGU[:, tg * 512:(tg + 1) * 512], csg[:, 1:2],
                        silu_g[:, tg * 512:(tg + 1) * 512],
                        ALU.subtract, ALU.mult), sV2)
                kb.wait("gpsimd", sV2, sV2.n)
                th = kb.dma("gpsimd", AP(hfm_d, p * 128 * L, [[L, 128], [1, L]]),
                            hprime[:], sH2)
                ctx[f"p2_h_{p}"] = th

    kb.wait("gpsimd", sH2, sH2.n)
    d = dbg_dram("csg", (128, 2))
    if d is not None:
        sDB = kb.sem("dbcs")
        kb.dma("gpsimd", d.ap(), csg[:], sDB)
        kb.wait("gpsimd", sDB, sDB.n)



def _emit_p3(ctx):
    nc, kb, dr = ctx["nc"], ctx["kb"], ctx["dr"]
    ar = ctx["ar"]; ar.reset()
    ctx["barrier"]()
    pbS, pbT, pbF, pbGU = ctx["pbS"], ctx["pbT"], ctx["pbF"], ctx["pbGU"]
    q1fm, lam1, swv = ctx["q1fm"], ctx["lam1"], ctx["swv"]
    hfm_d, g_st_d, wdT_st_d = ctx["hfm_d"], ctx["g_st_d"], ctx["wdT_st_d"]
    H128s, identf_s, idbfs = ctx["H128s"], ctx["identf_s"], ctx["idbfs"]
    onesf_row = ctx["onesf_row"]
    onescol_bf = ctx["onescol_bf"]
    c13 = 1.0 / math.sqrt(float(IF))

    csd = nc.alloc_sbuf_tensor("csd", [128, 16], F32)
    lam2 = nc.alloc_sbuf_tensor("lam2", [128, 4, 512], F32)

    sW3 = kb.sem("p3w")
    sT3 = kb.sem("p3t")
    sA3 = kb.sem("p3a")
    sV3 = kb.sem("p3v")
    sG3 = kb.sem("p3g")
    sS3 = kb.sem("p3s")
    sH3 = kb.sem("p3h")
    sD3 = kb.sem("p3d")
    sGS = kb.sem("p3gs")

    wdp = [ar.alloc(f"wdp{i}", [128, 4096], F32) for i in range(2)]
    wqdT = ar.alloc("wqdT", [128, 4, 128], BF16)
    sdA = ar.alloc("sdA", [128, 512], BF16)
    sdB = ar.alloc("sdB", [128, 512], BF16)
    prev_stash = 0
    for cp in range(16):
        for half in range(2):
            if cp > 0:
                kb.wait("sync", sT3, ctx[f"p3a_T_{cp - 1}"])
            kb.dma("sync", wdp[half][:],
                   AP(dr["w_down"], cp * 128 * IF + half * 4096,
                      [[IF, 128], [1, 4096]]), sW3)
        kb.wait("tensor", sW3, sW3.n)
        for grp in range(16):
            if not (cp == 0 and grp == 0):
                kb.wait("tensor", sA3, sA3.n)
            for q in range(4):
                it = grp * 4 + q
                half, loc = divmod(it, 32)
                kb.run("tensor", lambda e, q=q, half=half, loc=loc: e.transpose(
                    pbF[:, q * 128:(q + 1) * 128],
                    wdp[half][:, loc * 128:(loc + 1) * 128], identf_s[:]),
                    sT3 if q == 3 else None)
            if grp == 15:
                ctx[f"p3a_T_{cp}"] = sT3.n
            kb.wait("scalar", sT3, sT3.n)
            kb.wait("scalar", sV3, sV3.n)
            kb.run("scalar", lambda e: e.activation(
                sdA[:], pbF[:], AF.Sign, bias=swv[:, 5:6]))
            kb.run("scalar", lambda e: e.activation(
                sdB[:], pbF[:], AF.Sign, bias=swv[:, 8:9]), sA3)
            kb.wait("vector", sA3, sA3.n)
            kb.wait("vector", sS3, prev_stash)
            tq = kb.run("vector", lambda e: e.tensor_add(
                wqdT[:], sdA[:], sdB[:]), sV3)
            kb.wait("tensor", sV3, tq)
            for q in range(4):
                kb.run("tensor", lambda e, q=q, grp=grp: e.matmul(
                    pbS[:, 300:301], wqdT[:, q, :], onescol_bf[:],
                    start=(grp == 0 and q == 0), stop=(grp == 15 and q == 3)),
                    sT3 if (grp == 15 and q == 3) else None)
            kb.wait("gpsimd", sV3, tq)
            for q in range(4):
                it = grp * 4 + q
                kb.dma("gpsimd",
                       AP(wdT_st_d, (cp * 64 + it) * 128 * 128,
                          [[128, 128], [1, 128]]),
                       wqdT[:, q, :], sS3)
            prev_stash = sS3.n
        kb.wait("vector", sT3, sT3.n)
        kb.run("vector", lambda e, cp=cp: e.tensor_scalar_mul(
            csd[:, cp:cp + 1], pbS[:, 300:301], 128.0), sV3)

    ar.reset()
    ctx["barrier"]()
    bflyA = ar.alloc("bflyA", [128, 8192], F32)
    bflyB = ar.alloc("bflyB", [128, 8192], F32)
    aA = nc.lookup_mloc(bflyA).addr
    aB = nc.lookup_mloc(bflyB).addr
    h2h = nc.alloc_sbuf_tensor_at("h2h", [128, 64, 128], BF16, offset=aA)
    h2l = nc.alloc_sbuf_tensor_at("h2l", [128, 64, 128], BF16, offset=aA + 16384)
    f2sb = nc.alloc_sbuf_tensor_at("f2sb", [128, 64, 128], F32, offset=aB)
    yq3 = ar.alloc("yq3", [128, 4, 128], F32)
    mq3 = ar.alloc("mq3", [128, 4, 128], F32)
    rmax3 = ar.alloc("rmax3", [128, 128], F32)
    amaxb3 = ar.alloc("amaxb3", [128, 128], F32)
    sc3b = ar.alloc("sc3b", [128, 128], F32)
    dtb3 = ar.alloc("dtb3", [128, 128], F32)
    am_row3 = ar.alloc("am_row3", [1, 128], F32)
    gsin = ar.alloc("gsin", [128, 512], F32)
    gfm = ar.alloc("gfm", [128, 512], BF16)
    gpc = ar.alloc("gpc", [128, 4, 128], BF16)
    wdt = [ar.alloc(f"wdt{i}", [128, 8, 128], BF16) for i in range(2)]
    q1addr = nc.lookup_mloc(q1fm).addr
    q2fm = nc.alloc_sbuf_tensor_at("q2fm", [128, 64, 512], BF16, offset=q1addr)

    for tc in range(4):
        for sub in range(4):
            t0 = tc * 512 + sub * 128
            if not (tc == 0 and sub == 0):
                kb.wait("sync", sV3, ctx["p3b_q2_done"])
            th = kb.dma("sync", bflyB[:],
                        AP(hfm_d, t0, [[L, 128], [128 * L, 64], [1, 128]]), sH3)
            cur, oth = bflyB, bflyA
            kb.wait("vector", sH3, th)
            kb.wait("gpsimd", sH3, th)
            for si, hh in enumerate((1, 2, 4, 8, 16, 32)):
                eng = "vector" if si % 2 == 0 else "gpsimd"
                osem = sV3 if eng == "vector" else sG3
                if si > 0:
                    prev = sG3 if si % 2 == 0 else sV3
                    kb.wait(eng, prev, prev.n)
                na = 64 // (2 * hh)
                dims = [[8192, 128], [2 * hh * 128, na], [128, hh], [1, 128]]
                kb.run(eng, lambda e, a=AP(cur, 0, dims), b_=AP(cur, hh * 128, dims),
                       o=AP(oth, 0, dims): e.tensor_tensor(o, a, b_, ALU.add), osem)
                kb.run(eng, lambda e, a=AP(cur, 0, dims), b_=AP(cur, hh * 128, dims),
                       o=AP(oth, hh * 128, dims): e.tensor_tensor(o, a, b_, ALU.subtract), osem)
                cur, oth = oth, cur
            kb.wait("vector", sG3, sG3.n)
            kb.run("vector", lambda e, c_=cur: e.tensor_copy(
                h2h[:].rearrange("p a b -> p (a b)"), c_[:]))
            tbf = kb.run("vector", lambda e, c_=cur: e.tensor_sub(
                h2l[:].rearrange("p a b -> p (a b)"), c_[:],
                h2h[:].rearrange("p a b -> p (a b)")), sV3)
            kb.wait("tensor", sV3, tbf)
            for grp in range(16):
                if grp > 0:
                    kb.wait("tensor", sA3, sA3.n)
                for q in range(4):
                    it = grp * 4 + q
                    kb.run("tensor", lambda e, it=it, q=q: e.matmul(
                        pbF[:, q * 128:(q + 1) * 128], H128s[:], h2h[:, it, :],
                        start=True, stop=False))
                    kb.run("tensor", lambda e, it=it, q=q: e.matmul(
                        pbF[:, q * 128:(q + 1) * 128], H128s[:], h2l[:, it, :],
                        start=False, stop=True), sT3 if q == 3 else None)
                kb.wait("scalar", sT3, sT3.n)
                kb.run("scalar", lambda e, grp=grp: e.copy(
                    f2sb[:, grp * 4:(grp + 1) * 4, :], pbF[:]), sA3)
            kb.wait("vector", sA3, sA3.n)
            tvr = kb.run("vector", lambda e: e.tensor_reduce(
                rmax3[:], AP(f2sb, 0, [[8192, 128], [1, 128], [128, 64]]),
                axis=AXL.X, op=ALU.max, apply_absolute_value=True), sV3)
            kb.wait("gpsimd", sV3, tvr)
            tg3 = kb.run("gpsimd", lambda e: e.tensor_reduce(
                am_row3[:], rmax3[:], axis=AXL.C, op=ALU.max), sG3)
            kb.wait("tensor", sG3, tg3)
            tm3 = kb.run("tensor", lambda e: e.matmul(
                pbS[:, 128:256], onesf_row[:], am_row3[:],
                start=True, stop=True), sT3)
            kb.wait("vector", sT3, tm3)
            kb.run("vector", lambda e, t0=t0: e.tensor_scalar(
                dtb3[:], lam1[:, t0:t0 + 128], swv[:, 10:11], 1.0,
                ALU.mult, ALU.mult))
            kb.run("vector", lambda e: e.tensor_mul(
                amaxb3[:], pbS[:, 128:256], dtb3[:]))
            kb.run("vector", lambda e: e.tensor_scalar(
                amaxb3[:], amaxb3[:], c13, EPS, ALU.mult, ALU.max))
            kb.run("vector", lambda e, tc=tc, sub=sub: e.tensor_scalar_mul(
                lam2[:, tc, sub * 128:(sub + 1) * 128], amaxb3[:], 1.0 / 127.0))
            kb.run("vector", lambda e: e.reciprocal(sc3b[:], amaxb3[:]))
            kb.run("vector", lambda e: e.tensor_mul(sc3b[:], sc3b[:], dtb3[:]))
            kb.run("vector", lambda e: e.tensor_scalar_mul(
                sc3b[:], sc3b[:], 127.0 * c13), sV3)
            for grp in range(16):
                isl = slice(grp * 4, (grp + 1) * 4)
                sc_ap = AP(sc3b, 0, [[128, 128], [0, 4], [1, 128]])
                kb.run("vector", lambda e, isl=isl, sc_ap=sc_ap: e.tensor_tensor(
                    yq3[:], f2sb[:, isl, :], sc_ap, ALU.mult))
                kb.run("vector", lambda e, grp=grp, sub=sub: e.tensor_scalar(
                    AP(q2fm, (grp * 4) * 512 + sub * 128,
                       [[64 * 512, 128], [512, 4], [1, 128]]),
                    yq3[:], 8388736.0, 8388608.0, ALU.add, ALU.subtract), sV3)
            ctx["p3b_q2_done"] = sV3.n

        kb.wait("tensor", sV3, sV3.n)
        mm_grp_done = {}
        for cp in range(16):
            qtr = cp % 4
            for g in range(8):
                if cp * 8 + g >= 2:
                    kb.wait("gpsimd", sT3, mm_grp_done[cp * 8 + g - 2])
                kb.dma("gpsimd", wdt[g % 2][:],
                       AP(wdT_st_d, (cp * 64 + g * 8) * 128 * 128,
                          [[128, 128], [128 * 128, 8], [1, 128]]), sD3)
                kb.wait("tensor", sD3, sD3.n)
                if g == 0 and cp >= 4:
                    kb.wait("tensor", sV3, ctx[f"p3b_gs_{tc}_{cp - 4}"])
                for j in range(8):
                    it = g * 8 + j
                    kb.run("tensor", lambda e, g=g, j=j, it=it, qtr=qtr: e.matmul(
                        pbGU[:, qtr * 512:(qtr + 1) * 512],
                        wdt[g % 2][:, j, :], q2fm[:, it, :],
                        start=(it == 0), stop=(it == 63)),
                        sT3 if j == 7 else None)
                mm_grp_done[cp * 8 + g] = sT3.n
            kb.wait("vector", sT3, sT3.n)
            if cp >= 1:
                kb.wait("vector", sA3, ctx[f"p3b_silu_{tc}_{cp - 1}"])
            tgs = kb.run("vector", lambda e, cp=cp, qtr=qtr, tc=tc: e.scalar_tensor_tensor(
                gsin[:], pbGU[:, qtr * 512:(qtr + 1) * 512], csd[:, cp:cp + 1],
                lam2[:, tc, :], ALU.subtract, ALU.mult), sV3)
            ctx[f"p3b_gs_{tc}_{cp}"] = tgs
            kb.wait("scalar", sV3, tgs)
            if cp >= 1:
                kb.wait("scalar", sT3, ctx[f"p3b_gT_{tc}_{cp - 1}"])
            tga = kb.run("scalar", lambda e: e.activation(
                gfm[:], gsin[:], AF.Silu, scale=swv[:, 11:12]), sA3)
            ctx[f"p3b_silu_{tc}_{cp}"] = tga
            kb.wait("tensor", sA3, tga)
            for sb_ in range(4):
                kb.run("tensor", lambda e, sb_=sb_: e.transpose(
                    pbT[:, sb_ * 128:(sb_ + 1) * 128],
                    gfm[:, sb_ * 128:(sb_ + 1) * 128], idbfs[:]),
                    sT3 if sb_ == 3 else None)
            ctx[f"p3b_gT_{tc}_{cp}"] = sT3.n
            kb.wait("scalar", sT3, sT3.n)
            kb.wait("scalar", sGS, sGS.n)
            kb.run("scalar", lambda e: e.copy(gpc[:], pbT[:]), sA3)
            kb.wait("sync", sA3, sA3.n)
            for sb_ in range(4):
                kb.dma("sync",
                       AP(g_st_d, (tc * 4 + sb_) * 128 * C + cp * 128,
                          [[C, 128], [1, 128]]),
                       gpc[:, sb_, :], sGS)

    kb.wait("sync", sGS, sGS.n)


def _emit_p4(ctx):
    nc, kb, dr = ctx["nc"], ctx["kb"], ctx["dr"]
    ar = ctx["ar"]; ar.reset()
    ctx["barrier"]()
    pbS, pbT, pbF, pbGU, pbX = (ctx["pbS"], ctx["pbT"], ctx["pbF"],
                                ctx["pbGU"], ctx["pbX"])
    lam1, swv, mrstd = ctx["lam1"], ctx["swv"], ctx["mrstd"]
    identf_s, idbfs = ctx["identf_s"], ctx["idbfs"]
    onesf_row = ctx["onesf_row"]
    sD0 = ctx["sD0"]
    c = ctx

    sW4 = kb.sem("p4w")
    sX4 = kb.sem("p4x")
    sT4 = kb.sem("p4t")
    sA4 = kb.sem("p4a")
    sV4 = kb.sem("p4v")
    sG4 = kb.sem("p4g")
    sO4 = kb.sem("p4o")
    sST = kb.sem("p4st")

    lwB = ar.alloc("lwBsc", [128, 512], F32)
    lw2B = ar.alloc("lw2Bsc", [128, 512], F32)
    lbB = ar.alloc("lbBsc", [128, 512], F32)
    b1B = ar.alloc("b1Bsc", [128, 512], F32)
    W1bd = ar.alloc("W1bd", [128, 2048], BF16)
    W1sb = ar.alloc("W1sb", [128, 1024], F32)
    negmtB = ar.alloc("negmtB", [128, 256], F32)
    tokB = ar.alloc("tokB", [128, 16], F32)
    ntokones = ar.alloc("ntokones", [128, 16], BF16)
    ntok15 = ar.alloc("ntok15", [128, 1], F32)
    ntok15m = ar.alloc("ntok15m", [128, 4], F32)
    ntokm4 = ar.alloc("ntokm4", [128, 4, 16], BF16)
    halfmS = ar.alloc("halfmS", [128, 2], F32)
    blkmS = ar.alloc("blkmS", [128, 4], F32)
    I8s4s = ar.alloc("I8s4s", [128, 32], F32)
    lrwT = ar.alloc("lrwT", [128, 16, 128], BF16)
    lrbC = ar.alloc("lrbC", [128, 1], F32)
    lbfmB = ar.alloc("lbfmB", [128, 16], F32)
    lwfmB = ar.alloc("lwfmB", [128, 16], F32)
    cosB = ar.alloc("cosB", [128, 64], F32)
    sinpB = ar.alloc("sinpB", [128, 64], F32)
    ncsrow = ar.alloc("ncsrow", [1, 2048], BF16)
    pbowB = ar.alloc("pbowB", [128, 2048], BF16)
    b1frepl = ar.alloc("b1frepl", [128, 1024], F32)
    b2frow = ar.alloc("b2frow", [1, 2048], BF16)
    I8sb = ar.alloc("I8sb", [128, 8], F32)
    J16s = ar.alloc("J16s", [16, 128], F32)
    S16s = ar.alloc("S16s", [128, 16], F32)
    bandmS = ar.alloc("bandmS", [128, 8], F32)
    onesrow_bf = ar.alloc("onesrow_bf", [1, 128], BF16)
    onescol_bf4 = ar.alloc("onescol_bf4", [128, 1], BF16)
    murow = ar.alloc("murow", [1, 128], BF16)
    negmtS = ar.alloc("negmtS", [128, 256], F32)
    pwcol = ar.alloc("pwcol", [128, 16], F32)
    fwcol = ar.alloc("fwcol", [128, 16], F32)
    fbcol = ar.alloc("fbcol", [128, 16], F32)
    pbcol = ar.alloc("pbcol", [128, 16], F32)
    q1addr = nc.lookup_mloc(ctx["q1fm"]).addr
    ar2 = Arena(nc, base=q1addr, top=q1addr + 65536)
    flat1 = ar2.alloc("flat1", [1, 3072], F32)
    flat2 = ar2.alloc("flat2", [1, 4096], F32)
    flat3 = ar2.alloc("flat3", [1, 2560], F32)

    kb.dma("sync", cosB[:], c["cos_d"].ap(), sD0)
    kb.dma("sync", sinpB[:], c["sinp_d"].ap(), sD0)
    kb.dma("sync", I8sb[:], c["I8s_d"].ap(), sD0)
    kb.dma("sync", J16s[:], c["J16_d"].ap(), sD0)
    kb.dma("sync", S16s[:], c["S16_d"].ap(), sD0)
    kb.dma("sync", bandmS[:], c["bandm_d"].ap(), sD0)
    kb.dma("sync", halfmS[:], c["halfm_d"].ap(), sD0)
    kb.dma("sync", blkmS[:], c["blkm_d"].ap(), sD0)
    kb.dma("sync", I8s4s[:], c["I8s4_d"].ap(), sD0)
    kb.dma("sync", onesrow_bf[:], c["onesrow_d"].ap(), sD0)
    kb.dma("sync", onescol_bf4[:], c["onescolbf_d"].ap(), sD0)
    kb.dma("sync", negmtS[:], c["negmaskT_d"].ap(), sD0)
    kb.dma("sync", pwcol[:], AP(dr["post_w"], 0, [[1, 128], [128, 16]]), sD0, slow=True)
    kb.dma("sync", fwcol[:], AP(dr["ffn_ln_w"], 0, [[1, 128], [128, 16]]), sD0, slow=True)
    kb.dma("sync", fbcol[:], AP(dr["ffn_ln_b"], 0, [[1, 128], [128, 16]]), sD0, slow=True)
    kb.dma("gpsimd", pbcol[:], AP(dr["post_b"], 0, [[1, 128], [128, 16]]), sG4, slow=True)
    kb.dma("sync", lbfmB[:], AP(dr["ttt_norm_b"], 0, [[1, 128], [128, 16]]), sD0, slow=True)
    kb.dma("sync", lwfmB[:], AP(dr["ttt_norm_w"], 0, [[1, 128], [128, 16]]), sD0, slow=True)
    tnwrow = flat1[:, 0:2048]
    tnbrow = flat2[:, 0:2048]
    b1row = flat3[:, 0:2048]
    tiorow = flat3[:, 2048:2064]
    invrow = flat3[:, 2064:2080]
    lrbrow = flat3[:, 2080:2112]
    tokrow = flat3[:, 2128:2144]
    lrbsp = flat3[:, 2176:2304]
    onecol_f = flat3[:, 2304:2305]
    b1fr_f = flat1[:, 2048:3072]
    b2fr_f = flat2[:, 2048:4096]
    kb.dma("sync", tnwrow, AP(dr["ttt_norm_w"], 0, [[2048, 1], [1, 2048]]), sD0)
    kb.dma("sync", tnbrow, AP(dr["ttt_norm_b"], 0, [[2048, 1], [1, 2048]]), sD0)
    kb.dma("sync", b1row, AP(dr["b1"], 0, [[2048, 1], [1, 2048]]), sD0)
    kb.dma("sync", tiorow, AP(dr["token_idx_off"], 0, [[16, 1], [1, 16]]), sD0)
    kb.dma("sync", invrow, c["invn_d"].ap(), sD0)
    kb.dma("sync", lrbrow, AP(dr["ttt_lr_b"], 0, [[32, 1], [1, 32]]), sD0)
    kb.dma("sync", b2fr_f, AP(dr["b2f"], 0, [[2048, 1], [1, 2048]]), sD0)
    kb.dma("sync", b1fr_f, AP(dr["b1f"], 0, [[1024, 1], [1, 1024]]), sD0)
    kb.wait("vector", sD0, sD0.n)
    kb.wait("vector", sG4, sG4.n)

    kb.run("vector", lambda e: e.tensor_add(tokrow, invrow, tiorow))
    tv = kb.run("vector", lambda e: e.tensor_scalar_max(tokrow, tokrow, 0.0), sV4)
    kb.wait("tensor", sV4, tv)
    tm = kb.run("tensor", lambda e: e.matmul(
        pbS[:, 0:16], onesf_row[:], tokrow, start=True, stop=True), sT4)
    kb.wait("vector", sT4, tm)
    kb.run("vector", lambda e: e.tensor_copy(tokB[:], pbS[:, 0:16]))
    kb.run("vector", lambda e: e.tensor_scalar_mul(
        ntokones[:], AP(tokB, 15, [[16, 128], [0, 16]]), -1.0))
    kb.run("vector", lambda e: e.tensor_scalar_mul(
        ntok15[:], tokB[:, 15:16], -1.0))
    kb.run("vector", lambda e: e.tensor_scalar_mul(
        ntok15m[:], blkmS[:], ntok15[:]))
    for blk in range(4):
        kb.run("vector", lambda e, blk=blk: e.tensor_scalar_mul(
            ntokm4[:, blk, :], ntokones[:], blkmS[:, blk:blk + 1]))
    kb.run("vector", lambda e: e.tensor_tensor(
        negmtB[:], negmtS[:], AP(tokB, 0, [[16, 128], [0, 16], [1, 16]]),
        ALU.mult))
    kb.run("vector", lambda e: e.memset(lrbsp, 0.0))
    kb.run("vector", lambda e: e.memset(onecol_f, 1.0))
    tv2 = kb.run("vector", lambda e: e.tensor_copy(
        AP(lrbsp.tensor, lrbsp.offset, [[2560, 1], [2, 4], [32, 4], [1, 2]]),
        AP(lrbrow.tensor, lrbrow.offset, [[2560, 1], [8, 4], [2, 4], [1, 2]])), sV4)
    kb.wait("tensor", sV4, tv2)
    tm2 = kb.run("tensor", lambda e: e.matmul(
        pbS[:, 16:17], lrbsp, onecol_f, start=True, stop=True), sT4)
    kb.wait("vector", sT4, tm2)
    kb.run("vector", lambda e: e.tensor_copy(lrbC[:], pbS[:, 16:17]), sV4)

    kb.wait("tensor", sV4, sV4.n)
    targets = [(tnwrow, pbF, 0), (tnbrow, pbGU, 0), (b1row, pbGU, 512)]
    for srow, ps, off in targets:
        for blk in range(4):
            rsz = srow.tensor.shape[1]
            rhs = AP(srow.tensor, srow.offset + blk * 128,
                     [[rsz, 1], [512, 4], [64, 2], [1, 64]])
            kb.run("tensor", lambda e, blk=blk, rhs=rhs, ps=ps, off=off: e.matmul(
                ps[blk * 32:blk * 32 + 32, off:off + 512],
                onesf_row[0:1, 0:32], rhs, start=True, stop=True,
                tile_position=(0, blk * 32)),
                sT4 if blk == 3 else None)
    kb.wait("vector", sT4, sT4.n)
    kb.run("vector", lambda e: e.tensor_copy(lwB[:], pbF[:, 0:512]))
    kb.run("vector", lambda e: e.tensor_copy(lbB[:], pbGU[:, 0:512]))
    kb.run("vector", lambda e: e.tensor_copy(b1B[:], pbGU[:, 512:1024]))
    tv3 = kb.run("vector", lambda e: e.tensor_mul(lw2B[:], lwB[:], lwB[:]), sV4)

    P4CUT = getattr(_emit_p4, "CUT", 99)
    ctx["P4CUT"] = P4CUT
    if P4CUT <= 1:
        return
    lrw_s = ar2.alloc("lrw_s", [32, 2048], BF16)
    kb.dma("gpsimd", lrw_s[:], AP(dr["ttt_lr_w"], 0, [[2048, 32], [1, 2048]]), sG4)
    kb.wait("tensor", sG4, sG4.n)
    kb.wait("vector", sV4, sV4.n)
    kb.run("vector", lambda e: e.memset(
        lrwT[:].rearrange("p a b -> p (a b)"), 0.0))
    zbT = ar2.alloc("zbT", [128, 128], BF16)
    tv4 = kb.run("vector", lambda e: e.memset(zbT[:], 0.0), sV4)
    kb.wait("tensor", sV4, tv4)
    kb.wait("tensor", sD0, sD0.n)
    for zi in range(4):
        kb.run("tensor", lambda e, zi=zi: e.transpose(
            pbT[:, zi * 128:(zi + 1) * 128], zbT[:], idbfs[:]))
    for kc in range(16):
        tt = kb.run("tensor", lambda e, kc=kc: e.transpose(
            pbT[:, 0:32], lrw_s[:, kc * 128:(kc + 1) * 128],
            idbfs[0:32, 0:32]), sT4)
        kb.wait("vector", sT4, tt)
        tv5 = kb.run("vector", lambda e, kc=kc: e.tensor_copy(
            AP(lrwT, kc * 128, [[16 * 128, 128], [2, 4], [32, 4], [1, 2]]),
            AP(pbT, 0, [[512, 128], [8, 4], [2, 4], [1, 2]])), sV4)
        kb.wait("tensor", sV4, tv5)

    _stash_mark = ar.mark()
    wsp4 = [ar.alloc(f"wsp4_{i}", [128, 2048], F32) for i in range(2)]
    wsb4 = [ar.alloc(f"wsb4_{i}", [128, 2048], BF16) for i in range(2)]
    prev_cast = {}
    jobs = ([("q_w", c["qw_st_d"], pi, None) for pi in range(16)]
            + [("k_w", c["kw_st_d"], pi, None) for pi in range(16)]
            + [("v_w", c["vw_st_d"], pi, None) for pi in range(16)]
            + [("o_w", c["ow_st_d"], pi, "pw") for pi in range(16)]
            + [("w1f", c["w1f_st_d"], pi, "fw") for pi in range(16)]
            + [("w2f", c["w2f_st_d"], pi, None) for pi in range(8)])
    prev_mmT = {}
    for ji, (wname, dst, pi, fold) in enumerate(jobs):
        cols = dr[wname].shape[1]
        buf, cbuf = wsp4[ji % 2], wsb4[ji % 2]
        if ji >= 2:
            kb.wait("sync", sST, prev_cast[ji - 2])
            if (ji - 2) in prev_mmT:
                kb.wait("sync", sT4, prev_mmT[ji - 2])
                kb.wait("vector", sT4, prev_mmT[ji - 2])
        kb.dma("sync", buf[:, 0:cols],
               AP(dr[wname], pi * 128 * cols, [[cols, 128], [1, cols]]), sX4)
        kb.wait("vector", sX4, sX4.n)
        if wname == "w1f":
            kb.wait("tensor", sX4, sX4.n)
            for w in range(2):
                kb.run("tensor", lambda e, buf=buf, w=w, pi=pi: e.matmul(
                    pbGU[64:65, w * 512:(w + 1) * 512], fbcol[:, pi:pi + 1],
                    buf[:, w * 512:(w + 1) * 512],
                    start=(pi == 0), stop=(pi == 15), tile_position=(0, 64)),
                    sT4 if w == 1 else None)
            prev_mmT[ji] = sT4.n
        if fold == "pw":
            kb.run("vector", lambda e, buf=buf, cbuf=cbuf, pi=pi, cols=cols:
                   e.tensor_scalar_mul(cbuf[:, 0:cols], buf[:, 0:cols],
                                       pwcol[:, pi:pi + 1]), sV4)
        elif fold == "fw":
            kb.run("vector", lambda e, buf=buf, cbuf=cbuf, pi=pi, cols=cols:
                   e.tensor_scalar_mul(cbuf[:, 0:cols], buf[:, 0:cols],
                                       fwcol[:, pi:pi + 1]), sV4)
        else:
            kb.run("vector", lambda e, buf=buf, cbuf=cbuf, cols=cols:
                   e.tensor_copy(cbuf[:, 0:cols], buf[:, 0:cols]), sV4)
        kb.wait("sync", sV4, sV4.n)
        kb.dma("sync", AP(dst, pi * 128 * cols, [[cols, 128], [1, cols]]),
               cbuf[:, 0:cols], sST)
        prev_cast[ji] = sST.n
        if wname == "o_w":
            kb.wait("tensor", sV4, sV4.n)
            kb.wait("tensor", sX4, sX4.n)
            for w in range(4):
                kb.run("tensor", lambda e, cbuf=cbuf, w=w, pi=pi: e.matmul(
                    pbGU[0:1, w * 512:(w + 1) * 512], onescol_bf4[:],
                    cbuf[:, w * 512:(w + 1) * 512],
                    start=(pi == 0), stop=(pi == 15)))
                kb.run("tensor", lambda e, buf=buf, w=w, pi=pi: e.matmul(
                    pbGU[32:33, w * 512:(w + 1) * 512], pbcol[:, pi:pi + 1],
                    buf[:, w * 512:(w + 1) * 512],
                    start=(pi == 0), stop=(pi == 15), tile_position=(0, 32)),
                    sT4 if (pi == 15 and w == 3) else None)
            prev_mmT[ji] = sT4.n
    kb.wait("vector", sT4, sT4.n)
    kb.run("vector", lambda e: e.tensor_copy(ncsrow[:], pbGU[0:1, 0:2048]))
    pbrow1 = ar2.alloc("pbrow1", [1, 2048], BF16)
    tv6 = kb.run("vector", lambda e: e.tensor_copy(
        pbrow1[:], pbGU[32:33, 0:2048]), sV4)
    b1fr_bf = ar2.alloc("b1fr_bf", [1, 1024], BF16)
    kb.run("vector", lambda e: e.tensor_tensor(
        b1fr_bf[:], b1fr_f, pbGU[64:65, 0:1024], ALU.add))
    kb.run("vector", lambda e: e.tensor_copy(b2frow[:], b2fr_f), sV4)
    kb.wait("tensor", sV4, sV4.n)
    for w in range(4):
        tmr = kb.run("tensor", lambda e, w=w: e.matmul(
            pbF[:, 0:512], onesrow_bf[:], pbrow1[0:1, w * 512:(w + 1) * 512],
            start=True, stop=True), sT4)
        kb.wait("vector", sT4, tmr)
        tvr = kb.run("vector", lambda e, w=w: e.tensor_copy(
            pbowB[:, w * 512:(w + 1) * 512], pbF[:, 0:512]), sV4)
        kb.wait("tensor", sV4, tvr)
    for w in range(2):
        tmr = kb.run("tensor", lambda e, w=w: e.matmul(
            pbF[:, 0:512], onesrow_bf[:], b1fr_bf[0:1, w * 512:(w + 1) * 512],
            start=True, stop=True), sT4)
        kb.wait("vector", sT4, tmr)
        tvr = kb.run("vector", lambda e, w=w: e.tensor_copy(
            b1frepl[:, w * 512:(w + 1) * 512], pbF[:, 0:512]), sV4)
        kb.wait("tensor", sV4, tvr)

    sWi = kb.sem("p4wi")
    kb.dma("sync", W1sb[0:64, :],
           AP(dr["W1"], 0, [[64, 64], [8192, 16], [1, 64]]), sWi)
    kb.dma("sync", W1sb[64:128, :],
           AP(dr["W1"], 4096, [[64, 64], [8192, 16], [1, 64]]), sWi)
    kb.wait("vector", sWi, sWi.n)
    kb.wait("vector", sT4, sT4.n)
    kb.run("vector", lambda e: e.memset(pbGU[:, 0:1024], 0.0))
    for zi in range(4):
        kb.run("tensor", lambda e, zi=zi: e.transpose(
            pbT[:, zi * 128:(zi + 1) * 128], zbT[:], idbfs[:]))
    kb.run("vector", lambda e: e.memset(pbF[:], 0.0))
    kb.run("vector", lambda e: e.memset(pbX[:], 0.0))
    kb.run("vector", lambda e: e.memset(pbS[:], 0.0))
    kb.run("vector", lambda e: e.memset(W1bd[:], 0.0))
    kb.run("vector", lambda e: e.tensor_copy(
        AP(W1bd, 0, [[2048, 64], [128, 16], [1, 64]]),
        AP(W1sb, 0, [[1024, 64], [64, 16], [1, 64]])))
    kb.run("vector", lambda e: e.tensor_copy(
        AP(W1bd, 64 * 2048 + 64, [[2048, 64], [128, 16], [1, 64]]),
        AP(W1sb, 64 * 1024, [[1024, 64], [64, 16], [1, 64]])), sV4)
    ctx["barrier"]()
    if P4CUT <= 2:
        return

    ar2.reset()
    ar.restore(_stash_mark)
    bufs = {}
    bufs["x1f"] = ar2.alloc("x1f", [128, 16, 128], BF16)
    bufs["wv4"] = [ar2.alloc(f"wv4_{i}", [128, 2048], BF16) for i in range(2)]
    bufs["xqfm"] = ar2.alloc("xqfm", [128, 16, 128], BF16)
    bufs["xkfm"] = ar2.alloc("xkfm", [128, 16, 128], BF16)
    bufs["xvfm"] = ar2.alloc("xvfm", [128, 16, 128], BF16)
    bufs["tlfm"] = ar2.alloc("tlfm", [128, 16, 128], BF16)
    bufs["rp1"] = ar.alloc("rp1", [128, 512], F32)
    bufs["rppc"] = ar.alloc("rppc", [128, 512], BF16)
    bufs["lrS"] = ar.alloc("lrS", [128, 128], F32)
    bufs["lrtmp"] = ar.alloc("lrtmp", [128, 128], F32)
    bufs["Zf"] = ar.alloc("Zf4", [128, 512], F32)
    bufs["sqb"] = ar.alloc("sqb4", [128, 512], F32)
    bufs["xcb"] = ar.alloc("xcb4", [128, 512], BF16)
    bufs["m1b"] = ar.alloc("m1b4", [128, 512], BF16)
    bufs["g1b"] = ar.alloc("g1b4", [128, 512], BF16)
    bufs["gxh"] = ar.alloc("gxh4", [128, 512], F32)
    bufs["t1b"] = ar.alloc("t1b4", [128, 512], F32)
    bufs["t2b"] = ar.alloc("t2b4", [128, 512], BF16)
    bufs["gradb"] = ar.alloc("gradb4", [128, 512], BF16)
    bufs["lmask"] = ar.alloc("lmask4", [128, 256], BF16)
    bufs["lmb"] = [ar.alloc(f"lmb{i}", [128, 256], BF16) for i in range(4)]
    bufs["xk15b"] = [ar.alloc(f"xk15b{i}", [128, 512], BF16) for i in range(4)]
    bufs["xqAs"] = ar.alloc("xqAs", [128, 256], BF16)
    bufs["xqBs"] = ar.alloc("xqBs", [128, 256], BF16)
    bufs["tlsc"] = ar.alloc("tlsc4", [128, 512], BF16)
    bufs["xqlb"] = ar.alloc("xqlb4", [128, 512], F32)
    bufs["Zbf"] = ar.alloc("Zbf4", [128, 512], F32)
    bufs["ocb"] = ar.alloc("ocb4", [128, 512], BF16)
    bufs["o2b"] = ar.alloc("o2b4", [128, 512], BF16)
    bufs["outsc"] = ar.alloc("outsc4", [128, 512], F32)
    bufs["outfm"] = ar2.alloc("outfm4", [128, 16, 128], BF16)
    bufs["tin4"] = ar.alloc("tin4", [128, 48], F32)
    bufs["postmu"] = ar.alloc("postmu", [128, 1], F32)
    bufs["postr"] = ar.alloc("postr", [128, 1], F32)
    bufs["strow"] = ar.alloc("strow", [16, 2], F32)
    bufs["xsub"] = ar2.alloc("xsub", [128, 2048], F32)
    bufs["e1b"] = ar2.alloc("e1b", [128, 2048], F32)
    bufs["y2b"] = ar2.alloc("y2b", [128, 2048], F32)
    bufs["ztm"] = ar2.alloc("ztm", [128, 2048], BF16)
    bufs["zfm"] = bufs["x1f"]
    bufs["hftm"] = ar2.alloc("hftm", [128, 1024], BF16)
    bufs["hffm"] = ar2.alloc("hffm", [128, 8, 128], BF16)
    bufs["gsub"] = bufs["ztm"]
    bufs["xob"] = bufs["e1b"]
    sems = dict(sW4=sW4, sX4=sX4, sT4=sT4, sA4=sA4, sV4=sV4, sG4=sG4,
                sO4=sO4)
    stat = dict(lwB=lwB, lw2B=lw2B, lbB=lbB, b1B=b1B, W1bd=W1bd, W1sb=W1sb,
                negmtB=negmtB, tokB=tokB, ntokones=ntokones, lrwT=lrwT,
                lrbC=lrbC, lbfmB=lbfmB, lwfmB=lwfmB, cosB=cosB, sinpB=sinpB,
                ncsrow=ncsrow, pbowB=pbowB, b1frepl=b1frepl, b2frow=b2frow,
                I8sb=I8sb, J16s=J16s, S16s=S16s, bandmS=bandmS,
                ntok15=ntok15, ntok15m=ntok15m, ntokm4=ntokm4,
                halfmS=halfmS, blkmS=blkmS, I8s4s=I8s4s,
                onesrow_bf=onesrow_bf,
                onescol_bf4=onescol_bf4, murow=murow)

    nsteps_total = getattr(_emit_p4, "NCHUNK", 16)
    for k in range(nsteps_total):
        _emit_p4_chunk(ctx, k, bufs, sems, stat)

    for e in ENGINES:
        kb.wait(e, sO4, sO4.n)


def _p4tap(ctx, name, src_ap, shape, dtype=F32, eng=None):
    d = ctx["dbg_dram"](name, shape, dtype)
    if d is None:
        return
    kb = ctx["kb"]
    if "_dbsem" not in ctx:
        ctx["_dbsem"] = kb.sem("dbsem")
        ctx["_dbflush"] = kb.sem("dbflush")
    if eng is not None:
        kb.run(eng, lambda e: e.engine_nop(), ctx["_dbflush"])
    for sm in list(kb._sems.values()):
        if sm.n > 0:
            kb.wait("gpsimd", sm, sm.n)
    kb.dma("gpsimd", d.ap(), src_ap, ctx["_dbsem"])
    ctx["barrier"]()


def _emit_p4_chunk(ctx, k, bufs, sems, stat):
    nc, kb, dr = ctx["nc"], ctx["kb"], ctx["dr"]
    pbS, pbT, pbF, pbGU, pbX = (ctx["pbS"], ctx["pbT"], ctx["pbF"],
                                ctx["pbGU"], ctx["pbX"])
    idbfs = ctx["idbfs"]
    c = ctx
    b = bufs
    s = stat
    sW4, sX4, sT4, sA4 = sems["sW4"], sems["sX4"], sems["sT4"], sems["sA4"]
    sV4, sG4, sO4 = sems["sV4"], sems["sG4"], sems["sO4"]
    ZW, BW = 1024, 1536
    QKV_BANKS = [(pbGU, ZW, 2048), (pbGU, BW, 2048), (pbF, 0, 512), (pbX, 0, 512)]

    if k > 0:
        kb.wait("sync", sT4, ctx["p4_x1_free"])
    tx1 = kb.dma("sync", b["x1f"][:],
                 AP(c["x1c_st_d"], k * 128 * C, [[C, 128], [1, C]]), sX4)
    kb.wait("tensor", sX4, tx1)

    def bank_sl(nt):
        t, off, _ = QKV_BANKS[nt]
        return t[:, off:off + 512]

    def rope_evac(dst_fm, do_rope, tag):
        for nt in range(4):
            t, off, stride = QKV_BANKS[nt]
            ps = t[:, off:off + 512]
            kb.wait("vector", sT4, ctx.get("rppc_free", 0))
            if do_rope:
                cos_ap = AP(s["cosB"], 0, [[64, 128], [0, 8], [1, 64]])
                kb.run("vector", lambda e, ps=ps, cos_ap=cos_ap: e.tensor_tensor(
                    b["rp1"][:], ps, cos_ap, ALU.mult))
                rot_ap = AP(t, off + 32,
                            [[stride, 128], [64, 8], [-32, 2], [1, 32]])
                sin_ap = AP(s["sinpB"], 0, [[64, 128], [0, 8], [1, 64]])
                kb.run("vector", lambda e, rot_ap=rot_ap, sin_ap=sin_ap:
                       e.tensor_tensor(b["sqb"][:], rot_ap, sin_ap, ALU.mult))
                tv = kb.run("vector", lambda e: e.tensor_add(
                    b["rppc"][:], b["rp1"][:], b["sqb"][:]), sV4)
            else:
                tv = kb.run("vector", lambda e, ps=ps: e.tensor_copy(
                    b["rppc"][:], ps), sV4)
            kb.wait("tensor", sV4, tv)
            kb.wait("tensor", sA4, sA4.n)
            for i in range(4):
                kb.run("tensor", lambda e, i=i: e.transpose(
                    pbT[:, i * 128:(i + 1) * 128],
                    b["rppc"][:, i * 128:(i + 1) * 128], idbfs[:]),
                    sT4 if i == 3 else None)
            ctx["rppc_free"] = sT4.n
            kb.wait("scalar", sT4, sT4.n)
            kb.run("scalar", lambda e, nt=nt, dst_fm=dst_fm: e.copy(
                dst_fm[:, nt * 4:(nt + 1) * 4, :], pbT[:]), sA4)

    for wi, (stname, dst_fm, do_rope) in enumerate((
            ("qw_st", b["xqfm"], True),
            ("kw_st", b["xkfm"], True),
            ("vw_st", b["xvfm"], False))):
        std = c[stname + "_d"]
        for kc in range(16):
            wb = b["wv4"][kc % 2]
            key = (k, wi, kc - 2)
            if key in ctx.setdefault("p4_wv", {}):
                kb.wait("scalar", sT4, ctx["p4_wv"][key])
            elif kc < 2 and (k + wi) > 0:
                kb.wait("scalar", sT4, ctx["p4_wv_last"])
            tw = kb.dma("scalar", wb[:],
                        AP(std, kc * 128 * C, [[C, 128], [1, C]]), sW4)
            kb.wait("tensor", sW4, tw)
            if kc == 0:
                kb.wait("tensor", sV4, sV4.n)
            for nt in range(4):
                kb.run("tensor", lambda e, kc=kc, nt=nt, wb=wb: e.matmul(
                    bank_sl(nt), b["x1f"][:, kc, :],
                    wb[:, nt * 512:(nt + 1) * 512],
                    start=(kc == 0), stop=(kc == 15)),
                    sT4 if nt == 3 else None)
            ctx["p4_wv"][(k, wi, kc)] = sT4.n
            ctx["p4_wv_last"] = sT4.n
        kb.wait("vector", sT4, sT4.n)
        rope_evac(dst_fm, do_rope, wi)

    kb.wait("tensor", sA4, sA4.n)
    for kc in range(16):
        kb.run("tensor", lambda e, kc=kc: e.matmul(
            pbS[:, 256:384], b["x1f"][:, kc, :], s["lrwT"][:, kc, :],
            start=(kc == 0), stop=(kc == 15)), sT4 if kc == 15 else None)
    ctx["p4_x1_free"] = sT4.n

    kb.wait("vector", sA4, sA4.n)
    kb.run("vector", lambda e: e.tensor_sub(
        b["tlfm"][:].rearrange("p a b -> p (a b)"),
        b["xkfm"][:].rearrange("p a b -> p (a b)"),
        b["xvfm"][:].rearrange("p a b -> p (a b)")))
    kb.run("vector", lambda e: e.tensor_tensor(
        b["tlfm"][:], b["tlfm"][:],
        AP(s["lbfmB"], 0, [[16, 128], [1, 16], [0, 128]]), ALU.add))
    kb.run("vector", lambda e: e.tensor_tensor(
        b["tlfm"][:], b["tlfm"][:],
        AP(s["lwfmB"], 0, [[16, 128], [1, 16], [0, 128]]), ALU.mult), sV4)

    kb.wait("vector", sT4, sT4.n)
    tv = kb.run("vector", lambda e: e.tensor_copy(
        b["rppc"][:, 0:128], pbS[:, 256:384]), sV4)
    kb.wait("tensor", sV4, tv)
    kb.wait("tensor", sA4, sA4.n)
    tt = kb.run("tensor", lambda e: e.transpose(
        pbT[:, 0:128], b["rppc"][:, 0:128], idbfs[:]), sT4)
    kb.wait("scalar", sT4, tt)
    ta = kb.run("scalar", lambda e: e.activation(
        b["lrS"][:], pbT[:, 0:128], AF.Sigmoid, bias=s["lrbC"][:]), sA4)
    kb.wait("vector", sA4, ta)
    kb.wait("tensor", sA4, ta)

    if k == 0:
        _p4tap(ctx, "x1f", b["x1f"][:].rearrange("p a b -> p (a b)"), (128, 2048))
        _p4tap(ctx, "xqfm", b["xqfm"][:].rearrange("p a b -> p (a b)"), (128, 2048))
        _p4tap(ctx, "xkfm", b["xkfm"][:].rearrange("p a b -> p (a b)"), (128, 2048))
        _p4tap(ctx, "xvfm", b["xvfm"][:].rearrange("p a b -> p (a b)"), (128, 2048))
        _p4tap(ctx, "tlfm", b["tlfm"][:].rearrange("p a b -> p (a b)"), (128, 2048))
        _p4tap(ctx, "lrS", b["lrS"][:], (128, 128))
        _p4tap(ctx, "W1bd0", s["W1bd"][:], (128, 2048))
        _p4tap(ctx, "b1B0", s["b1B"][:], (128, 512))

    if ctx["P4CUT"] <= 3:
        return
    for n in range(8 if ctx["P4CUT"] > 4 else 1):
        _emit_scan_step(ctx, k, n, bufs, sems, stat)
    if ctx["P4CUT"] <= 5:
        return

    _emit_p4_tail(ctx, k, bufs, sems, stat)


def _emit_scan_step(ctx, k, n, bufs, sems, stat):
    nc, kb = ctx["nc"], ctx["kb"]
    pbS, pbT, pbF, pbGU, pbX = (ctx["pbS"], ctx["pbT"], ctx["pbF"],
                                ctx["pbGU"], ctx["pbX"])
    idbfs = ctx["idbfs"]
    b = bufs
    s = stat
    sT4, sA4, sV4 = sems["sT4"], sems["sA4"], sems["sV4"]
    ZW, BW = 1024, 1536
    jsl = slice(n * 16, (n + 1) * 16)
    dbgme = (k == 0 and n == getattr(_emit_scan_step, "DBGN", 0))
    SCUT = getattr(_emit_scan_step, "SCUT", 99)

    def T16(src_fm, tag):
        kb.wait("tensor", sA4, sA4.n)
        for q in range(16):
            kb.run("tensor", lambda e, q=q, src_fm=src_fm: e.transpose(
                pbT[(q % 4) * 32:(q % 4) * 32 + 16,
                    (q // 4) * 128:(q // 4) * 128 + 128],
                src_fm[:, q, jsl], idbfs[:],
                tile_position=(0, (q % 4) * 32)),
                sT4 if q == 15 else None)
        kb.wait("vector", sT4, sT4.n)

    T16(b["xqfm"], 0)
    tv = kb.run("vector", lambda e: e.scalar_tensor_tensor(
        b["xqlb"][:], pbT[:], 1.0, s["lbB"][:], ALU.mult, ALU.add), sV4)
    ctx["pbT_free"] = tv
    kb.wait("tensor", sV4, tv)
    if dbgme:
        _p4tap(ctx, "xqlb", b["xqlb"][:], (128, 512))
    T16(b["tlfm"], 1)
    tv = kb.run("vector", lambda e: e.tensor_copy(b["tlsc"][:], pbT[:]), sV4)
    kb.wait("tensor", sV4, tv)
    if dbgme:
        _p4tap(ctx, "tlsc", b["tlsc"][:], (128, 512))
    T16(b["xkfm"], 2)
    for blk in range(4):
        kb.run("vector", lambda e, blk=blk: e.tensor_scalar_mul(
            b["xk15b"][blk][:], pbT[:], s["ntok15m"][:, blk:blk + 1]),
            sV4 if blk == 3 else None)
    kb.run("vector", lambda e: e.tensor_scalar_mul(
        b["xqAs"][:], AP(b["xqfm"], n * 16, [[2048, 128], [128, 16], [1, 16]]),
        s["halfmS"][:, 0:1]))
    tv = kb.run("vector", lambda e: e.tensor_scalar_mul(
        b["xqBs"][:], AP(b["xqfm"], n * 16, [[2048, 128], [128, 16], [1, 16]]),
        s["halfmS"][:, 1:2]), sV4)
    kb.wait("tensor", sV4, tv)
    if dbgme:
        _p4tap(ctx, "xk15", b["xk15b"][0][:], (128, 512))

    if SCUT <= 1:
        return
    for blkq in range(4):
        for i in range(4):
            q = blkq + 4 * i
            kb.run("tensor", lambda e, q=q, i=i, blkq=blkq: e.matmul(
                pbGU[blkq * 32:blkq * 32 + 16,
                     ZW + i * 128:ZW + i * 128 + 128],
                b["xkfm"][:, q, jsl], s["W1bd"][:, q * 128:(q + 1) * 128],
                start=True, stop=True,
                tile_position=(0, blkq * 32)),
                sT4 if (blkq == 3 and i == 3) else None)
    if SCUT <= 11:
        return
    for blkq in range(4):
        for i in range(4):
            q = blkq + 4 * i
            for abh in range(2):
                kb.run("tensor", lambda e, q=q, i=i, blkq=blkq, abh=abh: e.matmul(
                    pbS[blkq * 32:blkq * 32 + 16,
                        i * 64 + abh * 16:i * 64 + abh * 16 + 16],
                    b["xkfm"][:, q, jsl],
                    AP(b["xqAs"] if abh == 0 else b["xqBs"], q * 16,
                       [[256, 128], [1, 16]]),
                    start=True, stop=True,
                    tile_position=(0, blkq * 32)),
                    sT4 if (blkq == 3 and i == 3 and abh == 1) else None)
    ctx["p4_qkT_done"] = sT4.n
    if SCUT <= 12:
        return
    for blkq in range(4):
        for i in range(4):
            q = blkq + 4 * i
            kb.run("tensor", lambda e, q=q, i=i, blkq=blkq: e.matmul(
                pbGU[blkq * 32:blkq * 32 + 16,
                     BW + i * 128:BW + i * 128 + 128],
                b["xqfm"][:, q, jsl], s["W1bd"][:, q * 128:(q + 1) * 128],
                start=True, stop=True,
                tile_position=(0, blkq * 32)),
                sT4 if (blkq == 3 and i == 3) else None)
    if SCUT <= 13:
        return
    for blk in range(4):
        kb.run("tensor", lambda e, blk=blk: e.matmul(
            pbS[blk * 32:blk * 32 + 16, 384:392],
            AP(b["lrS"], n * 16, [[128, 128], [1, 16]]),
            s["I8s4s"][:, blk * 8:(blk + 1) * 8],
            start=True, stop=True, tile_position=(0, blk * 32)),
            sT4 if blk == 3 else None)
    tMM1 = sT4.n
    if SCUT <= 2:
        return

    kb.wait("vector", sT4, tMM1)
    t8 = b["tin4"]
    kb.run("vector", lambda e: e.tensor_add(b["Zf"][:], pbGU[:, ZW:ZW + 512],
                                            s["b1B"][:]))
    if dbgme:
        _p4tap(ctx, "Zf", b["Zf"][:], (128, 512), eng="vector")
    kb.run("vector", lambda e: e.tensor_reduce(
        t8[:, 0:8], AP(b["Zf"], 0, [[512, 128], [64, 8], [1, 64]]),
        axis=AXL.X, op=ALU.add))
    kb.run("vector", lambda e: e.tensor_mul(b["sqb"][:], b["Zf"][:], b["Zf"][:]))
    kb.run("vector", lambda e: e.tensor_reduce(
        t8[:, 8:16], AP(b["sqb"], 0, [[512, 128], [64, 8], [1, 64]]),
        axis=AXL.X, op=ALU.add))
    kb.run("vector", lambda e: e.tensor_scalar_mul(t8[:, 0:8], t8[:, 0:8],
                                                   1.0 / 64.0))
    kb.run("vector", lambda e: e.tensor_mul(t8[:, 16:24], t8[:, 0:8], t8[:, 0:8]))
    kb.run("vector", lambda e: e.scalar_tensor_tensor(
        t8[:, 8:16], t8[:, 8:16], 1.0 / 64.0, t8[:, 16:24],
        ALU.mult, ALU.subtract))
    tv = kb.run("vector", lambda e: e.tensor_scalar_add(
        t8[:, 16:24], t8[:, 8:16], 1e-6), sV4)
    kb.wait("scalar", sV4, tv)
    ta = kb.run("scalar", lambda e: e.activation(
        t8[:, 24:32], t8[:, 16:24], AF.Sqrt), sA4)
    kb.wait("vector", sA4, ta)
    kb.run("vector", lambda e: e.reciprocal(t8[:, 32:40], t8[:, 24:32]))
    kb.run("vector", lambda e: e.tensor_tensor(
        b["xcb"][:], b["Zf"][:],
        AP(t8, 0, [[48, 128], [1, 8], [0, 64]]), ALU.subtract))
    kb.run("vector", lambda e: e.tensor_mul(b["m1b"][:], b["xcb"][:], s["lw2B"][:]))
    kb.run("vector", lambda e: e.tensor_tensor(
        b["g1b"][:], b["m1b"][:],
        AP(t8, 32, [[48, 128], [1, 8], [0, 64]]), ALU.mult))
    kb.run("vector", lambda e: e.tensor_add(b["gxh"][:], b["g1b"][:], b["tlsc"][:]))
    kb.run("vector", lambda e: e.tensor_reduce(
        t8[:, 16:24], AP(b["gxh"], 0, [[512, 128], [64, 8], [1, 64]]),
        axis=AXL.X, op=ALU.add))
    kb.run("vector", lambda e: e.tensor_mul(b["sqb"][:], b["gxh"][:], b["xcb"][:]))
    kb.run("vector", lambda e: e.tensor_reduce(
        t8[:, 24:32], AP(b["sqb"], 0, [[512, 128], [64, 8], [1, 64]]),
        axis=AXL.X, op=ALU.add))
    kb.run("vector", lambda e: e.tensor_mul(t8[:, 40:48], t8[:, 32:40],
                                            pbS[:, 384:392]))
    kb.run("vector", lambda e: e.tensor_scalar_mul(t8[:, 40:48], t8[:, 40:48],
                                                   1.0 / 64.0))
    kb.run("vector", lambda e: e.tensor_scalar_mul(t8[:, 16:24], t8[:, 16:24],
                                                   1.0 / 64.0))
    kb.run("vector", lambda e: e.tensor_mul(t8[:, 8:16], t8[:, 32:40],
                                            t8[:, 32:40]))
    kb.run("vector", lambda e: e.tensor_mul(t8[:, 8:16], t8[:, 8:16],
                                            t8[:, 24:32]))
    kb.run("vector", lambda e: e.tensor_scalar_mul(t8[:, 8:16], t8[:, 8:16],
                                                   1.0 / 64.0))
    kb.run("vector", lambda e: e.tensor_tensor(
        b["t1b"][:], b["gxh"][:],
        AP(t8, 16, [[48, 128], [1, 8], [0, 64]]), ALU.subtract))
    kb.run("vector", lambda e: e.tensor_tensor(
        b["t2b"][:], b["xcb"][:],
        AP(t8, 8, [[48, 128], [1, 8], [0, 64]]), ALU.mult))
    kb.run("vector", lambda e: e.tensor_sub(b["t1b"][:], b["t1b"][:], b["t2b"][:]))
    tgrad = kb.run("vector", lambda e: e.tensor_tensor(
        b["gradb"][:], b["t1b"][:],
        AP(t8, 40, [[48, 128], [1, 8], [0, 64]]), ALU.mult), sV4)
    kb.run("vector", lambda e: e.scalar_tensor_tensor(
        b["lmask"][:], pbS[:, 0:256], 1.0, s["negmtB"][:],
        ALU.add, ALU.mult))
    for blk in range(4):
        kb.run("vector", lambda e, blk=blk: e.tensor_scalar_mul(
            b["lmb"][blk][:], b["lmask"][:], s["blkmS"][:, blk:blk + 1]),
            sV4 if blk == 3 else None)
    tmask = sV4.n
    if dbgme:
        _p4tap(ctx, "c4d", b["tin4"][:, 40:48], (128, 8), eng="vector")
        _p4tap(ctx, "gradb", b["gradb"][:], (128, 512))
        _p4tap(ctx, "lmask", b["lmask"][:], (128, 256))

    if SCUT <= 3:
        return
    kb.wait("tensor", sV4, tmask)
    for blkq in range(4):
        for i in range(4):
            for abh in range(2):
                kb.run("tensor", lambda e, i=i, blkq=blkq, abh=abh: e.matmul(
                    pbX[blkq * 32:blkq * 32 + 16,
                        i * 128 + abh * 64:i * 128 + abh * 64 + 64],
                    AP(b["lmb"][blkq], i * 64 + abh * 16,
                       [[256, 128], [1, 16]]),
                    AP(b["gradb"], i * 128 + abh * 64, [[512, 128], [1, 64]]),
                    start=True, stop=True,
                    tile_position=(0, blkq * 32)),
                    sT4 if (blkq == 3 and i == 3 and abh == 1) else None)
    for q in range(16):
        for abh in range(2):
            kb.run("tensor", lambda e, q=q, abh=abh: e.matmul(
                pbGU[abh * 64:abh * 64 + 64, q * 64:(q + 1) * 64],
                AP(b["xk15b"][q % 4], (q // 4) * 128 + abh * 64,
                   [[512, 128], [1, 64]]),
                AP(b["gradb"], (q // 4) * 128 + abh * 64, [[512, 128], [1, 64]]),
                start=True, stop=True,
                tile_position=(0, abh * 64)),
                sT4 if (q == 15 and abh == 1) else None)
    for blk in range(4):
        for g in range(4):
            kb.run("tensor", lambda e, blk=blk, g=g: e.matmul(
                pbF[blk * 32:blk * 32 + 16, g * 128:(g + 1) * 128],
                AP(s["ntokm4"], blk * 16, [[64, 128], [1, 16]]),
                AP(b["gradb"], g * 128, [[512, 128], [1, 128]]),
                start=True, stop=True,
                tile_position=(0, blk * 32)),
                sT4 if (blk == 3 and g == 3) else None)
    tMM2 = sT4.n
    if SCUT <= 4:
        return

    kb.wait("vector", sT4, tMM2)
    kb.run("vector", lambda e: e.tensor_add(
        s["W1sb"][:], s["W1sb"][:], pbGU[:, 0:1024]))
    kb.run("vector", lambda e: e.tensor_add(b["Zbf"][:], pbGU[:, BW:BW + 512],
                                            s["b1B"][:]))
    kb.run("vector", lambda e: e.tensor_add(b["Zbf"][:], b["Zbf"][:],
                                            pbX[:, 0:512]))
    if dbgme:
        _p4tap(ctx, "Zbf", b["Zbf"][:], (128, 512), eng="vector")
    kb.run("vector", lambda e: e.tensor_add(s["b1B"][:], s["b1B"][:], pbF[:]))
    if dbgme:
        _p4tap(ctx, "b1Bn", s["b1B"][:], (128, 512), eng="vector")
    kb.run("vector", lambda e: e.tensor_reduce(
        t8[:, 0:8], AP(b["Zbf"], 0, [[512, 128], [64, 8], [1, 64]]),
        axis=AXL.X, op=ALU.add))
    kb.run("vector", lambda e: e.tensor_mul(b["sqb"][:], b["Zbf"][:], b["Zbf"][:]))
    kb.run("vector", lambda e: e.tensor_reduce(
        t8[:, 8:16], AP(b["sqb"], 0, [[512, 128], [64, 8], [1, 64]]),
        axis=AXL.X, op=ALU.add))
    kb.run("vector", lambda e: e.tensor_scalar_mul(t8[:, 0:8], t8[:, 0:8],
                                                   1.0 / 64.0))
    kb.run("vector", lambda e: e.tensor_mul(t8[:, 16:24], t8[:, 0:8], t8[:, 0:8]))
    kb.run("vector", lambda e: e.scalar_tensor_tensor(
        t8[:, 8:16], t8[:, 8:16], 1.0 / 64.0, t8[:, 16:24],
        ALU.mult, ALU.subtract))
    tv = kb.run("vector", lambda e: e.tensor_scalar_add(
        t8[:, 16:24], t8[:, 8:16], 1e-6), sV4)
    kb.wait("scalar", sV4, tv)
    ta = kb.run("scalar", lambda e: e.activation(
        t8[:, 24:32], t8[:, 16:24], AF.Sqrt), sA4)
    kb.wait("vector", sA4, ta)
    kb.run("vector", lambda e: e.reciprocal(t8[:, 32:40], t8[:, 24:32]))
    kb.run("vector", lambda e: e.tensor_tensor(
        b["ocb"][:], b["Zbf"][:],
        AP(t8, 0, [[48, 128], [1, 8], [0, 64]]), ALU.subtract))
    kb.run("vector", lambda e: e.tensor_mul(b["o2b"][:], b["ocb"][:], s["lwB"][:]))
    kb.run("vector", lambda e: e.tensor_tensor(
        b["o2b"][:], b["o2b"][:],
        AP(t8, 32, [[48, 128], [1, 8], [0, 64]]), ALU.mult))
    tout = kb.run("vector", lambda e: e.tensor_add(
        b["outsc"][:], b["o2b"][:], b["xqlb"][:]), sV4)
    if dbgme:
        _p4tap(ctx, "outsc", b["outsc"][:], (128, 512))
    kb.run("vector", lambda e: e.tensor_reduce(
        t8[:, 40:41], b["outsc"][:], axis=AXL.X, op=ALU.add))
    kb.run("vector", lambda e: e.tensor_mul(b["sqb"][:], b["outsc"][:],
                                            b["outsc"][:]))
    tst = kb.run("vector", lambda e: e.tensor_reduce(
        t8[:, 41:42], b["sqb"][:], axis=AXL.X, op=ALU.add), sV4)
    kb.wait("tensor", sV4, tst)
    kb.run("tensor", lambda e: e.matmul(
        pbS[0:16, 392:393], s["S16s"][:], t8[:, 40:41],
        start=True, stop=True))
    kb.run("tensor", lambda e: e.matmul(
        pbS[0:16, 393:394], s["S16s"][:], t8[:, 41:42],
        start=True, stop=True), sT4)
    kb.wait("vector", sT4, sT4.n)
    tv = kb.run("vector", lambda e: e.tensor_copy(
        b["strow"][:], pbS[0:16, 392:394]), sV4)
    kb.wait("tensor", sV4, tv)
    kb.run("tensor", lambda e: e.matmul(
        pbS[:, 394:395], s["J16s"][:], b["strow"][:, 0:1],
        start=True, stop=True))
    kb.run("tensor", lambda e: e.matmul(
        pbS[:, 395:396], s["J16s"][:], b["strow"][:, 1:2],
        start=True, stop=True), sT4)
    kb.wait("vector", sT4, sT4.n)
    kb.run("vector", lambda e: e.tensor_scalar_mul(
        t8[:, 42:43], pbS[:, 394:395], 1.0 / 2048.0))
    kb.run("vector", lambda e: e.tensor_mul(t8[:, 43:44], t8[:, 42:43],
                                            t8[:, 42:43]))
    kb.run("vector", lambda e: e.scalar_tensor_tensor(
        t8[:, 44:45], pbS[:, 395:396], 1.0 / 2048.0, t8[:, 43:44],
        ALU.mult, ALU.subtract))
    tv = kb.run("vector", lambda e: e.tensor_scalar_add(
        t8[:, 44:45], t8[:, 44:45], 1e-6), sV4)
    kb.wait("scalar", sV4, tv)
    ta = kb.run("scalar", lambda e: e.activation(
        t8[:, 45:46], t8[:, 44:45], AF.Sqrt), sA4)
    kb.wait("vector", sA4, ta)
    kb.run("vector", lambda e: e.reciprocal(t8[:, 46:47], t8[:, 45:46]))
    if n == 0:
        kb.run("vector", lambda e: e.tensor_mul(
            b["postmu"][:], t8[:, 42:43], s["bandmS"][:, 0:1]))
        kb.run("vector", lambda e: e.tensor_mul(
            b["postr"][:], t8[:, 46:47], s["bandmS"][:, 0:1]))
    else:
        kb.run("vector", lambda e, n=n: e.tensor_mul(
            t8[:, 47:48], t8[:, 42:43], s["bandmS"][:, n:n + 1]))
        kb.run("vector", lambda e: e.tensor_add(
            b["postmu"][:], b["postmu"][:], t8[:, 47:48]))
        kb.run("vector", lambda e, n=n: e.tensor_mul(
            t8[:, 47:48], t8[:, 46:47], s["bandmS"][:, n:n + 1]))
        kb.run("vector", lambda e: e.tensor_add(
            b["postr"][:], b["postr"][:], t8[:, 47:48]))
    tvc = kb.run("vector", lambda e: e.tensor_copy(
        b["ocb"][:], b["outsc"][:]), sV4)

    kb.wait("tensor", sV4, tvc)
    kb.wait("tensor", sA4, sA4.n)
    for g in range(4):
        kb.run("tensor", lambda e, g=g: e.transpose(
            pbT[:, g * 128:(g + 1) * 128],
            b["ocb"][:, g * 128:(g + 1) * 128], idbfs[:]),
            sT4 if g == 3 else None)
    kb.wait("scalar", sT4, sT4.n)
    for g in range(4):
        kb.run("scalar", lambda e, g=g, n=n: e.copy(
            AP(b["outfm"], (4 * g) * 128 + n * 16,
               [[16 * 128, 128], [128, 4], [1, 16]]),
            AP(pbT, g * 128, [[512, 128], [32, 4], [1, 16]])), sA4)

    kb.run("vector", lambda e: e.tensor_copy(
        AP(s["W1bd"], 0, [[2048, 64], [128, 16], [1, 64]]),
        AP(s["W1sb"], 0, [[1024, 64], [64, 16], [1, 64]])))
    kb.run("vector", lambda e: e.tensor_copy(
        AP(s["W1bd"], 64 * 2048 + 64, [[2048, 64], [128, 16], [1, 64]]),
        AP(s["W1sb"], 64 * 1024, [[1024, 64], [64, 16], [1, 64]])), sV4)
    if dbgme:
        _p4tap(ctx, "W1bdn", s["W1bd"][:], (128, 2048), eng="vector")
    if k == 0 and n == 7:
        _p4tap(ctx, "outfm", b["outfm"][:].rearrange("p a b -> p (a b)"),
               (128, 2048))
        _p4tap(ctx, "postmu", b["postmu"][:], (128, 1), eng="vector")
        _p4tap(ctx, "postr", b["postr"][:], (128, 1), eng="vector")


def _emit_p4_tail(ctx, k, bufs, sems, stat):
    nc, kb, dr = ctx["nc"], ctx["kb"], ctx["dr"]
    pbS, pbT, pbF, pbGU, pbX = (ctx["pbS"], ctx["pbT"], ctx["pbF"],
                                ctx["pbGU"], ctx["pbX"])
    idbfs, mrstd = ctx["idbfs"], ctx["mrstd"]
    c = ctx
    b = bufs
    s = stat
    sW4, sX4, sT4, sA4 = sems["sW4"], sems["sX4"], sems["sT4"], sems["sA4"]
    sV4, sG4, sO4 = sems["sV4"], sems["sG4"], sems["sO4"]
    ZW, BW = 1024, 1536
    QKV_BANKS = [(pbGU, ZW, 2048), (pbGU, BW, 2048), (pbF, 0, 512), (pbX, 0, 512)]

    def bank_sl(w):
        t, off, _ = QKV_BANKS[w]
        return t[:, off:off + 512]

    t8 = b["tin4"]

    kb.run("vector", lambda e: e.tensor_scalar_mul(
        b["ocb"][:, 0:1], b["postmu"][:], -1.0))
    tv = kb.run("vector", lambda e: e.engine_nop(), sV4)
    kb.wait("tensor", sV4, tv)
    kb.wait("tensor", sA4, sA4.n)
    tt = kb.run("tensor", lambda e: e.transpose(
        pbT[0:1, 0:128], b["ocb"][:, 0:1], idbfs[:]), sT4)
    kb.wait("scalar", sT4, tt)
    tmu = kb.run("scalar", lambda e: e.copy(s["murow"][:], pbT[0:1, 0:128]), sA4)
    kb.wait("tensor", sA4, tmu)

    kb.wait("tensor", sV4, sV4.n)
    for q in range(16):
        wb = b["wv4"][q % 2]
        key = ("ow", k, q - 2)
        if key in ctx["p4_wv"]:
            kb.wait("scalar", sT4, ctx["p4_wv"][key])
        tw = kb.dma("scalar", wb[:],
                    AP(c["ow_st_d"], q * 128 * C, [[C, 128], [1, C]]), sW4)
        kb.wait("tensor", sW4, tw)
        for w in range(4):
            kb.run("tensor", lambda e, q=q, w=w, wb=wb: e.matmul(
                bank_sl(w), b["outfm"][:, q, :],
                wb[:, w * 512:(w + 1) * 512],
                start=(q == 0), stop=False),
                sT4 if w == 3 else None)
        ctx["p4_wv"][("ow", k, q)] = sT4.n
    for w in range(4):
        kb.run("tensor", lambda e, w=w: e.matmul(
            bank_sl(w), s["murow"][:], s["ncsrow"][0:1, w * 512:(w + 1) * 512],
            start=False, stop=True), sT4 if w == 3 else None)
    towmm = sT4.n

    tx = kb.dma("sync", b["xsub"][:],
                AP(dr["x"], k * 128 * C, [[C, 128], [1, C]]), sX4)
    kb.wait("vector", sT4, towmm)
    kb.wait("vector", sX4, sX4.n)
    for w in range(4):
        kb.run("vector", lambda e, w=w: e.scalar_tensor_tensor(
            b["e1b"][:, w * 512:(w + 1) * 512], bank_sl(w), b["postr"][:],
            s["pbowB"][:, w * 512:(w + 1) * 512], ALU.mult, ALU.add))
    if k == 0:
        _p4tap(ctx, "xsubd", b["xsub"][:], (128, 2048))
        _p4tap(ctx, "e1d", b["e1b"][:], (128, 2048), eng="vector")
    kb.run("vector", lambda e, k=k: e.tensor_mul(
        t8[:, 0:1], mrstd[:, k, 0:1], mrstd[:, k, 1:2]))
    kb.run("vector", lambda e: e.tensor_scalar_mul(t8[:, 0:1], t8[:, 0:1], -1.0))
    kb.run("vector", lambda e, k=k: e.tensor_scalar(
        b["y2b"][:], b["xsub"][:], mrstd[:, k, 1:2], t8[:, 0:1],
        ALU.mult, ALU.add))
    tu = kb.run("vector", lambda e: e.tensor_add(
        b["y2b"][:], b["y2b"][:], b["e1b"][:]), sV4)
    kb.run("vector", lambda e: e.tensor_reduce(
        t8[:, 1:2], b["y2b"][:], axis=AXL.X, op=ALU.add))
    kb.run("vector", lambda e: e.scalar_tensor_tensor(
        b["e1b"][:], b["y2b"][:], 1.0, b["y2b"][:], ALU.mult, ALU.mult,
        accum_out=t8[:, 2:3]))
    kb.run("vector", lambda e: e.tensor_scalar_mul(t8[:, 1:2], t8[:, 1:2],
                                                   1.0 / 2048.0))
    kb.run("vector", lambda e: e.tensor_mul(t8[:, 3:4], t8[:, 1:2], t8[:, 1:2]))
    kb.run("vector", lambda e: e.scalar_tensor_tensor(
        t8[:, 2:3], t8[:, 2:3], 1.0 / 2048.0, t8[:, 3:4],
        ALU.mult, ALU.subtract))
    tv = kb.run("vector", lambda e: e.tensor_scalar_add(
        t8[:, 2:3], t8[:, 2:3], 1e-6), sV4)
    kb.wait("scalar", sV4, tv)
    ta = kb.run("scalar", lambda e: e.activation(
        t8[:, 3:4], t8[:, 2:3], AF.Sqrt), sA4)
    kb.wait("vector", sA4, ta)
    kb.run("vector", lambda e: e.reciprocal(t8[:, 4:5], t8[:, 3:4]))
    tz = kb.run("vector", lambda e: e.tensor_scalar(
        b["ztm"][:], b["y2b"][:], t8[:, 1:2], t8[:, 4:5],
        ALU.subtract, ALU.mult), sV4)
    if k == 0:
        _p4tap(ctx, "y2d", b["y2b"][:], (128, 2048))
        _p4tap(ctx, "ztmd", b["ztm"][:], (128, 2048))

    kb.wait("tensor", sV4, tz)
    for g in range(4):
        kb.wait("tensor", sA4, sA4.n)
        for i in range(4):
            kb.run("tensor", lambda e, g=g, i=i: e.transpose(
                pbT[:, i * 128:(i + 1) * 128],
                b["ztm"][:, (g * 4 + i) * 128:(g * 4 + i + 1) * 128],
                idbfs[:]), sT4 if i == 3 else None)
        kb.wait("scalar", sT4, sT4.n)
        kb.run("scalar", lambda e, g=g: e.copy(
            b["zfm"][:, g * 4:(g + 1) * 4, :], pbT[:]), sA4)
    kb.wait("sync", sT4, sT4.n)
    tg = kb.dma("sync", b["gsub"][:],
                AP(c["g_st_d"], k * 128 * C, [[C, 128], [1, C]]), sX4)

    kb.wait("tensor", sA4, sA4.n)
    for kc in range(16):
        wb = b["wv4"][kc % 2]
        key = ("w1f", k, kc - 2)
        if key in ctx["p4_wv"]:
            kb.wait("scalar", sT4, ctx["p4_wv"][key])
        tw = kb.dma("scalar", wb[:, 0:1024],
                    AP(c["w1f_st_d"], kc * 128 * DH, [[DH, 128], [1, DH]]), sW4)
        kb.wait("tensor", sW4, tw)
        for w in range(2):
            kb.run("tensor", lambda e, kc=kc, w=w, wb=wb: e.matmul(
                pbGU[:, ZW + w * 512:ZW + (w + 1) * 512],
                b["zfm"][:, kc, :], wb[:, w * 512:(w + 1) * 512],
                start=(kc == 0), stop=(kc == 15)),
                sT4 if w == 1 else None)
        ctx["p4_wv"][("w1f", k, kc)] = sT4.n
    ctx["p4_x1_free"] = sT4.n
    kb.wait("vector", sT4, sT4.n)
    for w in range(2):
        kb.run("vector", lambda e, w=w: e.tensor_tensor(
            b["rp1"][:], pbGU[:, ZW + w * 512:ZW + (w + 1) * 512],
            s["b1frepl"][:, w * 512:(w + 1) * 512], ALU.add))
        kb.run("vector", lambda e, w=w: e.tensor_relu(
            b["hftm"][:, w * 512:(w + 1) * 512], b["rp1"][:]), sV4)
    if k == 0:
        _p4tap(ctx, "hftd", b["hftm"][:], (128, 1024), eng="vector")
    kb.wait("tensor", sV4, sV4.n)
    for g in range(2):
        kb.wait("tensor", sA4, sA4.n)
        for i in range(4):
            kb.run("tensor", lambda e, g=g, i=i: e.transpose(
                pbT[:, i * 128:(i + 1) * 128],
                b["hftm"][:, (g * 4 + i) * 128:(g * 4 + i + 1) * 128],
                idbfs[:]), sT4 if i == 3 else None)
        kb.wait("scalar", sT4, sT4.n)
        kb.run("scalar", lambda e, g=g: e.copy(
            b["hffm"][:, g * 4:(g + 1) * 4, :], pbT[:]), sA4)

    kb.wait("tensor", sA4, sA4.n)
    for i in range(8):
        wb = b["wv4"][i % 2]
        key = ("w2f", k, i - 2)
        if key in ctx["p4_wv"]:
            kb.wait("scalar", sT4, ctx["p4_wv"][key])
        tw = kb.dma("scalar", wb[:],
                    AP(c["w2f_st_d"], i * 128 * C, [[C, 128], [1, C]]), sW4)
        kb.wait("tensor", sW4, tw)
        for w in range(4):
            kb.run("tensor", lambda e, i=i, w=w, wb=wb: e.matmul(
                bank_sl(w), b["hffm"][:, i, :], wb[:, w * 512:(w + 1) * 512],
                start=(i == 0), stop=False), sT4 if w == 3 else None)
        ctx["p4_wv"][("w2f", k, i)] = sT4.n
    for w in range(4):
        kb.run("tensor", lambda e, w=w: e.matmul(
            bank_sl(w), s["onesrow_bf"][:],
            s["b2frow"][0:1, w * 512:(w + 1) * 512],
            start=False, stop=True), sT4 if w == 3 else None)
    kb.wait("vector", sT4, sT4.n)
    for w in range(4):
        kb.run("vector", lambda e, w=w: e.tensor_add(
            b["e1b"][:, w * 512:(w + 1) * 512], bank_sl(w),
            b["y2b"][:, w * 512:(w + 1) * 512]))
    if k == 0:
        _p4tap(ctx, "sfin", b["e1b"][:], (128, 2048), eng="vector")
    kb.wait("vector", sX4, tg)
    if k == 0:
        _p4tap(ctx, "gsubd", b["gsub"][:], (128, 2048))
    kb.run("vector", lambda e: e.tensor_mul(b["y2b"][:], b["e1b"][:],
                                            b["gsub"][:]))
    txo = kb.run("vector", lambda e: e.tensor_add(
        b["xob"][:], b["y2b"][:], b["xsub"][:]), sV4)
    kb.wait("sync", sV4, txo)
    kb.dma("sync", AP(c["out_d"], k * 128 * C, [[C, 128], [1, C]]),
           b["xob"][:], sO4)



_IN_NAMES = [
    ("x", "L0_attn_enc_input"), ("q_w", "L0_attn_q_w"), ("k_w", "L0_attn_k_w"),
    ("v_w", "L0_attn_v_w"), ("o_w", "L0_attn_o_w"),
    ("ln1_w", "L0_attn_ln1_w"), ("ln1_b", "L0_attn_ln1_b"),
    ("ttt_lr_w", "L0_attn_ttt_lr_w"), ("ttt_lr_b", "L0_attn_ttt_lr_b"),
    ("token_idx_off", "L0_attn_token_idx_off"),
    ("ttt_norm_w", "L0_attn_ttt_norm_w"), ("ttt_norm_b", "L0_attn_ttt_norm_b"),
    ("W1", "L0_attn_W1"), ("b1", "L0_attn_b1"),
    ("post_w", "L0_attn_post_w"), ("post_b", "L0_attn_post_b"),
    ("ffn_ln_w", "L0_attn_ffn_ln_w"), ("ffn_ln_b", "L0_attn_ffn_ln_b"),
    ("w1f", "L0_attn_w1f"), ("b1f", "L0_attn_b1f"),
    ("w2f", "L0_attn_w2f"), ("b2f", "L0_attn_b2f"),
    ("w_gate", "L0_ffn_w_gate"), ("w_up", "L0_ffn_w_up"),
    ("w_down", "L0_ffn_w_down"),
]


def kernel(**inputs):
    nc, _ = build_program()
    x = np.ascontiguousarray(np.asarray(inputs["L0_attn_enc_input"], np.float32))
    base = {k: np.ascontiguousarray(np.asarray(inputs[v], np.float32))
            for k, v in _IN_NAMES if k != "x"}
    in_maps = []
    for c in range(8):
        m = dict(base)
        m["x"] = x[c]
        in_maps.append(m)
    res = run_bass_kernel_spmd(nc, in_maps, core_ids=list(range(8)))
    return np.stack([res.results[c]["out"] for c in range(8)], axis=0)


# version 05 (final)
# speedup vs baseline: 1.0239x; 1.0239x over previous
import math
import numpy as np
import ml_dtypes

import concourse.bass as bass
import concourse.mybir as mybir
from concourse.ap import AP
from concourse.bass_isa import ReduceOp
from concourse import library_config
from concourse.bass_utils import run_bass_kernel_spmd

F32 = mybir.dt.float32
BF16 = mybir.dt.bfloat16
I8 = mybir.dt.int8
AF = mybir.ActivationFunctionType
ALU = mybir.AluOpType
AXL = mybir.AxisListType

B, L, C = 8, 2048, 2048
NH, HD, MB = 32, 64, 16
DH = 1024
IF = 8192
EPS = 1e-5
ROPE_THETA = 10000.0
CH4 = 256
NCH4 = L // CH4
SUB = 128

ENGINES = ("sync", "vector", "scalar", "tensor", "gpsimd")


def _np_bf16(x):
    return np.asarray(x).astype(ml_dtypes.bfloat16)


class Ctr:
    def __init__(self, h):
        self.h = h
        self.n = 0

    def inc(self, k):
        self.n += k
        return self.n


class KB:

    def __init__(self, nc):
        self.nc = nc
        self.ops = {e: [] for e in ENGINES}
        self._sems = {}
        self._tiny = False

    def tiny(self, on):
        self._tiny = on

    def sem(self, name):
        if name not in self._sems:
            self._sems[name] = Ctr(self.nc.alloc_semaphore(name))
        return self._sems[name]

    def on(self, eng, fn):
        self.ops[eng].append(fn)

    def wait(self, eng, sem, val=None):
        v = sem.n if val is None else val
        self.on(eng, lambda e, s=sem.h, v=v: e.wait_ge(s, v))

    def dma(self, eng, out, in_, sem, slow=False):
        def fn(e, out=out, in_=in_, s=sem.h, slow=slow):
            e.dma_start(out=out, in_=in_,
                        allow_slow_non_contiguous=slow).then_inc(s, 16)
        self.on(eng, fn)
        return sem.inc(16)

    def run(self, eng, fn, sem=None):
        if sem is None:
            if self._tiny and eng in ("vector", "gpsimd"):
                def plain(e, fn=fn):
                    fn(e)
                    e.drain()
                self.on(eng, plain)
            else:
                self.on(eng, fn)
            return None

        def wrapped(e, fn=fn, s=sem.h):
            fn(e)
            e.drain().then_inc(s, 1)
        self.on(eng, wrapped)
        return sem.inc(1)

    def finalize(self):
        with self.nc.Block() as blk:
            for e in ENGINES:
                getattr(blk, e)(lambda eng, e=e: [f(eng) for f in self.ops[e]])


ARENA_TOP = 229344
ARENA_SZ = 98304
ARENA_BASE = ARENA_TOP - ARENA_SZ


class Arena:

    _DT = {F32: 4, BF16: 2, I8: 1}

    def __init__(self, nc, base=ARENA_BASE, top=ARENA_TOP):
        self.nc = nc
        self.base = base
        self.top = top
        self.off = base
        self.n = 0

    def reset(self):
        self.off = self.base

    def mark(self):
        return self.off

    def restore(self, m):
        self.off = m

    def alloc(self, name, shape, dtype):
        import functools, operator
        free_elems = functools.reduce(operator.mul, shape[1:], 1)
        sz = free_elems * self._DT[dtype]
        sz = (sz + 31) // 32 * 32
        assert self.off + sz <= self.top, f"arena overflow at {name}"
        self.n += 1
        t = self.nc.alloc_sbuf_tensor_at(f"{name}_{self.n}", shape, dtype,
                                         offset=self.off)
        self.off += sz
        return t



def build_program(stages=("p0", "p1", "p2", "p3", "p4"), debug=()):
    nc = bass.Bass("TRN2", target_bir_lowering=False, detect_race_conditions=False)
    kb = KB(nc)
    ar = Arena(nc)
    st = set(stages)
    dbg_names = set(debug)

    def barrier():
        for e in ENGINES:
            for s in list(kb._sems.values()):
                if s.n > 0:
                    kb.wait(e, s, s.n)

    dr = {}
    def din(name, shape):
        dr[name] = nc.dram_tensor(name, list(shape), F32, kind="ExternalInput")
        return dr[name]

    din("x", (L, C))
    din("q_w", (C, C)); din("k_w", (C, C)); din("v_w", (C, C)); din("o_w", (C, C))
    din("ln1_w", (C,)); din("ln1_b", (C,))
    din("ttt_lr_w", (NH, 1, C)); din("ttt_lr_b", (NH,))
    din("token_idx_off", (MB,))
    din("ttt_norm_w", (NH, HD)); din("ttt_norm_b", (NH, HD))
    din("W1", (NH, HD, HD)); din("b1", (NH, 1, HD))
    din("post_w", (C,)); din("post_b", (C,))
    din("ffn_ln_w", (C,)); din("ffn_ln_b", (C,))
    din("w1f", (C, DH)); din("b1f", (DH,))
    din("w2f", (DH, C)); din("b2f", (C,))
    din("w_gate", (IF, C)); din("w_up", (IF, C)); din("w_down", (C, IF))

    out_d = nc.dram_tensor("out", [L, C], F32, kind="ExternalOutput")

    hfm_d = nc.dram_tensor("hfm", [IF, L], F32, kind="Internal")
    x1c_st_d = nc.dram_tensor("x1c_st", [16, 128, C], BF16, kind="Internal")
    g_st_d = nc.dram_tensor("g_st", [16, 128, C], BF16, kind="Internal")
    wdT_st_d = nc.dram_tensor("wdT_st", [IF // 128, C // 128, 128, 128], I8, kind="Internal")
    qw_st_d = nc.dram_tensor("qw_st", [C, C], BF16, kind="Internal")
    kw_st_d = nc.dram_tensor("kw_st", [C, C], BF16, kind="Internal")
    vw_st_d = nc.dram_tensor("vw_st", [C, C], BF16, kind="Internal")
    ow_st_d = nc.dram_tensor("ow_st", [C, C], BF16, kind="Internal")
    w1f_st_d = nc.dram_tensor("w1f_st", [C, DH], BF16, kind="Internal")
    w2f_st_d = nc.dram_tensor("w2f_st", [DH, C], BF16, kind="Internal")

    dbg = {}

    def dbg_dram(name, shape, dtype=F32):
        if name in dbg_names:
            dbg[name] = nc.dram_tensor("dbg_" + name, list(shape), dtype,
                                       kind="ExternalOutput")
            return dbg[name]
        return None

    idx = np.arange(128)
    a_and = np.bitwise_and.outer(idx, idx)
    pc = np.zeros((128, 128), np.int64)
    for k in range(7):
        pc += (a_and >> k) & 1
    H128_np = np.where(pc % 2 == 0, 1.0, -1.0).astype(np.float32)
    H128_d = nc.inline_tensor(_np_bf16(H128_np), name="cH128")
    identbf_d = nc.inline_tensor(np.eye(128, dtype=ml_dtypes.bfloat16), name="cIdbf")
    identf_d = nc.inline_tensor(np.eye(128, dtype=np.float32), name="cIdf")
    onesbf_d = nc.inline_tensor(np.ones((128, 128), ml_dtypes.bfloat16), name="cOnesbf")
    onescolbf_d = nc.inline_tensor(np.ones((128, 1), ml_dtypes.bfloat16), name="cOnescolbf")
    onesrow_d = nc.inline_tensor(np.ones((1, 128), ml_dtypes.bfloat16), name="cOnesrowbf")
    onesf_d = nc.inline_tensor(np.ones((128, 128), np.float32), name="cOnesf")

    pos = (np.arange(128) % MB).astype(np.float32)
    invf = 1.0 / ROPE_THETA ** (np.arange(0, HD, 2, dtype=np.float32) / HD)
    fr = pos[:, None] * invf[None, :]
    emb = np.concatenate([fr, fr], axis=1)
    cos_d = nc.inline_tensor(np.cos(emb).astype(np.float32), name="cCos")
    sinp_d = nc.inline_tensor(
        np.concatenate([-np.sin(emb[:, :32]), np.sin(emb[:, 32:])], axis=1)
        .astype(np.float32), name="cSinp")

    mT = (np.arange(16)[:, None] <= np.arange(16)[None, :]).astype(np.float32)
    negmaskT_np = -np.tile(np.tile(mT, (4, 1)), (2, 16 // 16))
    nm = np.zeros((128, 256), np.float32)
    for blk in range(4):
        for j in range(16):
            for g in range(4):
                for ab in range(2):
                    nm[blk * 32 + j, g * 64 + ab * 16: g * 64 + ab * 16 + 16] = -mT[j]
    negmaskT_d = nc.inline_tensor(nm, name="cNegMaskT")
    invn_d = nc.inline_tensor((1.0 / np.arange(1, MB + 1, dtype=np.float32))
                              .reshape(1, MB), name="cInvN")
    J16 = np.zeros((16, 128), np.float32)
    for p in range(128):
        J16[p % 16, p] = 1.0
    J16_d = nc.inline_tensor(J16, name="cJ16")
    S16 = np.zeros((128, 16), np.float32)
    for p in range(128):
        if p % 32 < 16:
            S16[p, p % 32] = 1.0
    S16_d = nc.inline_tensor(S16, name="cS16")
    I8t = np.zeros((128, 8), np.float32)
    for p in range(128):
        if p % 32 < 8:
            I8t[p, p % 32] = 1.0
    I8s_d = nc.inline_tensor(I8t, name="cI8s")
    bandm_np = np.zeros((128, 8), np.float32)
    for p in range(128):
        bandm_np[p, p // 16] = 1.0
    bandm_d = nc.inline_tensor(bandm_np, name="cBandM")
    halfm_np = np.zeros((128, 2), np.float32)
    halfm_np[0:64, 0] = 1.0
    halfm_np[64:128, 1] = 1.0
    halfm_d = nc.inline_tensor(halfm_np, name="cHalfM")
    blkm_np = np.zeros((128, 4), np.float32)
    for p in range(128):
        if p % 32 < 16:
            blkm_np[p, p // 32] = 1.0
    blkm_d = nc.inline_tensor(blkm_np, name="cBlkM")
    I8s4_np = np.zeros((128, 32), np.float32)
    for bq in range(4):
        for cc in range(8):
            I8s4_np[bq * 32 + cc, bq * 8 + cc] = 1.0
    I8s4_d = nc.inline_tensor(I8s4_np, name="cI8s4")

    sD0 = kb.sem("d0")
    sC0 = kb.sem("c0")

    wlnB = nc.alloc_sbuf_tensor("wlnB", [128, C], F32)
    kb.dma("sync", wlnB[:], AP(dr["ln1_w"], 0, [[0, 128], [1, C]]), sD0)
    blnB = nc.alloc_sbuf_tensor("blnB", [128, C], F32)
    kb.dma("sync", blnB[:], AP(dr["ln1_b"], 0, [[0, 128], [1, C]]), sD0)

    wacc = nc.alloc_sbuf_tensor("wacc", [128, 4], F32)
    swv = nc.alloc_sbuf_tensor("swv", [128, 12], F32)
    onesq_f = nc.alloc_sbuf_tensor("onesq_f", [128, 128], F32)
    kb.dma("sync", onesq_f[:], onesf_d.ap(), sD0)
    pbS = nc.alloc_psum_tensor("pbS", [128, 512], F32)
    pbX = nc.alloc_psum_tensor("pbX", [128, 512], F32)
    t0 = kb.run("vector", lambda e: e.memset(wacc[:], 0.0), sC0)

    if "p0" in st:
        kb.tiny(True)
        sWD = kb.sem("wsd")
        sWR = kb.sem("wsr")
        wsp = [ar.alloc(f"wsp{i}", [128, 2048], F32) for i in range(2)]
        wtmp = ar.alloc("wtmp", [128, 1], F32)
        kb.wait("vector", sC0, t0)
        panels = []
        for name, slot in (("w_gate", 0), ("w_up", 1), ("w_down", 2)):
            md = dr[name]
            rows, cols = md.shape
            for pi in range(rows // 128):
                for si in range(cols // 2048):
                    panels.append((md, slot, pi, si, cols))
        for i, (md, slot, pi, si, cols) in enumerate(panels):
            buf = wsp[i % 2]
            src = AP(md, pi * 128 * cols + si * 2048, [[cols, 128], [1, 2048]])
            if i >= 2:
                kb.wait("scalar", sWR, 2 * (i - 1))
            tl_ = kb.dma("scalar", buf[:], src, sWD)
            kb.wait("vector", sWD, tl_)
            kb.run("vector", lambda e, b_=buf: e.reduce_sum(
                wtmp[:], b_[:], axis=AXL.X, apply_absolute_value=True), sWR)
            kb.run("vector", lambda e, s_=slot: e.tensor_add(
                wacc[:, s_:s_ + 1], wacc[:, s_:s_ + 1], wtmp[:]), sWR)
        sM0 = kb.sem("m0")
        kb.wait("tensor", sWR, sWR.n)
        tm = kb.run("tensor", lambda e: e.matmul(
            pbS[:, 0:4], onesq_f[:], wacc[:], start=True, stop=True), sM0)
        kb.wait("vector", sM0, tm)
        for slot, nel in ((0, IF * C), (1, IF * C), (2, C * IF)):
            kb.run("vector", lambda e, s_=slot, n_=nel: e.tensor_scalar(
                swv[:, s_:s_ + 1], pbS[:, s_:s_ + 1], 1.0 / n_, EPS,
                ALU.mult, ALU.max), sC0)
            kb.run("vector", lambda e, s_=slot: e.tensor_scalar_mul(
                swv[:, 3 + s_:4 + s_], swv[:, s_:s_ + 1], -0.5), sC0)
            kb.run("vector", lambda e, s_=slot: e.tensor_scalar_mul(
                swv[:, 6 + s_:7 + s_], swv[:, s_:s_ + 1], 0.5), sC0)
            kb.run("vector", lambda e, s_=slot: e.tensor_scalar_mul(
                swv[:, 9 + s_:10 + s_], swv[:, s_:s_ + 1], 0.5), sC0)
        d = dbg_dram("swv", (128, 12))
        if d is not None:
            sDB = kb.sem("dbsw")
            kb.wait("gpsimd", sC0, sC0.n)
            kb.dma("gpsimd", d.ap(), swv[:], sDB)
            kb.wait("gpsimd", sDB, sDB.n)

    kb.tiny(False)
    ctx = dict(locals())
    if "p1" in st:
        _emit_p1(ctx)
    if "p2" in st:
        _emit_p2(ctx)
    if "p3" in st:
        _emit_p3(ctx)
    if "p4" in st:
        _emit_p4(ctx)

    kb.finalize()
    return nc, dbg



def _emit_p1(ctx):
    nc, kb, dr = ctx["nc"], ctx["kb"], ctx["dr"]
    ar = ctx["ar"]
    pbS = ctx["pbS"]
    H128_d, identbf_d = ctx["H128_d"], ctx["identbf_d"]
    wlnB = ctx["wlnB"]
    blnB = ctx["blnB"]
    x1c_st_d = ctx["x1c_st_d"]
    dbg_dram = ctx["dbg_dram"]

    q1fm = nc.alloc_sbuf_tensor("q1fm", [128, 16, 2048], BF16)
    lam1 = nc.alloc_sbuf_tensor("lam1", [128, L], F32)
    mrstd = nc.alloc_sbuf_tensor("mrstd", [128, 16, 4], F32)
    ctx["q1fm"] = q1fm; ctx["lam1"] = lam1; ctx["mrstd"] = mrstd

    xch = [ar.alloc(f"xch{i}", [128, C], F32) for i in range(2)]
    xnb = ar.alloc("xnb", [128, C], F32)
    xbb = ar.alloc("xbb", [128, C], F32)
    x1cb = ar.alloc("x1cb", [128, C], BF16)
    xbh = ar.alloc("xbh", [128, C], BF16)
    xbl = ar.alloc("xbl", [128, C], BF16)
    hfmt = ar.alloc("hfmt", [128, 16, 128], BF16)
    lfmt = ar.alloc("lfmt", [128, 16, 128], BF16)
    x1cfm = ar.alloc("x1cfm", [128, 16, 128], BF16)
    f1sb = ar.alloc("f1sb", [128, 16, 128], F32)
    rmaxb = ar.alloc("rmaxb", [128, 128], F32)
    amaxb = ar.alloc("amaxb", [128, 128], F32)
    sc1b = ar.alloc("sc1b", [128, 128], F32)
    yq = ar.alloc("yq", [128, 4, 128], F32)
    mq = ar.alloc("mq", [128, 4, 128], F32)
    tin = ar.alloc("tinyP1", [128, 8], F32)

    H128s = nc.alloc_sbuf_tensor("H128s", [128, 128], BF16)
    idbfs = nc.alloc_sbuf_tensor("idbfs", [128, 128], BF16)
    sD0 = ctx["sD0"]
    kb.dma("sync", H128s[:], H128_d.ap(), sD0)
    kb.dma("sync", idbfs[:], identbf_d.ap(), sD0)

    pbT = nc.alloc_psum_tensor("pbT", [128, 512], BF16)
    pbF = nc.alloc_psum_tensor("pbF", [128, 512], F32)

    sX = kb.sem("p1x")
    sV = kb.sem("p1v")
    sA = kb.sem("p1a")
    sG = kb.sem("p1g")
    sT = kb.sem("p1t")
    sS = kb.sem("p1s")

    onesf_row = nc.alloc_sbuf_tensor("onesf_row", [1, 128], F32)
    amax_row = nc.alloc_sbuf_tensor("amax_row", [1, 128], F32)
    kb.dma("sync", onesf_row[:], AP(ctx["onesf_d"], 0, [[128, 1], [1, 128]]), ctx["sD0"])

    P1CUT = getattr(_emit_p1, "CUT", 99)
    for k in range(1 if P1CUT < 99 else 16):
        xb = xch[k % 2]
        if k >= 2:
            kb.wait("sync", sV, ctx[f"p1_xn_done_{k-2}"])
        tx = kb.dma("sync", xb[:], AP(dr["x"], k * 128 * C, [[C, 128], [1, C]]), sX)

        kb.tiny(True)
        kb.wait("vector", sX, tx)
        kb.run("vector", lambda e, xb=xb: e.reduce_sum(
            tin[:, 5:6], xb[:], axis=AXL.X))
        kb.run("vector", lambda e, xb=xb: e.scalar_tensor_tensor(
            xbb[:], xb[:], 1.0, xb[:], ALU.mult, ALU.mult,
            accum_out=tin[:, 0:1]))
        kb.run("vector", lambda e: e.tensor_scalar_mul(
            tin[:, 6:7], tin[:, 5:6], 1.0 / C))
        kb.run("vector", lambda e: e.tensor_mul(tin[:, 2:3], tin[:, 6:7], tin[:, 6:7]))
        kb.run("vector", lambda e: e.scalar_tensor_tensor(
            tin[:, 7:8], tin[:, 0:1], 1.0 / C, tin[:, 2:3],
            ALU.mult, ALU.subtract))
        tvs1 = kb.run("vector", lambda e: e.tensor_scalar_add(
            tin[:, 3:4], tin[:, 7:8], 1e-5), sV)
        kb.wait("scalar", sV, tvs1)
        tas1 = kb.run("scalar", lambda e: e.activation(
            tin[:, 0:1], tin[:, 3:4], AF.Sqrt), sA)
        kb.wait("vector", sA, tas1)
        kb.run("vector", lambda e: e.reciprocal(tin[:, 1:2], tin[:, 0:1]))
        kb.run("vector", lambda e: e.tensor_mul(tin[:, 2:3], tin[:, 1:2], tin[:, 1:2]))
        kb.run("vector", lambda e: e.tensor_mul(tin[:, 2:3], tin[:, 2:3], tin[:, 7:8]))
        tvs2 = kb.run("vector", lambda e: e.tensor_scalar_add(
            tin[:, 2:3], tin[:, 2:3], 1e-5), sV)
        kb.wait("scalar", sV, tvs2)
        tas2 = kb.run("scalar", lambda e: e.activation(
            tin[:, 3:4], tin[:, 2:3], AF.Sqrt), sA)
        kb.wait("vector", sA, tas2)
        kb.run("vector", lambda e: e.reciprocal(tin[:, 4:5], tin[:, 3:4]))
        kb.run("vector", lambda e, k=k: e.tensor_copy(mrstd[:, k, 0:1], tin[:, 6:7]))
        kb.run("vector", lambda e, k=k: e.tensor_copy(mrstd[:, k, 1:2], tin[:, 1:2]))
        kb.run("vector", lambda e, k=k: e.tensor_copy(mrstd[:, k, 2:3], tin[:, 4:5]))
        kb.tiny(False)
        kb.run("vector", lambda e, xb=xb: e.tensor_scalar(
            xnb[:], xb[:], tin[:, 6:7], tin[:, 1:2], ALU.subtract, ALU.mult))
        kb.run("vector", lambda e: e.scalar_tensor_tensor(
            xbb[:], xnb[:], tin[:, 4:5], wlnB[:], ALU.mult, ALU.mult))
        tv = kb.run("vector", lambda e: e.tensor_add(
            x1cb[:], xbb[:], blnB[:]), sV)
        ctx[f"p1_xn_done_{k}"] = tv
        if k == getattr(_emit_p1, "DBGK", -1):
            _p4tap(ctx, "p1xb", xb[:], (128, 2048), eng="vector")
            _p4tap(ctx, "p1tin", tin[:], (128, 8), eng="vector")
            _p4tap(ctx, "p1xnb", xnb[:], (128, 2048), eng="vector")
        if P1CUT <= 1:
            return

        cur, oth = xnb, xbb
        tglast = None
        for si, hh in enumerate((1, 2, 4, 8)):
            eng = "vector" if si % 2 == 0 else "gpsimd"
            if si == 1:
                kb.wait("gpsimd", sV, sV.n)
            if si == 2:
                kb.wait("vector", sG, sG.n)
            if si == 3:
                kb.wait("gpsimd", sV, sV.n)
            na = 16 // (2 * hh)
            ap_a = AP(cur, 0, [[C, 128], [2 * hh * 128, na], [128, hh], [1, 128]])
            ap_b = AP(cur, hh * 128, [[C, 128], [2 * hh * 128, na], [128, hh], [1, 128]])
            ap_pl = AP(oth, 0, [[C, 128], [2 * hh * 128, na], [128, hh], [1, 128]])
            ap_mi = AP(oth, hh * 128, [[C, 128], [2 * hh * 128, na], [128, hh], [1, 128]])
            ssem = sV if eng == "vector" else sG
            kb.run(eng, lambda e, a=ap_a, b_=ap_b, o=ap_pl: e.tensor_tensor(
                o, a, b_, ALU.add), ssem)
            tglast = kb.run(eng, lambda e, a=ap_a, b_=ap_b, o=ap_mi: e.tensor_tensor(
                o, a, b_, ALU.subtract), ssem)
            cur, oth = oth, cur
        kb.wait("vector", sG, sG.n)
        kb.run("vector", lambda e, c_=cur: e.tensor_copy(xbh[:], c_[:]))
        tv2 = kb.run("vector", lambda e, c_=cur: e.tensor_sub(xbl[:], c_[:], xbh[:]), sV)
        if k == getattr(_emit_p1, "DBGK", -1):
            _p4tap(ctx, "p1bfly", cur[:], (128, 2048), eng="vector")
            _p4tap(ctx, "p1x1cb", x1cb[:], (128, 2048), eng="vector")
        if P1CUT <= 2:
            return

        def emit_T_group(src_bf, dst_fm, wait_sem, wait_val, tag):
            for grp in range(4):
                for q in range(4):
                    ct = grp * 4 + q
                    if q == 0:
                        kb.wait("tensor", wait_sem, wait_val)
                        if (grp > 0 or tag > 0):
                            kb.wait("tensor", sA, sA.n)
                    kb.run("tensor", lambda e, ct=ct, q=q, s_=src_bf: e.transpose(
                        pbT[:, q * 128:(q + 1) * 128], s_[:, ct * 128:(ct + 1) * 128],
                        idbfs[:]), sT if q == 3 else None)
                kb.wait("scalar", sT, sT.n)
                kb.run("scalar", lambda e, grp=grp, d_=dst_fm: e.copy(
                    d_[:, grp * 4:(grp + 1) * 4, :], pbT[:]), sA)

        if k >= 1:
            kb.wait("tensor", sS, sS.n)
        emit_T_group(x1cb, x1cfm, sV, tv, 0)
        v_x1 = sA.n
        emit_T_group(xbh, hfmt, sV, tv2, 1)
        emit_T_group(xbl, lfmt, sV, tv2, 2)

        kb.wait("sync", sA, v_x1)
        kb.dma("sync", AP(x1c_st_d, k * 128 * C, [[C, 128], [1, C]]),
               x1cfm[:], sS)

        if P1CUT <= 3:
            return
        kb.wait("tensor", sA, sA.n)
        if k >= 1:
            kb.wait("scalar", sV, ctx.get("p1_q1_done", 0))
        for grp in range(4):
            for q in range(4):
                ct = grp * 4 + q
                kb.run("tensor", lambda e, ct=ct, q=q: e.matmul(
                    pbF[:, q * 128:(q + 1) * 128], H128s[:], hfmt[:, ct, :],
                    start=True, stop=False))
                kb.run("tensor", lambda e, ct=ct, q=q: e.matmul(
                    pbF[:, q * 128:(q + 1) * 128], H128s[:], lfmt[:, ct, :],
                    start=False, stop=True), sT if q == 3 else None)
            kb.wait("scalar", sT, sT.n)
            kb.run("scalar", lambda e, grp=grp: e.copy(
                f1sb[:, grp * 4:(grp + 1) * 4, :], pbF[:]), sA)
            if grp < 3:
                kb.wait("tensor", sA, sA.n)

        if k == getattr(_emit_p1, "DBGK", -1):
            _p4tap(ctx, "p1hfmt",
                   hfmt[:].rearrange("p a b -> p (a b)"), (128, 2048))
            _p4tap(ctx, "p1f1sb",
                   f1sb[:].rearrange("p a b -> p (a b)"), (128, 2048))
        if P1CUT <= 4:
            return
        kb.wait("vector", sA, sA.n)
        tv_r = kb.run("vector", lambda e: e.tensor_reduce(
            rmaxb[:], AP(f1sb, 0, [[2048, 128], [1, 128], [128, 16]]),
            axis=AXL.X, op=ALU.max, apply_absolute_value=True), sV)
        kb.wait("gpsimd", sV, tv_r)
        tg = kb.run("gpsimd", lambda e: e.tensor_reduce(
            amax_row[:], rmaxb[:], axis=AXL.C, op=ALU.max), sG)
        kb.wait("tensor", sG, tg)
        tm_r = kb.run("tensor", lambda e: e.matmul(
            pbS[:, 128:256], onesf_row[:], amax_row[:],
            start=True, stop=True), sT)
        kb.wait("vector", sT, tm_r)
        if P1CUT <= 5:
            return
        c8 = 1.0 / math.sqrt(2048.0)
        kb.tiny(True)
        kb.run("vector", lambda e: e.tensor_scalar(
            amaxb[:], pbS[:, 128:256], c8, EPS, ALU.mult, ALU.max))
        kb.run("vector", lambda e, k=k: e.tensor_scalar_mul(
            lam1[:, k * 128:(k + 1) * 128], amaxb[:], 1.0 / 127.0))
        kb.run("vector", lambda e: e.reciprocal(sc1b[:], amaxb[:]))
        tv3 = kb.run("vector", lambda e: e.tensor_scalar_mul(
            sc1b[:], sc1b[:], 127.0 * c8), sV)

        if P1CUT <= 6:
            return
        kb.tiny(False)
        for grp in range(4):
            sl = slice(grp * 4, (grp + 1) * 4)
            sc_ap = AP(sc1b, 0, [[128, 128], [0, 4], [1, 128]])
            kb.run("vector", lambda e, sl=sl, sc_ap=sc_ap: e.tensor_tensor(
                yq[:], f1sb[:, sl, :], sc_ap, ALU.mult))
            kb.run("vector", lambda e, k=k, sl=sl: e.tensor_scalar(
                q1fm[:, sl, k * 128:(k + 1) * 128],
                yq[:], 8388736.0, 8388608.0, ALU.add, ALU.subtract), sV)
        ctx["p1_q1_done"] = sV.n

    kb.wait("sync", sS, sS.n)
    ctx["sV"] = sV; ctx["sA"] = sA; ctx["sT"] = sT; ctx["sG"] = sG
    ctx["pbT"] = pbT; ctx["pbF"] = pbF
    ctx["idbfs"] = idbfs
    ctx["H128s"] = H128s
    ctx["onesf_row"] = onesf_row; ctx["amax_row"] = amax_row
    identf_s = nc.alloc_sbuf_tensor("identf_s", [128, 128], F32)
    kb.dma("sync", identf_s[:], ctx["identf_d"].ap(), ctx["sD0"])
    ctx["identf_s"] = identf_s

    d = dbg_dram("q1fm", (128, 16, 2048), BF16)
    if d is not None:
        sDB = kb.sem("dbq1")
        kb.wait("gpsimd", sV, sV.n)
        kb.dma("gpsimd", d.ap(), q1fm[:], sDB)
        kb.wait("gpsimd", sDB, sDB.n)
    d = dbg_dram("lam1", (128, L))
    if d is not None:
        sDB2 = kb.sem("dblam")
        kb.wait("gpsimd", sV, sV.n)
        kb.dma("gpsimd", d.ap(), lam1[:], sDB2)
        kb.wait("gpsimd", sDB2, sDB2.n)


def _emit_p2(ctx):
    nc, kb, dr = ctx["nc"], ctx["kb"], ctx["dr"]
    ar = ctx["ar"]; ar.reset()
    ctx["barrier"]()
    pbS, pbT, pbF = ctx["pbS"], ctx["pbT"], ctx["pbF"]
    q1fm, lam1 = ctx["q1fm"], ctx["lam1"]
    swv = ctx["swv"]
    hfm_d = ctx["hfm_d"]
    idbfs = ctx["idbfs"]
    onesbf_d = ctx["onesbf_d"]
    dbg_dram = ctx["dbg_dram"]

    pbGU = nc.alloc_psum_tensor("pbGU", [128, 2048], F32)
    ctx["pbGU"] = pbGU

    wpan2 = [ar.alloc(f"wpan2_{i}", [128, 2048], F32) for i in range(2)]
    wqT = ar.alloc("wqT", [128, 16, 128], BF16)
    sgA = ar.alloc("sgA", [128, 512], BF16)
    sgB = ar.alloc("sgB", [128, 512], BF16)
    silu_g = ar.alloc("silu_g", [128, 2048], F32)
    sg1 = ar.alloc("sg1", [128, 512], F32)
    hprime = ar.alloc("hprime", [128, 2048], F32)
    csg = ar.alloc("csg", [128, 2], F32)
    onescol_bf = nc.alloc_sbuf_tensor("onescol_bf", [128, 1], BF16)
    ctx["onescol_bf"] = onescol_bf
    kb.dma("sync", onescol_bf[:], ctx["onescolbf_d"].ap(), ctx["sD0"])

    sW2 = kb.sem("p2w")
    sT2 = kb.sem("p2t")
    sA2 = kb.sem("p2a")
    sV2 = kb.sem("p2v")
    sH2 = kb.sem("p2h")

    kb.wait("tensor", ctx["sV"], ctx["sV"].n)
    kb.wait("tensor", ctx["sA"], ctx["sA"].n)
    kb.wait("tensor", ctx["sC0"], ctx["sC0"].n)

    NP = getattr(_emit_p2, 'NP', IF // 128)
    for p in range(NP):
        for m, wname in ((0, "w_gate"), (1, "w_up")):
            buf = wpan2[(2 * p + m) % 2]
            if 2 * p + m >= 2:
                kb.wait("sync", sV2, ctx[f"p2_q_{2 * p + m - 2}"])
            md = dr[wname]
            tw = kb.dma("sync", buf[:], AP(md, p * 128 * C, [[C, 128], [1, C]]), sW2)
            kb.wait("tensor", sW2, tw)
            for grp in range(4):
                if not (p == 0 and m == 0 and grp == 0):
                    kb.wait("tensor", sA2, sA2.n)
                for q in range(4):
                    it = grp * 4 + q
                    kb.run("tensor", lambda e, it=it, q=q, b_=buf: e.transpose(
                        pbF[:, q * 128:(q + 1) * 128],
                        b_[:, it * 128:(it + 1) * 128], ctx["identf_s"][:]),
                        sT2 if q == 3 else None)
                kb.wait("scalar", sT2, sT2.n)
                kb.wait("scalar", sV2, sV2.n)
                kb.run("scalar", lambda e, m=m: e.activation(
                    sgA[:], pbF[:], AF.Sign, bias=swv[:, 3 + m:4 + m]))
                kb.run("scalar", lambda e, m=m: e.activation(
                    sgB[:], pbF[:], AF.Sign, bias=swv[:, 6 + m:7 + m]), sA2)
                kb.wait("vector", sA2, sA2.n)
                tq = kb.run("vector", lambda e, grp=grp: e.tensor_add(
                    wqT[:, grp * 4:(grp + 1) * 4, :], sgA[:], sgB[:]), sV2)
            ctx[f"p2_q_{2 * p + m}"] = tq
            kb.wait("tensor", sV2, tq)
            if m == 0 and p > 0:
                kb.wait("tensor", sH2, ctx[f"p2_h_{p - 1}"])
            for it in range(16):
                for tg in range(4):
                    kb.run("tensor", lambda e, it=it, tg=tg: e.matmul(
                        pbGU[:, tg * 512:(tg + 1) * 512], wqT[:, it, :],
                        q1fm[:, it, tg * 512:(tg + 1) * 512],
                        start=(it == 0), stop=(it == 15)))
                kb.run("tensor", lambda e, it=it, m=m: e.matmul(
                    pbS[:, 260 + m:261 + m], wqT[:, it, :], onescol_bf[:],
                    start=(it == 0), stop=(it == 15)),
                    sT2 if it == 15 else None)
            kb.wait("vector", sT2, sT2.n)
            kb.tiny(True)
            kb.run("vector", lambda e, m=m: e.tensor_scalar_mul(
                csg[:, m:m + 1], pbS[:, 260 + m:261 + m], 128.0))
            kb.tiny(False)
            if m == 0:
                for tg in range(4):
                    if tg > 0:
                        kb.wait("vector", sA2, sA2.n)
                    kb.run("vector", lambda e, tg=tg: e.scalar_tensor_tensor(
                        sg1[:], pbGU[:, tg * 512:(tg + 1) * 512], csg[:, 0:1],
                        lam1[:, tg * 512:(tg + 1) * 512],
                        ALU.subtract, ALU.mult), sV2)
                    kb.wait("scalar", sV2, sV2.n)
                    kb.run("scalar", lambda e, tg=tg: e.activation(
                        silu_g[:, tg * 512:(tg + 1) * 512], sg1[:], AF.Silu,
                        scale=swv[:, 9:10]), sA2)
                kb.wait("tensor", sA2, sA2.n)
            else:
                if p > 0:
                    kb.wait("vector", sH2, ctx[f"p2_h_{p - 1}"])
                for tg in range(4):
                    kb.run("vector", lambda e, tg=tg: e.scalar_tensor_tensor(
                        hprime[:, tg * 512:(tg + 1) * 512],
                        pbGU[:, tg * 512:(tg + 1) * 512], csg[:, 1:2],
                        silu_g[:, tg * 512:(tg + 1) * 512],
                        ALU.subtract, ALU.mult), sV2)
                kb.wait("gpsimd", sV2, sV2.n)
                th = kb.dma("gpsimd", AP(hfm_d, p * 128 * L, [[L, 128], [1, L]]),
                            hprime[:], sH2)
                ctx[f"p2_h_{p}"] = th

    kb.wait("gpsimd", sH2, sH2.n)
    d = dbg_dram("csg", (128, 2))
    if d is not None:
        sDB = kb.sem("dbcs")
        kb.dma("gpsimd", d.ap(), csg[:], sDB)
        kb.wait("gpsimd", sDB, sDB.n)



def _emit_p3(ctx):
    nc, kb, dr = ctx["nc"], ctx["kb"], ctx["dr"]
    ar = ctx["ar"]; ar.reset()
    ctx["barrier"]()
    pbS, pbT, pbF, pbGU = ctx["pbS"], ctx["pbT"], ctx["pbF"], ctx["pbGU"]
    q1fm, lam1, swv = ctx["q1fm"], ctx["lam1"], ctx["swv"]
    hfm_d, g_st_d, wdT_st_d = ctx["hfm_d"], ctx["g_st_d"], ctx["wdT_st_d"]
    H128s, identf_s, idbfs = ctx["H128s"], ctx["identf_s"], ctx["idbfs"]
    onesf_row = ctx["onesf_row"]
    onescol_bf = ctx["onescol_bf"]
    c13 = 1.0 / math.sqrt(float(IF))

    csd = nc.alloc_sbuf_tensor("csd", [128, 16], F32)
    lam2 = nc.alloc_sbuf_tensor("lam2", [128, 4, 512], F32)

    sW3 = kb.sem("p3w")
    sT3 = kb.sem("p3t")
    sA3 = kb.sem("p3a")
    sV3 = kb.sem("p3v")
    sG3 = kb.sem("p3g")
    sS3 = kb.sem("p3s")
    sH3 = kb.sem("p3h")
    sD3 = kb.sem("p3d")
    sGS = kb.sem("p3gs")

    wdp = [ar.alloc(f"wdp{i}", [128, 4096], F32) for i in range(2)]
    wqdT = ar.alloc("wqdT", [128, 4, 128], BF16)
    sdA = ar.alloc("sdA", [128, 512], BF16)
    sdB = ar.alloc("sdB", [128, 512], BF16)
    prev_stash = 0
    for cp in range(16):
        for half in range(2):
            if cp > 0:
                kb.wait("sync", sT3, ctx[f"p3a_T_{cp - 1}"])
            kb.dma("sync", wdp[half][:],
                   AP(dr["w_down"], cp * 128 * IF + half * 4096,
                      [[IF, 128], [1, 4096]]), sW3)
        kb.wait("tensor", sW3, sW3.n)
        for grp in range(16):
            if not (cp == 0 and grp == 0):
                kb.wait("tensor", sA3, sA3.n)
            for q in range(4):
                it = grp * 4 + q
                half, loc = divmod(it, 32)
                kb.run("tensor", lambda e, q=q, half=half, loc=loc: e.transpose(
                    pbF[:, q * 128:(q + 1) * 128],
                    wdp[half][:, loc * 128:(loc + 1) * 128], identf_s[:]),
                    sT3 if q == 3 else None)
            if grp == 15:
                ctx[f"p3a_T_{cp}"] = sT3.n
            kb.wait("scalar", sT3, sT3.n)
            kb.wait("scalar", sV3, sV3.n)
            kb.run("scalar", lambda e: e.activation(
                sdA[:], pbF[:], AF.Sign, bias=swv[:, 5:6]))
            kb.run("scalar", lambda e: e.activation(
                sdB[:], pbF[:], AF.Sign, bias=swv[:, 8:9]), sA3)
            kb.wait("vector", sA3, sA3.n)
            kb.wait("vector", sS3, prev_stash)
            tq = kb.run("vector", lambda e: e.tensor_add(
                wqdT[:], sdA[:], sdB[:]), sV3)
            kb.wait("tensor", sV3, tq)
            for q in range(4):
                kb.run("tensor", lambda e, q=q, grp=grp: e.matmul(
                    pbS[:, 300:301], wqdT[:, q, :], onescol_bf[:],
                    start=(grp == 0 and q == 0), stop=(grp == 15 and q == 3)),
                    sT3 if (grp == 15 and q == 3) else None)
            kb.wait("gpsimd", sV3, tq)
            for q in range(4):
                it = grp * 4 + q
                kb.dma("gpsimd",
                       AP(wdT_st_d, (cp * 64 + it) * 128 * 128,
                          [[128, 128], [1, 128]]),
                       wqdT[:, q, :], sS3)
            prev_stash = sS3.n
        kb.wait("vector", sT3, sT3.n)
        kb.run("vector", lambda e, cp=cp: e.tensor_scalar_mul(
            csd[:, cp:cp + 1], pbS[:, 300:301], 128.0), sV3)

    ar.reset()
    ctx["barrier"]()
    bflyA = ar.alloc("bflyA", [128, 8192], F32)
    bflyB = ar.alloc("bflyB", [128, 8192], F32)
    aA = nc.lookup_mloc(bflyA).addr
    aB = nc.lookup_mloc(bflyB).addr
    h2h = nc.alloc_sbuf_tensor_at("h2h", [128, 64, 128], BF16, offset=aA)
    h2l = nc.alloc_sbuf_tensor_at("h2l", [128, 64, 128], BF16, offset=aA + 16384)
    f2sb = nc.alloc_sbuf_tensor_at("f2sb", [128, 64, 128], F32, offset=aB)
    yq3 = ar.alloc("yq3", [128, 4, 128], F32)
    mq3 = ar.alloc("mq3", [128, 4, 128], F32)
    rmax3 = ar.alloc("rmax3", [128, 128], F32)
    amaxb3 = ar.alloc("amaxb3", [128, 128], F32)
    sc3b = ar.alloc("sc3b", [128, 128], F32)
    dtb3 = ar.alloc("dtb3", [128, 128], F32)
    am_row3 = ar.alloc("am_row3", [1, 128], F32)
    gsin = ar.alloc("gsin", [128, 512], F32)
    gfm = ar.alloc("gfm", [128, 512], BF16)
    gpc = ar.alloc("gpc", [128, 4, 128], BF16)
    wdt = [ar.alloc(f"wdt{i}", [128, 8, 128], BF16) for i in range(2)]
    q1addr = nc.lookup_mloc(q1fm).addr
    q2fm = nc.alloc_sbuf_tensor_at("q2fm", [128, 64, 512], BF16, offset=q1addr)

    for tc in range(4):
        for sub in range(4):
            t0 = tc * 512 + sub * 128
            if not (tc == 0 and sub == 0):
                kb.wait("sync", sV3, ctx["p3b_q2_done"])
            th = kb.dma("sync", bflyB[:],
                        AP(hfm_d, t0, [[L, 128], [128 * L, 64], [1, 128]]), sH3)
            cur, oth = bflyB, bflyA
            kb.wait("vector", sH3, th)
            kb.wait("gpsimd", sH3, th)
            for si, hh in enumerate((1, 2, 4, 8, 16, 32)):
                eng = "vector" if si % 2 == 0 else "gpsimd"
                osem = sV3 if eng == "vector" else sG3
                if si > 0:
                    prev = sG3 if si % 2 == 0 else sV3
                    kb.wait(eng, prev, prev.n)
                na = 64 // (2 * hh)
                dims = [[8192, 128], [2 * hh * 128, na], [128, hh], [1, 128]]
                kb.run(eng, lambda e, a=AP(cur, 0, dims), b_=AP(cur, hh * 128, dims),
                       o=AP(oth, 0, dims): e.tensor_tensor(o, a, b_, ALU.add), osem)
                kb.run(eng, lambda e, a=AP(cur, 0, dims), b_=AP(cur, hh * 128, dims),
                       o=AP(oth, hh * 128, dims): e.tensor_tensor(o, a, b_, ALU.subtract), osem)
                cur, oth = oth, cur
            kb.wait("vector", sG3, sG3.n)
            kb.run("vector", lambda e, c_=cur: e.tensor_copy(
                h2h[:].rearrange("p a b -> p (a b)"), c_[:]))
            tbf = kb.run("vector", lambda e, c_=cur: e.tensor_sub(
                h2l[:].rearrange("p a b -> p (a b)"), c_[:],
                h2h[:].rearrange("p a b -> p (a b)")), sV3)
            kb.wait("tensor", sV3, tbf)
            for grp in range(16):
                if grp > 0:
                    kb.wait("tensor", sA3, sA3.n)
                for q in range(4):
                    it = grp * 4 + q
                    kb.run("tensor", lambda e, it=it, q=q: e.matmul(
                        pbF[:, q * 128:(q + 1) * 128], H128s[:], h2h[:, it, :],
                        start=True, stop=False))
                    kb.run("tensor", lambda e, it=it, q=q: e.matmul(
                        pbF[:, q * 128:(q + 1) * 128], H128s[:], h2l[:, it, :],
                        start=False, stop=True), sT3 if q == 3 else None)
                kb.wait("scalar", sT3, sT3.n)
                kb.run("scalar", lambda e, grp=grp: e.copy(
                    f2sb[:, grp * 4:(grp + 1) * 4, :], pbF[:]), sA3)
            kb.wait("vector", sA3, sA3.n)
            tvr = kb.run("vector", lambda e: e.tensor_reduce(
                rmax3[:], AP(f2sb, 0, [[8192, 128], [1, 128], [128, 64]]),
                axis=AXL.X, op=ALU.max, apply_absolute_value=True), sV3)
            kb.wait("gpsimd", sV3, tvr)
            tg3 = kb.run("gpsimd", lambda e: e.tensor_reduce(
                am_row3[:], rmax3[:], axis=AXL.C, op=ALU.max), sG3)
            kb.wait("tensor", sG3, tg3)
            tm3 = kb.run("tensor", lambda e: e.matmul(
                pbS[:, 128:256], onesf_row[:], am_row3[:],
                start=True, stop=True), sT3)
            kb.wait("vector", sT3, tm3)
            kb.tiny(True)
            kb.run("vector", lambda e, t0=t0: e.tensor_scalar(
                dtb3[:], lam1[:, t0:t0 + 128], swv[:, 10:11], 1.0,
                ALU.mult, ALU.mult))
            kb.run("vector", lambda e: e.tensor_mul(
                amaxb3[:], pbS[:, 128:256], dtb3[:]))
            kb.run("vector", lambda e: e.tensor_scalar(
                amaxb3[:], amaxb3[:], c13, EPS, ALU.mult, ALU.max))
            kb.run("vector", lambda e, tc=tc, sub=sub: e.tensor_scalar_mul(
                lam2[:, tc, sub * 128:(sub + 1) * 128], amaxb3[:], 1.0 / 127.0))
            kb.run("vector", lambda e: e.reciprocal(sc3b[:], amaxb3[:]))
            kb.run("vector", lambda e: e.tensor_mul(sc3b[:], sc3b[:], dtb3[:]))
            kb.run("vector", lambda e: e.tensor_scalar_mul(
                sc3b[:], sc3b[:], 127.0 * c13), sV3)
            kb.tiny(False)
            for grp in range(16):
                isl = slice(grp * 4, (grp + 1) * 4)
                sc_ap = AP(sc3b, 0, [[128, 128], [0, 4], [1, 128]])
                kb.run("vector", lambda e, isl=isl, sc_ap=sc_ap: e.tensor_tensor(
                    yq3[:], f2sb[:, isl, :], sc_ap, ALU.mult))
                kb.run("vector", lambda e, grp=grp, sub=sub: e.tensor_scalar(
                    AP(q2fm, (grp * 4) * 512 + sub * 128,
                       [[64 * 512, 128], [512, 4], [1, 128]]),
                    yq3[:], 8388736.0, 8388608.0, ALU.add, ALU.subtract), sV3)
            ctx["p3b_q2_done"] = sV3.n

        kb.wait("tensor", sV3, sV3.n)
        mm_grp_done = {}
        for cp in range(16):
            qtr = cp % 4
            for g in range(8):
                if cp * 8 + g >= 2:
                    kb.wait("gpsimd", sT3, mm_grp_done[cp * 8 + g - 2])
                kb.dma("gpsimd", wdt[g % 2][:],
                       AP(wdT_st_d, (cp * 64 + g * 8) * 128 * 128,
                          [[128, 128], [128 * 128, 8], [1, 128]]), sD3)
                kb.wait("tensor", sD3, sD3.n)
                if g == 0 and cp >= 4:
                    kb.wait("tensor", sV3, ctx[f"p3b_gs_{tc}_{cp - 4}"])
                for j in range(8):
                    it = g * 8 + j
                    kb.run("tensor", lambda e, g=g, j=j, it=it, qtr=qtr: e.matmul(
                        pbGU[:, qtr * 512:(qtr + 1) * 512],
                        wdt[g % 2][:, j, :], q2fm[:, it, :],
                        start=(it == 0), stop=(it == 63)),
                        sT3 if j == 7 else None)
                mm_grp_done[cp * 8 + g] = sT3.n
            kb.wait("vector", sT3, sT3.n)
            if cp >= 1:
                kb.wait("vector", sA3, ctx[f"p3b_silu_{tc}_{cp - 1}"])
            tgs = kb.run("vector", lambda e, cp=cp, qtr=qtr, tc=tc: e.scalar_tensor_tensor(
                gsin[:], pbGU[:, qtr * 512:(qtr + 1) * 512], csd[:, cp:cp + 1],
                lam2[:, tc, :], ALU.subtract, ALU.mult), sV3)
            ctx[f"p3b_gs_{tc}_{cp}"] = tgs
            kb.wait("scalar", sV3, tgs)
            if cp >= 1:
                kb.wait("scalar", sT3, ctx[f"p3b_gT_{tc}_{cp - 1}"])
            tga = kb.run("scalar", lambda e: e.activation(
                gfm[:], gsin[:], AF.Silu, scale=swv[:, 11:12]), sA3)
            ctx[f"p3b_silu_{tc}_{cp}"] = tga
            kb.wait("tensor", sA3, tga)
            for sb_ in range(4):
                kb.run("tensor", lambda e, sb_=sb_: e.transpose(
                    pbT[:, sb_ * 128:(sb_ + 1) * 128],
                    gfm[:, sb_ * 128:(sb_ + 1) * 128], idbfs[:]),
                    sT3 if sb_ == 3 else None)
            ctx[f"p3b_gT_{tc}_{cp}"] = sT3.n
            kb.wait("scalar", sT3, sT3.n)
            kb.wait("scalar", sGS, sGS.n)
            kb.run("scalar", lambda e: e.copy(gpc[:], pbT[:]), sA3)
            kb.wait("sync", sA3, sA3.n)
            for sb_ in range(4):
                kb.dma("sync",
                       AP(g_st_d, (tc * 4 + sb_) * 128 * C + cp * 128,
                          [[C, 128], [1, 128]]),
                       gpc[:, sb_, :], sGS)

    kb.wait("sync", sGS, sGS.n)


def _emit_p4(ctx):
    nc, kb, dr = ctx["nc"], ctx["kb"], ctx["dr"]
    ar = ctx["ar"]; ar.reset()
    ctx["barrier"]()
    pbS, pbT, pbF, pbGU, pbX = (ctx["pbS"], ctx["pbT"], ctx["pbF"],
                                ctx["pbGU"], ctx["pbX"])
    lam1, swv, mrstd = ctx["lam1"], ctx["swv"], ctx["mrstd"]
    identf_s, idbfs = ctx["identf_s"], ctx["idbfs"]
    onesf_row = ctx["onesf_row"]
    sD0 = ctx["sD0"]
    c = ctx

    sW4 = kb.sem("p4w")
    sX4 = kb.sem("p4x")
    sT4 = kb.sem("p4t")
    sA4 = kb.sem("p4a")
    sV4 = kb.sem("p4v")
    sG4 = kb.sem("p4g")
    sO4 = kb.sem("p4o")
    sST = kb.sem("p4st")

    lwB = ar.alloc("lwBsc", [128, 512], F32)
    lw2B = ar.alloc("lw2Bsc", [128, 512], F32)
    lbB = ar.alloc("lbBsc", [128, 512], F32)
    b1B = ar.alloc("b1Bsc", [128, 512], F32)
    W1bd = ar.alloc("W1bd", [128, 2048], BF16)
    W1sb = ar.alloc("W1sb", [128, 1024], F32)
    negmtB = ar.alloc("negmtB", [128, 256], F32)
    tokB = ar.alloc("tokB", [128, 16], F32)
    ntokones = ar.alloc("ntokones", [128, 16], BF16)
    ntok15 = ar.alloc("ntok15", [128, 1], F32)
    ntok15m = ar.alloc("ntok15m", [128, 4], F32)
    ntokm4 = ar.alloc("ntokm4", [128, 4, 16], BF16)
    halfmS = ar.alloc("halfmS", [128, 2], F32)
    blkmS = ar.alloc("blkmS", [128, 4], F32)
    I8s4s = ar.alloc("I8s4s", [128, 32], F32)
    lrwT = ar.alloc("lrwT", [128, 16, 128], BF16)
    lrbC = ar.alloc("lrbC", [128, 1], F32)
    lbfmB = ar.alloc("lbfmB", [128, 16], F32)
    lwfmB = ar.alloc("lwfmB", [128, 16], F32)
    cosB = ar.alloc("cosB", [128, 64], F32)
    sinpB = ar.alloc("sinpB", [128, 64], F32)
    ncsrow = ar.alloc("ncsrow", [1, 2048], BF16)
    pbowB = ar.alloc("pbowB", [128, 2048], BF16)
    b1frepl = ar.alloc("b1frepl", [128, 1024], F32)
    b2frow = ar.alloc("b2frow", [1, 2048], BF16)
    I8sb = ar.alloc("I8sb", [128, 8], F32)
    J16s = ar.alloc("J16s", [16, 128], F32)
    S16s = ar.alloc("S16s", [128, 16], F32)
    bandmS = ar.alloc("bandmS", [128, 8], F32)
    onesrow_bf = ar.alloc("onesrow_bf", [1, 128], BF16)
    onescol_bf4 = ar.alloc("onescol_bf4", [128, 1], BF16)
    murow = ar.alloc("murow", [1, 128], BF16)
    negmtS = ar.alloc("negmtS", [128, 256], F32)
    pwcol = ar.alloc("pwcol", [128, 16], F32)
    fwcol = ar.alloc("fwcol", [128, 16], F32)
    fbcol = ar.alloc("fbcol", [128, 16], F32)
    pbcol = ar.alloc("pbcol", [128, 16], F32)
    q1addr = nc.lookup_mloc(ctx["q1fm"]).addr
    ar2 = Arena(nc, base=q1addr, top=q1addr + 65536)
    flat1 = ar2.alloc("flat1", [1, 3072], F32)
    flat2 = ar2.alloc("flat2", [1, 4096], F32)
    flat3 = ar2.alloc("flat3", [1, 2560], F32)

    kb.dma("sync", cosB[:], c["cos_d"].ap(), sD0)
    kb.dma("sync", sinpB[:], c["sinp_d"].ap(), sD0)
    kb.dma("sync", I8sb[:], c["I8s_d"].ap(), sD0)
    kb.dma("sync", J16s[:], c["J16_d"].ap(), sD0)
    kb.dma("sync", S16s[:], c["S16_d"].ap(), sD0)
    kb.dma("sync", bandmS[:], c["bandm_d"].ap(), sD0)
    kb.dma("sync", halfmS[:], c["halfm_d"].ap(), sD0)
    kb.dma("sync", blkmS[:], c["blkm_d"].ap(), sD0)
    kb.dma("sync", I8s4s[:], c["I8s4_d"].ap(), sD0)
    kb.dma("sync", onesrow_bf[:], c["onesrow_d"].ap(), sD0)
    kb.dma("sync", onescol_bf4[:], c["onescolbf_d"].ap(), sD0)
    kb.dma("sync", negmtS[:], c["negmaskT_d"].ap(), sD0)
    kb.dma("sync", pwcol[:], AP(dr["post_w"], 0, [[1, 128], [128, 16]]), sD0, slow=True)
    kb.dma("sync", fwcol[:], AP(dr["ffn_ln_w"], 0, [[1, 128], [128, 16]]), sD0, slow=True)
    kb.dma("sync", fbcol[:], AP(dr["ffn_ln_b"], 0, [[1, 128], [128, 16]]), sD0, slow=True)
    kb.dma("gpsimd", pbcol[:], AP(dr["post_b"], 0, [[1, 128], [128, 16]]), sG4, slow=True)
    kb.dma("sync", lbfmB[:], AP(dr["ttt_norm_b"], 0, [[1, 128], [128, 16]]), sD0, slow=True)
    kb.dma("sync", lwfmB[:], AP(dr["ttt_norm_w"], 0, [[1, 128], [128, 16]]), sD0, slow=True)
    tnwrow = flat1[:, 0:2048]
    tnbrow = flat2[:, 0:2048]
    b1row = flat3[:, 0:2048]
    tiorow = flat3[:, 2048:2064]
    invrow = flat3[:, 2064:2080]
    lrbrow = flat3[:, 2080:2112]
    tokrow = flat3[:, 2128:2144]
    lrbsp = flat3[:, 2176:2304]
    onecol_f = flat3[:, 2304:2305]
    b1fr_f = flat1[:, 2048:3072]
    b2fr_f = flat2[:, 2048:4096]
    kb.dma("sync", tnwrow, AP(dr["ttt_norm_w"], 0, [[2048, 1], [1, 2048]]), sD0)
    kb.dma("sync", tnbrow, AP(dr["ttt_norm_b"], 0, [[2048, 1], [1, 2048]]), sD0)
    kb.dma("sync", b1row, AP(dr["b1"], 0, [[2048, 1], [1, 2048]]), sD0)
    kb.dma("sync", tiorow, AP(dr["token_idx_off"], 0, [[16, 1], [1, 16]]), sD0)
    kb.dma("sync", invrow, c["invn_d"].ap(), sD0)
    kb.dma("sync", lrbrow, AP(dr["ttt_lr_b"], 0, [[32, 1], [1, 32]]), sD0)
    kb.dma("sync", b2fr_f, AP(dr["b2f"], 0, [[2048, 1], [1, 2048]]), sD0)
    kb.dma("sync", b1fr_f, AP(dr["b1f"], 0, [[1024, 1], [1, 1024]]), sD0)
    kb.wait("vector", sD0, sD0.n)
    kb.wait("vector", sG4, sG4.n)

    kb.tiny(True)
    kb.run("vector", lambda e: e.tensor_add(tokrow, invrow, tiorow))
    tv = kb.run("vector", lambda e: e.tensor_scalar_max(tokrow, tokrow, 0.0), sV4)
    kb.wait("tensor", sV4, tv)
    tm = kb.run("tensor", lambda e: e.matmul(
        pbS[:, 0:16], onesf_row[:], tokrow, start=True, stop=True), sT4)
    kb.wait("vector", sT4, tm)
    kb.run("vector", lambda e: e.tensor_copy(tokB[:], pbS[:, 0:16]))
    kb.run("vector", lambda e: e.tensor_scalar_mul(
        ntokones[:], AP(tokB, 15, [[16, 128], [0, 16]]), -1.0))
    kb.run("vector", lambda e: e.tensor_scalar_mul(
        ntok15[:], tokB[:, 15:16], -1.0))
    kb.run("vector", lambda e: e.tensor_scalar_mul(
        ntok15m[:], blkmS[:], ntok15[:]))
    for blk in range(4):
        kb.run("vector", lambda e, blk=blk: e.tensor_scalar_mul(
            ntokm4[:, blk, :], ntokones[:], blkmS[:, blk:blk + 1]))
    kb.run("vector", lambda e: e.tensor_tensor(
        negmtB[:], negmtS[:], AP(tokB, 0, [[16, 128], [0, 16], [1, 16]]),
        ALU.mult))
    kb.run("vector", lambda e: e.memset(lrbsp, 0.0))
    kb.run("vector", lambda e: e.memset(onecol_f, 1.0))
    tv2 = kb.run("vector", lambda e: e.tensor_copy(
        AP(lrbsp.tensor, lrbsp.offset, [[2560, 1], [2, 4], [32, 4], [1, 2]]),
        AP(lrbrow.tensor, lrbrow.offset, [[2560, 1], [8, 4], [2, 4], [1, 2]])), sV4)
    kb.wait("tensor", sV4, tv2)
    tm2 = kb.run("tensor", lambda e: e.matmul(
        pbS[:, 16:17], lrbsp, onecol_f, start=True, stop=True), sT4)
    kb.wait("vector", sT4, tm2)
    kb.run("vector", lambda e: e.tensor_copy(lrbC[:], pbS[:, 16:17]), sV4)

    kb.wait("tensor", sV4, sV4.n)
    targets = [(tnwrow, pbF, 0), (tnbrow, pbGU, 0), (b1row, pbGU, 512)]
    for srow, ps, off in targets:
        for blk in range(4):
            rsz = srow.tensor.shape[1]
            rhs = AP(srow.tensor, srow.offset + blk * 128,
                     [[rsz, 1], [512, 4], [64, 2], [1, 64]])
            kb.run("tensor", lambda e, blk=blk, rhs=rhs, ps=ps, off=off: e.matmul(
                ps[blk * 32:blk * 32 + 32, off:off + 512],
                onesf_row[0:1, 0:32], rhs, start=True, stop=True,
                tile_position=(0, blk * 32)),
                sT4 if blk == 3 else None)
    kb.wait("vector", sT4, sT4.n)
    kb.run("vector", lambda e: e.tensor_copy(lwB[:], pbF[:, 0:512]))
    kb.run("vector", lambda e: e.tensor_copy(lbB[:], pbGU[:, 0:512]))
    kb.run("vector", lambda e: e.tensor_copy(b1B[:], pbGU[:, 512:1024]))
    tv3 = kb.run("vector", lambda e: e.tensor_mul(lw2B[:], lwB[:], lwB[:]), sV4)

    P4CUT = getattr(_emit_p4, "CUT", 99)
    ctx["P4CUT"] = P4CUT
    if P4CUT <= 1:
        return
    lrw_s = ar2.alloc("lrw_s", [32, 2048], BF16)
    kb.dma("gpsimd", lrw_s[:], AP(dr["ttt_lr_w"], 0, [[2048, 32], [1, 2048]]), sG4)
    kb.wait("tensor", sG4, sG4.n)
    kb.wait("vector", sV4, sV4.n)
    kb.run("vector", lambda e: e.memset(
        lrwT[:].rearrange("p a b -> p (a b)"), 0.0))
    zbT = ar2.alloc("zbT", [128, 128], BF16)
    tv4 = kb.run("vector", lambda e: e.memset(zbT[:], 0.0), sV4)
    kb.wait("tensor", sV4, tv4)
    kb.wait("tensor", sD0, sD0.n)
    for zi in range(4):
        kb.run("tensor", lambda e, zi=zi: e.transpose(
            pbT[:, zi * 128:(zi + 1) * 128], zbT[:], idbfs[:]))
    for kc in range(16):
        tt = kb.run("tensor", lambda e, kc=kc: e.transpose(
            pbT[:, 0:32], lrw_s[:, kc * 128:(kc + 1) * 128],
            idbfs[0:32, 0:32]), sT4)
        kb.wait("vector", sT4, tt)
        tv5 = kb.run("vector", lambda e, kc=kc: e.tensor_copy(
            AP(lrwT, kc * 128, [[16 * 128, 128], [2, 4], [32, 4], [1, 2]]),
            AP(pbT, 0, [[512, 128], [8, 4], [2, 4], [1, 2]])), sV4)
        kb.wait("tensor", sV4, tv5)

    _stash_mark = ar.mark()
    wsp4 = [ar.alloc(f"wsp4_{i}", [128, 2048], F32) for i in range(2)]
    wsb4 = [ar.alloc(f"wsb4_{i}", [128, 2048], BF16) for i in range(2)]
    prev_cast = {}
    jobs = ([("q_w", c["qw_st_d"], pi, None) for pi in range(16)]
            + [("k_w", c["kw_st_d"], pi, None) for pi in range(16)]
            + [("v_w", c["vw_st_d"], pi, None) for pi in range(16)]
            + [("o_w", c["ow_st_d"], pi, "pw") for pi in range(16)]
            + [("w1f", c["w1f_st_d"], pi, "fw") for pi in range(16)]
            + [("w2f", c["w2f_st_d"], pi, None) for pi in range(8)])
    prev_mmT = {}
    for ji, (wname, dst, pi, fold) in enumerate(jobs):
        cols = dr[wname].shape[1]
        buf, cbuf = wsp4[ji % 2], wsb4[ji % 2]
        if ji >= 2:
            kb.wait("sync", sST, prev_cast[ji - 2])
            if (ji - 2) in prev_mmT:
                kb.wait("sync", sT4, prev_mmT[ji - 2])
                kb.wait("vector", sT4, prev_mmT[ji - 2])
        kb.dma("sync", buf[:, 0:cols],
               AP(dr[wname], pi * 128 * cols, [[cols, 128], [1, cols]]), sX4)
        kb.wait("vector", sX4, sX4.n)
        if wname == "w1f":
            kb.wait("tensor", sX4, sX4.n)
            for w in range(2):
                kb.run("tensor", lambda e, buf=buf, w=w, pi=pi: e.matmul(
                    pbGU[64:65, w * 512:(w + 1) * 512], fbcol[:, pi:pi + 1],
                    buf[:, w * 512:(w + 1) * 512],
                    start=(pi == 0), stop=(pi == 15), tile_position=(0, 64)),
                    sT4 if w == 1 else None)
            prev_mmT[ji] = sT4.n
        if fold == "pw":
            kb.run("vector", lambda e, buf=buf, cbuf=cbuf, pi=pi, cols=cols:
                   e.tensor_scalar_mul(cbuf[:, 0:cols], buf[:, 0:cols],
                                       pwcol[:, pi:pi + 1]), sV4)
        elif fold == "fw":
            kb.run("vector", lambda e, buf=buf, cbuf=cbuf, pi=pi, cols=cols:
                   e.tensor_scalar_mul(cbuf[:, 0:cols], buf[:, 0:cols],
                                       fwcol[:, pi:pi + 1]), sV4)
        else:
            kb.run("vector", lambda e, buf=buf, cbuf=cbuf, cols=cols:
                   e.tensor_copy(cbuf[:, 0:cols], buf[:, 0:cols]), sV4)
        kb.wait("sync", sV4, sV4.n)
        kb.dma("sync", AP(dst, pi * 128 * cols, [[cols, 128], [1, cols]]),
               cbuf[:, 0:cols], sST)
        prev_cast[ji] = sST.n
        if wname == "o_w":
            kb.wait("tensor", sV4, sV4.n)
            kb.wait("tensor", sX4, sX4.n)
            for w in range(4):
                kb.run("tensor", lambda e, cbuf=cbuf, w=w, pi=pi: e.matmul(
                    pbGU[0:1, w * 512:(w + 1) * 512], onescol_bf4[:],
                    cbuf[:, w * 512:(w + 1) * 512],
                    start=(pi == 0), stop=(pi == 15)))
                kb.run("tensor", lambda e, buf=buf, w=w, pi=pi: e.matmul(
                    pbGU[32:33, w * 512:(w + 1) * 512], pbcol[:, pi:pi + 1],
                    buf[:, w * 512:(w + 1) * 512],
                    start=(pi == 0), stop=(pi == 15), tile_position=(0, 32)),
                    sT4 if (pi == 15 and w == 3) else None)
            prev_mmT[ji] = sT4.n
    kb.wait("vector", sT4, sT4.n)
    kb.run("vector", lambda e: e.tensor_copy(ncsrow[:], pbGU[0:1, 0:2048]))
    pbrow1 = ar2.alloc("pbrow1", [1, 2048], BF16)
    tv6 = kb.run("vector", lambda e: e.tensor_copy(
        pbrow1[:], pbGU[32:33, 0:2048]), sV4)
    b1fr_bf = ar2.alloc("b1fr_bf", [1, 1024], BF16)
    kb.run("vector", lambda e: e.tensor_tensor(
        b1fr_bf[:], b1fr_f, pbGU[64:65, 0:1024], ALU.add))
    kb.run("vector", lambda e: e.tensor_copy(b2frow[:], b2fr_f), sV4)
    kb.wait("tensor", sV4, sV4.n)
    for w in range(4):
        tmr = kb.run("tensor", lambda e, w=w: e.matmul(
            pbF[:, 0:512], onesrow_bf[:], pbrow1[0:1, w * 512:(w + 1) * 512],
            start=True, stop=True), sT4)
        kb.wait("vector", sT4, tmr)
        tvr = kb.run("vector", lambda e, w=w: e.tensor_copy(
            pbowB[:, w * 512:(w + 1) * 512], pbF[:, 0:512]), sV4)
        kb.wait("tensor", sV4, tvr)
    for w in range(2):
        tmr = kb.run("tensor", lambda e, w=w: e.matmul(
            pbF[:, 0:512], onesrow_bf[:], b1fr_bf[0:1, w * 512:(w + 1) * 512],
            start=True, stop=True), sT4)
        kb.wait("vector", sT4, tmr)
        tvr = kb.run("vector", lambda e, w=w: e.tensor_copy(
            b1frepl[:, w * 512:(w + 1) * 512], pbF[:, 0:512]), sV4)
        kb.wait("tensor", sV4, tvr)

    sWi = kb.sem("p4wi")
    kb.dma("sync", W1sb[0:64, :],
           AP(dr["W1"], 0, [[64, 64], [8192, 16], [1, 64]]), sWi)
    kb.dma("sync", W1sb[64:128, :],
           AP(dr["W1"], 4096, [[64, 64], [8192, 16], [1, 64]]), sWi)
    kb.wait("vector", sWi, sWi.n)
    kb.wait("vector", sT4, sT4.n)
    kb.run("vector", lambda e: e.memset(pbGU[:, 0:1024], 0.0))
    for zi in range(4):
        kb.run("tensor", lambda e, zi=zi: e.transpose(
            pbT[:, zi * 128:(zi + 1) * 128], zbT[:], idbfs[:]))
    kb.run("vector", lambda e: e.memset(pbF[:], 0.0))
    kb.run("vector", lambda e: e.memset(pbX[:], 0.0))
    kb.run("vector", lambda e: e.memset(pbS[:], 0.0))
    kb.run("vector", lambda e: e.memset(W1bd[:], 0.0))
    kb.run("vector", lambda e: e.tensor_copy(
        AP(W1bd, 0, [[2048, 64], [128, 16], [1, 64]]),
        AP(W1sb, 0, [[1024, 64], [64, 16], [1, 64]])))
    kb.run("vector", lambda e: e.tensor_copy(
        AP(W1bd, 64 * 2048 + 64, [[2048, 64], [128, 16], [1, 64]]),
        AP(W1sb, 64 * 1024, [[1024, 64], [64, 16], [1, 64]])), sV4)
    ctx["barrier"]()
    kb.tiny(False)
    if P4CUT <= 2:
        return

    ar2.reset()
    ar.restore(_stash_mark)
    bufs = {}
    bufs["x1f"] = ar2.alloc("x1f", [128, 16, 128], BF16)
    bufs["wv4"] = [ar2.alloc(f"wv4_{i}", [128, 2048], BF16) for i in range(2)]
    bufs["xqfm"] = ar2.alloc("xqfm", [128, 16, 128], BF16)
    bufs["xkfm"] = ar2.alloc("xkfm", [128, 16, 128], BF16)
    bufs["xvfm"] = ar2.alloc("xvfm", [128, 16, 128], BF16)
    bufs["tlfm"] = ar2.alloc("tlfm", [128, 16, 128], BF16)
    bufs["rp1"] = ar.alloc("rp1", [128, 512], F32)
    bufs["rppc"] = ar.alloc("rppc", [128, 512], BF16)
    bufs["lrS"] = ar.alloc("lrS", [128, 128], F32)
    bufs["lrtmp"] = ar.alloc("lrtmp", [128, 128], F32)
    bufs["Zf"] = ar.alloc("Zf4", [128, 512], F32)
    bufs["sqb"] = ar.alloc("sqb4", [128, 512], F32)
    bufs["xcb"] = ar.alloc("xcb4", [128, 512], BF16)
    bufs["m1b"] = ar.alloc("m1b4", [128, 512], BF16)
    bufs["g1b"] = ar.alloc("g1b4", [128, 512], BF16)
    bufs["gxh"] = ar.alloc("gxh4", [128, 512], F32)
    bufs["t1b"] = ar.alloc("t1b4", [128, 512], F32)
    bufs["t2b"] = ar.alloc("t2b4", [128, 512], BF16)
    bufs["gradb"] = ar.alloc("gradb4", [128, 512], BF16)
    bufs["lmask"] = ar.alloc("lmask4", [128, 256], BF16)
    bufs["lmb"] = [ar.alloc(f"lmb{i}", [128, 256], BF16) for i in range(4)]
    bufs["xk15b"] = [ar.alloc(f"xk15b{i}", [128, 512], BF16) for i in range(4)]
    bufs["xqAs"] = ar.alloc("xqAs", [128, 256], BF16)
    bufs["xqBs"] = ar.alloc("xqBs", [128, 256], BF16)
    bufs["tlsc"] = ar.alloc("tlsc4", [128, 512], BF16)
    bufs["xqlb"] = ar.alloc("xqlb4", [128, 512], F32)
    bufs["Zbf"] = ar.alloc("Zbf4", [128, 512], F32)
    bufs["ocb"] = ar.alloc("ocb4", [128, 512], BF16)
    bufs["o2b"] = ar.alloc("o2b4", [128, 512], BF16)
    bufs["outsc"] = ar.alloc("outsc4", [128, 512], F32)
    bufs["outfm"] = ar2.alloc("outfm4", [128, 16, 128], BF16)
    bufs["tin4"] = ar.alloc("tin4", [128, 48], F32)
    bufs["postmu"] = ar.alloc("postmu", [128, 1], F32)
    bufs["postr"] = ar.alloc("postr", [128, 1], F32)
    bufs["strow"] = ar.alloc("strow", [16, 2], F32)
    bufs["xsub"] = ar2.alloc("xsub", [128, 2048], F32)
    bufs["e1b"] = ar2.alloc("e1b", [128, 2048], F32)
    bufs["y2b"] = ar2.alloc("y2b", [128, 2048], F32)
    bufs["ztm"] = ar2.alloc("ztm", [128, 2048], BF16)
    bufs["zfm"] = bufs["x1f"]
    bufs["hftm"] = ar2.alloc("hftm", [128, 1024], BF16)
    bufs["hffm"] = ar2.alloc("hffm", [128, 8, 128], BF16)
    bufs["gsub"] = bufs["ztm"]
    bufs["xob"] = bufs["e1b"]
    sems = dict(sW4=sW4, sX4=sX4, sT4=sT4, sA4=sA4, sV4=sV4, sG4=sG4,
                sO4=sO4)
    stat = dict(lwB=lwB, lw2B=lw2B, lbB=lbB, b1B=b1B, W1bd=W1bd, W1sb=W1sb,
                negmtB=negmtB, tokB=tokB, ntokones=ntokones, lrwT=lrwT,
                lrbC=lrbC, lbfmB=lbfmB, lwfmB=lwfmB, cosB=cosB, sinpB=sinpB,
                ncsrow=ncsrow, pbowB=pbowB, b1frepl=b1frepl, b2frow=b2frow,
                I8sb=I8sb, J16s=J16s, S16s=S16s, bandmS=bandmS,
                ntok15=ntok15, ntok15m=ntok15m, ntokm4=ntokm4,
                halfmS=halfmS, blkmS=blkmS, I8s4s=I8s4s,
                onesrow_bf=onesrow_bf,
                onescol_bf4=onescol_bf4, murow=murow)

    nsteps_total = getattr(_emit_p4, "NCHUNK", 16)
    for k in range(nsteps_total):
        _emit_p4_chunk(ctx, k, bufs, sems, stat)

    for e in ENGINES:
        kb.wait(e, sO4, sO4.n)


def _p4tap(ctx, name, src_ap, shape, dtype=F32, eng=None):
    d = ctx["dbg_dram"](name, shape, dtype)
    if d is None:
        return
    kb = ctx["kb"]
    if "_dbsem" not in ctx:
        ctx["_dbsem"] = kb.sem("dbsem")
        ctx["_dbflush"] = kb.sem("dbflush")
    if eng is not None:
        kb.run(eng, lambda e: e.engine_nop(), ctx["_dbflush"])
    for sm in list(kb._sems.values()):
        if sm.n > 0:
            kb.wait("gpsimd", sm, sm.n)
    kb.dma("gpsimd", d.ap(), src_ap, ctx["_dbsem"])
    ctx["barrier"]()


def _emit_p4_chunk(ctx, k, bufs, sems, stat):
    nc, kb, dr = ctx["nc"], ctx["kb"], ctx["dr"]
    pbS, pbT, pbF, pbGU, pbX = (ctx["pbS"], ctx["pbT"], ctx["pbF"],
                                ctx["pbGU"], ctx["pbX"])
    idbfs = ctx["idbfs"]
    c = ctx
    b = bufs
    s = stat
    sW4, sX4, sT4, sA4 = sems["sW4"], sems["sX4"], sems["sT4"], sems["sA4"]
    sV4, sG4, sO4 = sems["sV4"], sems["sG4"], sems["sO4"]
    ZW, BW = 1024, 1536
    QKV_BANKS = [(pbGU, ZW, 2048), (pbGU, BW, 2048), (pbF, 0, 512), (pbX, 0, 512)]

    if k > 0:
        kb.wait("sync", sT4, ctx["p4_x1_free"])
    tx1 = kb.dma("sync", b["x1f"][:],
                 AP(c["x1c_st_d"], k * 128 * C, [[C, 128], [1, C]]), sX4)
    kb.wait("tensor", sX4, tx1)

    def bank_sl(nt):
        t, off, _ = QKV_BANKS[nt]
        return t[:, off:off + 512]

    def rope_evac(dst_fm, do_rope, tag):
        for nt in range(4):
            t, off, stride = QKV_BANKS[nt]
            ps = t[:, off:off + 512]
            kb.wait("vector", sT4, ctx.get("rppc_free", 0))
            if do_rope:
                cos_ap = AP(s["cosB"], 0, [[64, 128], [0, 8], [1, 64]])
                kb.run("vector", lambda e, ps=ps, cos_ap=cos_ap: e.tensor_tensor(
                    b["rp1"][:], ps, cos_ap, ALU.mult))
                rot_ap = AP(t, off + 32,
                            [[stride, 128], [64, 8], [-32, 2], [1, 32]])
                sin_ap = AP(s["sinpB"], 0, [[64, 128], [0, 8], [1, 64]])
                kb.run("vector", lambda e, rot_ap=rot_ap, sin_ap=sin_ap:
                       e.tensor_tensor(b["sqb"][:], rot_ap, sin_ap, ALU.mult))
                tv = kb.run("vector", lambda e: e.tensor_add(
                    b["rppc"][:], b["rp1"][:], b["sqb"][:]), sV4)
            else:
                tv = kb.run("vector", lambda e, ps=ps: e.tensor_copy(
                    b["rppc"][:], ps), sV4)
            kb.wait("tensor", sV4, tv)
            kb.wait("tensor", sA4, sA4.n)
            for i in range(4):
                kb.run("tensor", lambda e, i=i: e.transpose(
                    pbT[:, i * 128:(i + 1) * 128],
                    b["rppc"][:, i * 128:(i + 1) * 128], idbfs[:]),
                    sT4 if i == 3 else None)
            ctx["rppc_free"] = sT4.n
            kb.wait("scalar", sT4, sT4.n)
            kb.run("scalar", lambda e, nt=nt, dst_fm=dst_fm: e.copy(
                dst_fm[:, nt * 4:(nt + 1) * 4, :], pbT[:]), sA4)

    for wi, (stname, dst_fm, do_rope) in enumerate((
            ("qw_st", b["xqfm"], True),
            ("kw_st", b["xkfm"], True),
            ("vw_st", b["xvfm"], False))):
        std = c[stname + "_d"]
        for kc in range(16):
            wb = b["wv4"][kc % 2]
            key = (k, wi, kc - 2)
            if key in ctx.setdefault("p4_wv", {}):
                kb.wait("scalar", sT4, ctx["p4_wv"][key])
            elif kc < 2 and (k + wi) > 0:
                kb.wait("scalar", sT4, ctx["p4_wv_last"])
            tw = kb.dma("scalar", wb[:],
                        AP(std, kc * 128 * C, [[C, 128], [1, C]]), sW4)
            kb.wait("tensor", sW4, tw)
            if kc == 0:
                kb.wait("tensor", sV4, sV4.n)
            for nt in range(4):
                kb.run("tensor", lambda e, kc=kc, nt=nt, wb=wb: e.matmul(
                    bank_sl(nt), b["x1f"][:, kc, :],
                    wb[:, nt * 512:(nt + 1) * 512],
                    start=(kc == 0), stop=(kc == 15)),
                    sT4 if nt == 3 else None)
            ctx["p4_wv"][(k, wi, kc)] = sT4.n
            ctx["p4_wv_last"] = sT4.n
        kb.wait("vector", sT4, sT4.n)
        rope_evac(dst_fm, do_rope, wi)

    kb.wait("tensor", sA4, sA4.n)
    for kc in range(16):
        kb.run("tensor", lambda e, kc=kc: e.matmul(
            pbS[:, 256:384], b["x1f"][:, kc, :], s["lrwT"][:, kc, :],
            start=(kc == 0), stop=(kc == 15)), sT4 if kc == 15 else None)
    ctx["p4_x1_free"] = sT4.n

    kb.wait("vector", sA4, sA4.n)
    kb.run("vector", lambda e: e.tensor_sub(
        b["tlfm"][:].rearrange("p a b -> p (a b)"),
        b["xkfm"][:].rearrange("p a b -> p (a b)"),
        b["xvfm"][:].rearrange("p a b -> p (a b)")))
    kb.run("vector", lambda e: e.tensor_tensor(
        b["tlfm"][:], b["tlfm"][:],
        AP(s["lbfmB"], 0, [[16, 128], [1, 16], [0, 128]]), ALU.add))
    kb.run("vector", lambda e: e.tensor_tensor(
        b["tlfm"][:], b["tlfm"][:],
        AP(s["lwfmB"], 0, [[16, 128], [1, 16], [0, 128]]), ALU.mult), sV4)

    kb.wait("vector", sT4, sT4.n)
    tv = kb.run("vector", lambda e: e.tensor_copy(
        b["rppc"][:, 0:128], pbS[:, 256:384]), sV4)
    kb.wait("tensor", sV4, tv)
    kb.wait("tensor", sA4, sA4.n)
    tt = kb.run("tensor", lambda e: e.transpose(
        pbT[:, 0:128], b["rppc"][:, 0:128], idbfs[:]), sT4)
    kb.wait("scalar", sT4, tt)
    ta = kb.run("scalar", lambda e: e.activation(
        b["lrS"][:], pbT[:, 0:128], AF.Sigmoid, bias=s["lrbC"][:]), sA4)
    kb.wait("vector", sA4, ta)
    kb.wait("tensor", sA4, ta)

    if k == 0:
        _p4tap(ctx, "x1f", b["x1f"][:].rearrange("p a b -> p (a b)"), (128, 2048))
        _p4tap(ctx, "xqfm", b["xqfm"][:].rearrange("p a b -> p (a b)"), (128, 2048))
        _p4tap(ctx, "xkfm", b["xkfm"][:].rearrange("p a b -> p (a b)"), (128, 2048))
        _p4tap(ctx, "xvfm", b["xvfm"][:].rearrange("p a b -> p (a b)"), (128, 2048))
        _p4tap(ctx, "tlfm", b["tlfm"][:].rearrange("p a b -> p (a b)"), (128, 2048))
        _p4tap(ctx, "lrS", b["lrS"][:], (128, 128))
        _p4tap(ctx, "W1bd0", s["W1bd"][:], (128, 2048))
        _p4tap(ctx, "b1B0", s["b1B"][:], (128, 512))

    if ctx["P4CUT"] <= 3:
        return
    for n in range(8 if ctx["P4CUT"] > 4 else 1):
        _emit_scan_step(ctx, k, n, bufs, sems, stat)
    if ctx["P4CUT"] <= 5:
        return

    _emit_p4_tail(ctx, k, bufs, sems, stat)


def _emit_scan_step(ctx, k, n, bufs, sems, stat):
    nc, kb = ctx["nc"], ctx["kb"]
    pbS, pbT, pbF, pbGU, pbX = (ctx["pbS"], ctx["pbT"], ctx["pbF"],
                                ctx["pbGU"], ctx["pbX"])
    idbfs = ctx["idbfs"]
    b = bufs
    s = stat
    sT4, sA4, sV4 = sems["sT4"], sems["sA4"], sems["sV4"]
    ZW, BW = 1024, 1536
    jsl = slice(n * 16, (n + 1) * 16)
    dbgme = (k == 0 and n == getattr(_emit_scan_step, "DBGN", 0))
    SCUT = getattr(_emit_scan_step, "SCUT", 99)

    def T16(src_fm, tag):
        kb.wait("tensor", sA4, sA4.n)
        for q in range(16):
            kb.run("tensor", lambda e, q=q, src_fm=src_fm: e.transpose(
                pbT[(q % 4) * 32:(q % 4) * 32 + 16,
                    (q // 4) * 128:(q // 4) * 128 + 128],
                src_fm[:, q, jsl], idbfs[:],
                tile_position=(0, (q % 4) * 32)),
                sT4 if q == 15 else None)
        kb.wait("vector", sT4, sT4.n)

    T16(b["xqfm"], 0)
    tv = kb.run("vector", lambda e: e.scalar_tensor_tensor(
        b["xqlb"][:], pbT[:], 1.0, s["lbB"][:], ALU.mult, ALU.add), sV4)
    ctx["pbT_free"] = tv
    kb.wait("tensor", sV4, tv)
    if dbgme:
        _p4tap(ctx, "xqlb", b["xqlb"][:], (128, 512))
    T16(b["tlfm"], 1)
    tv = kb.run("vector", lambda e: e.tensor_copy(b["tlsc"][:], pbT[:]), sV4)
    kb.wait("tensor", sV4, tv)
    if dbgme:
        _p4tap(ctx, "tlsc", b["tlsc"][:], (128, 512))
    T16(b["xkfm"], 2)
    for blk in range(4):
        kb.run("vector", lambda e, blk=blk: e.tensor_scalar_mul(
            b["xk15b"][blk][:], pbT[:], s["ntok15m"][:, blk:blk + 1]),
            sV4 if blk == 3 else None)
    kb.run("vector", lambda e: e.tensor_scalar_mul(
        b["xqAs"][:], AP(b["xqfm"], n * 16, [[2048, 128], [128, 16], [1, 16]]),
        s["halfmS"][:, 0:1]))
    tv = kb.run("vector", lambda e: e.tensor_scalar_mul(
        b["xqBs"][:], AP(b["xqfm"], n * 16, [[2048, 128], [128, 16], [1, 16]]),
        s["halfmS"][:, 1:2]), sV4)
    kb.wait("tensor", sV4, tv)
    if dbgme:
        _p4tap(ctx, "xk15", b["xk15b"][0][:], (128, 512))

    if SCUT <= 1:
        return
    for blkq in range(4):
        for i in range(4):
            q = blkq + 4 * i
            kb.run("tensor", lambda e, q=q, i=i, blkq=blkq: e.matmul(
                pbGU[blkq * 32:blkq * 32 + 16,
                     ZW + i * 128:ZW + i * 128 + 128],
                b["xkfm"][:, q, jsl], s["W1bd"][:, q * 128:(q + 1) * 128],
                start=True, stop=True,
                tile_position=(0, blkq * 32)),
                sT4 if (blkq == 3 and i == 3) else None)
    if SCUT <= 11:
        return
    for blkq in range(4):
        for i in range(4):
            q = blkq + 4 * i
            for abh in range(2):
                kb.run("tensor", lambda e, q=q, i=i, blkq=blkq, abh=abh: e.matmul(
                    pbS[blkq * 32:blkq * 32 + 16,
                        i * 64 + abh * 16:i * 64 + abh * 16 + 16],
                    b["xkfm"][:, q, jsl],
                    AP(b["xqAs"] if abh == 0 else b["xqBs"], q * 16,
                       [[256, 128], [1, 16]]),
                    start=True, stop=True,
                    tile_position=(0, blkq * 32)),
                    sT4 if (blkq == 3 and i == 3 and abh == 1) else None)
    ctx["p4_qkT_done"] = sT4.n
    if SCUT <= 12:
        return
    for blkq in range(4):
        for i in range(4):
            q = blkq + 4 * i
            kb.run("tensor", lambda e, q=q, i=i, blkq=blkq: e.matmul(
                pbGU[blkq * 32:blkq * 32 + 16,
                     BW + i * 128:BW + i * 128 + 128],
                b["xqfm"][:, q, jsl], s["W1bd"][:, q * 128:(q + 1) * 128],
                start=True, stop=True,
                tile_position=(0, blkq * 32)),
                sT4 if (blkq == 3 and i == 3) else None)
    if SCUT <= 13:
        return
    for blk in range(4):
        kb.run("tensor", lambda e, blk=blk: e.matmul(
            pbS[blk * 32:blk * 32 + 16, 384:392],
            AP(b["lrS"], n * 16, [[128, 128], [1, 16]]),
            s["I8s4s"][:, blk * 8:(blk + 1) * 8],
            start=True, stop=True, tile_position=(0, blk * 32)),
            sT4 if blk == 3 else None)
    tMM1 = sT4.n
    if SCUT <= 2:
        return

    kb.tiny(True)
    kb.wait("vector", sT4, tMM1)
    t8 = b["tin4"]
    kb.run("vector", lambda e: e.tensor_add(b["Zf"][:], pbGU[:, ZW:ZW + 512],
                                            s["b1B"][:]))
    if dbgme:
        _p4tap(ctx, "Zf", b["Zf"][:], (128, 512), eng="vector")
    kb.run("vector", lambda e: e.tensor_reduce(
        t8[:, 0:8], AP(b["Zf"], 0, [[512, 128], [64, 8], [1, 64]]),
        axis=AXL.X, op=ALU.add))
    kb.run("vector", lambda e: e.tensor_mul(b["sqb"][:], b["Zf"][:], b["Zf"][:]))
    kb.run("vector", lambda e: e.tensor_reduce(
        t8[:, 8:16], AP(b["sqb"], 0, [[512, 128], [64, 8], [1, 64]]),
        axis=AXL.X, op=ALU.add))
    kb.run("vector", lambda e: e.tensor_scalar_mul(t8[:, 0:8], t8[:, 0:8],
                                                   1.0 / 64.0))
    kb.run("vector", lambda e: e.tensor_mul(t8[:, 16:24], t8[:, 0:8], t8[:, 0:8]))
    kb.run("vector", lambda e: e.scalar_tensor_tensor(
        t8[:, 8:16], t8[:, 8:16], 1.0 / 64.0, t8[:, 16:24],
        ALU.mult, ALU.subtract))
    tv = kb.run("vector", lambda e: e.tensor_scalar_add(
        t8[:, 16:24], t8[:, 8:16], 1e-6), sV4)
    kb.wait("scalar", sV4, tv)
    ta = kb.run("scalar", lambda e: e.activation(
        t8[:, 24:32], t8[:, 16:24], AF.Sqrt), sA4)
    kb.wait("vector", sA4, ta)
    kb.run("vector", lambda e: e.reciprocal(t8[:, 32:40], t8[:, 24:32]))
    kb.run("vector", lambda e: e.tensor_tensor(
        b["xcb"][:], b["Zf"][:],
        AP(t8, 0, [[48, 128], [1, 8], [0, 64]]), ALU.subtract))
    kb.run("vector", lambda e: e.tensor_mul(b["m1b"][:], b["xcb"][:], s["lw2B"][:]))
    kb.run("vector", lambda e: e.tensor_tensor(
        b["g1b"][:], b["m1b"][:],
        AP(t8, 32, [[48, 128], [1, 8], [0, 64]]), ALU.mult))
    kb.run("vector", lambda e: e.tensor_add(b["gxh"][:], b["g1b"][:], b["tlsc"][:]))
    kb.run("vector", lambda e: e.tensor_reduce(
        t8[:, 16:24], AP(b["gxh"], 0, [[512, 128], [64, 8], [1, 64]]),
        axis=AXL.X, op=ALU.add))
    kb.run("vector", lambda e: e.tensor_mul(b["sqb"][:], b["gxh"][:], b["xcb"][:]))
    kb.run("vector", lambda e: e.tensor_reduce(
        t8[:, 24:32], AP(b["sqb"], 0, [[512, 128], [64, 8], [1, 64]]),
        axis=AXL.X, op=ALU.add))
    kb.run("vector", lambda e: e.tensor_mul(t8[:, 40:48], t8[:, 32:40],
                                            pbS[:, 384:392]))
    kb.run("vector", lambda e: e.tensor_scalar_mul(t8[:, 40:48], t8[:, 40:48],
                                                   1.0 / 64.0))
    kb.run("vector", lambda e: e.tensor_scalar_mul(t8[:, 16:24], t8[:, 16:24],
                                                   1.0 / 64.0))
    kb.run("vector", lambda e: e.tensor_mul(t8[:, 8:16], t8[:, 32:40],
                                            t8[:, 32:40]))
    kb.run("vector", lambda e: e.tensor_mul(t8[:, 8:16], t8[:, 8:16],
                                            t8[:, 24:32]))
    kb.run("vector", lambda e: e.tensor_scalar_mul(t8[:, 8:16], t8[:, 8:16],
                                                   1.0 / 64.0))
    kb.run("vector", lambda e: e.tensor_tensor(
        b["t1b"][:], b["gxh"][:],
        AP(t8, 16, [[48, 128], [1, 8], [0, 64]]), ALU.subtract))
    kb.run("vector", lambda e: e.tensor_tensor(
        b["t2b"][:], b["xcb"][:],
        AP(t8, 8, [[48, 128], [1, 8], [0, 64]]), ALU.mult))
    kb.run("vector", lambda e: e.tensor_sub(b["t1b"][:], b["t1b"][:], b["t2b"][:]))
    tgrad = kb.run("vector", lambda e: e.tensor_tensor(
        b["gradb"][:], b["t1b"][:],
        AP(t8, 40, [[48, 128], [1, 8], [0, 64]]), ALU.mult), sV4)
    kb.tiny(False)
    kb.run("vector", lambda e: e.scalar_tensor_tensor(
        b["lmask"][:], pbS[:, 0:256], 1.0, s["negmtB"][:],
        ALU.add, ALU.mult))
    for blk in range(4):
        kb.run("vector", lambda e, blk=blk: e.tensor_scalar_mul(
            b["lmb"][blk][:], b["lmask"][:], s["blkmS"][:, blk:blk + 1]),
            sV4 if blk == 3 else None)
    tmask = sV4.n
    if dbgme:
        _p4tap(ctx, "c4d", b["tin4"][:, 40:48], (128, 8), eng="vector")
        _p4tap(ctx, "gradb", b["gradb"][:], (128, 512))
        _p4tap(ctx, "lmask", b["lmask"][:], (128, 256))

    if SCUT <= 3:
        return
    kb.wait("tensor", sV4, tmask)
    for blkq in range(4):
        for i in range(4):
            for abh in range(2):
                kb.run("tensor", lambda e, i=i, blkq=blkq, abh=abh: e.matmul(
                    pbX[blkq * 32:blkq * 32 + 16,
                        i * 128 + abh * 64:i * 128 + abh * 64 + 64],
                    AP(b["lmb"][blkq], i * 64 + abh * 16,
                       [[256, 128], [1, 16]]),
                    AP(b["gradb"], i * 128 + abh * 64, [[512, 128], [1, 64]]),
                    start=True, stop=True,
                    tile_position=(0, blkq * 32)),
                    sT4 if (blkq == 3 and i == 3 and abh == 1) else None)
    for q in range(16):
        for abh in range(2):
            kb.run("tensor", lambda e, q=q, abh=abh: e.matmul(
                pbGU[abh * 64:abh * 64 + 64, q * 64:(q + 1) * 64],
                AP(b["xk15b"][q % 4], (q // 4) * 128 + abh * 64,
                   [[512, 128], [1, 64]]),
                AP(b["gradb"], (q // 4) * 128 + abh * 64, [[512, 128], [1, 64]]),
                start=True, stop=True,
                tile_position=(0, abh * 64)),
                sT4 if (q == 15 and abh == 1) else None)
    for blk in range(4):
        for g in range(4):
            kb.run("tensor", lambda e, blk=blk, g=g: e.matmul(
                pbF[blk * 32:blk * 32 + 16, g * 128:(g + 1) * 128],
                AP(s["ntokm4"], blk * 16, [[64, 128], [1, 16]]),
                AP(b["gradb"], g * 128, [[512, 128], [1, 128]]),
                start=True, stop=True,
                tile_position=(0, blk * 32)),
                sT4 if (blk == 3 and g == 3) else None)
    tMM2 = sT4.n
    if SCUT <= 4:
        return

    kb.tiny(True)
    kb.wait("vector", sT4, tMM2)
    kb.run("vector", lambda e: e.tensor_add(
        s["W1sb"][:], s["W1sb"][:], pbGU[:, 0:1024]))
    kb.run("vector", lambda e: e.tensor_add(b["Zbf"][:], pbGU[:, BW:BW + 512],
                                            s["b1B"][:]))
    kb.run("vector", lambda e: e.tensor_add(b["Zbf"][:], b["Zbf"][:],
                                            pbX[:, 0:512]))
    if dbgme:
        _p4tap(ctx, "Zbf", b["Zbf"][:], (128, 512), eng="vector")
    kb.run("vector", lambda e: e.tensor_add(s["b1B"][:], s["b1B"][:], pbF[:]))
    if dbgme:
        _p4tap(ctx, "b1Bn", s["b1B"][:], (128, 512), eng="vector")
    kb.run("vector", lambda e: e.tensor_reduce(
        t8[:, 0:8], AP(b["Zbf"], 0, [[512, 128], [64, 8], [1, 64]]),
        axis=AXL.X, op=ALU.add))
    kb.run("vector", lambda e: e.tensor_mul(b["sqb"][:], b["Zbf"][:], b["Zbf"][:]))
    kb.run("vector", lambda e: e.tensor_reduce(
        t8[:, 8:16], AP(b["sqb"], 0, [[512, 128], [64, 8], [1, 64]]),
        axis=AXL.X, op=ALU.add))
    kb.run("vector", lambda e: e.tensor_scalar_mul(t8[:, 0:8], t8[:, 0:8],
                                                   1.0 / 64.0))
    kb.run("vector", lambda e: e.tensor_mul(t8[:, 16:24], t8[:, 0:8], t8[:, 0:8]))
    kb.run("vector", lambda e: e.scalar_tensor_tensor(
        t8[:, 8:16], t8[:, 8:16], 1.0 / 64.0, t8[:, 16:24],
        ALU.mult, ALU.subtract))
    tv = kb.run("vector", lambda e: e.tensor_scalar_add(
        t8[:, 16:24], t8[:, 8:16], 1e-6), sV4)
    kb.wait("scalar", sV4, tv)
    ta = kb.run("scalar", lambda e: e.activation(
        t8[:, 24:32], t8[:, 16:24], AF.Sqrt), sA4)
    kb.wait("vector", sA4, ta)
    kb.run("vector", lambda e: e.reciprocal(t8[:, 32:40], t8[:, 24:32]))
    kb.run("vector", lambda e: e.tensor_tensor(
        b["ocb"][:], b["Zbf"][:],
        AP(t8, 0, [[48, 128], [1, 8], [0, 64]]), ALU.subtract))
    kb.run("vector", lambda e: e.tensor_mul(b["o2b"][:], b["ocb"][:], s["lwB"][:]))
    kb.run("vector", lambda e: e.tensor_tensor(
        b["o2b"][:], b["o2b"][:],
        AP(t8, 32, [[48, 128], [1, 8], [0, 64]]), ALU.mult))
    tout = kb.run("vector", lambda e: e.tensor_add(
        b["outsc"][:], b["o2b"][:], b["xqlb"][:]), sV4)
    if dbgme:
        _p4tap(ctx, "outsc", b["outsc"][:], (128, 512))
    kb.run("vector", lambda e: e.tensor_reduce(
        t8[:, 40:41], b["outsc"][:], axis=AXL.X, op=ALU.add))
    kb.run("vector", lambda e: e.tensor_mul(b["sqb"][:], b["outsc"][:],
                                            b["outsc"][:]))
    tst = kb.run("vector", lambda e: e.tensor_reduce(
        t8[:, 41:42], b["sqb"][:], axis=AXL.X, op=ALU.add), sV4)
    kb.wait("tensor", sV4, tst)
    kb.run("tensor", lambda e: e.matmul(
        pbS[0:16, 392:393], s["S16s"][:], t8[:, 40:41],
        start=True, stop=True))
    kb.run("tensor", lambda e: e.matmul(
        pbS[0:16, 393:394], s["S16s"][:], t8[:, 41:42],
        start=True, stop=True), sT4)
    kb.wait("vector", sT4, sT4.n)
    tv = kb.run("vector", lambda e: e.tensor_copy(
        b["strow"][:], pbS[0:16, 392:394]), sV4)
    kb.wait("tensor", sV4, tv)
    kb.run("tensor", lambda e: e.matmul(
        pbS[:, 394:395], s["J16s"][:], b["strow"][:, 0:1],
        start=True, stop=True))
    kb.run("tensor", lambda e: e.matmul(
        pbS[:, 395:396], s["J16s"][:], b["strow"][:, 1:2],
        start=True, stop=True), sT4)
    kb.wait("vector", sT4, sT4.n)
    kb.run("vector", lambda e: e.tensor_scalar_mul(
        t8[:, 42:43], pbS[:, 394:395], 1.0 / 2048.0))
    kb.run("vector", lambda e: e.tensor_mul(t8[:, 43:44], t8[:, 42:43],
                                            t8[:, 42:43]))
    kb.run("vector", lambda e: e.scalar_tensor_tensor(
        t8[:, 44:45], pbS[:, 395:396], 1.0 / 2048.0, t8[:, 43:44],
        ALU.mult, ALU.subtract))
    tv = kb.run("vector", lambda e: e.tensor_scalar_add(
        t8[:, 44:45], t8[:, 44:45], 1e-6), sV4)
    kb.wait("scalar", sV4, tv)
    ta = kb.run("scalar", lambda e: e.activation(
        t8[:, 45:46], t8[:, 44:45], AF.Sqrt), sA4)
    kb.wait("vector", sA4, ta)
    kb.run("vector", lambda e: e.reciprocal(t8[:, 46:47], t8[:, 45:46]))
    if n == 0:
        kb.run("vector", lambda e: e.tensor_mul(
            b["postmu"][:], t8[:, 42:43], s["bandmS"][:, 0:1]))
        kb.run("vector", lambda e: e.tensor_mul(
            b["postr"][:], t8[:, 46:47], s["bandmS"][:, 0:1]))
    else:
        kb.run("vector", lambda e, n=n: e.tensor_mul(
            t8[:, 47:48], t8[:, 42:43], s["bandmS"][:, n:n + 1]))
        kb.run("vector", lambda e: e.tensor_add(
            b["postmu"][:], b["postmu"][:], t8[:, 47:48]))
        kb.run("vector", lambda e, n=n: e.tensor_mul(
            t8[:, 47:48], t8[:, 46:47], s["bandmS"][:, n:n + 1]))
        kb.run("vector", lambda e: e.tensor_add(
            b["postr"][:], b["postr"][:], t8[:, 47:48]))
    tvc = kb.run("vector", lambda e: e.tensor_copy(
        b["ocb"][:], b["outsc"][:]), sV4)

    kb.wait("tensor", sV4, tvc)
    kb.wait("tensor", sA4, sA4.n)
    for g in range(4):
        kb.run("tensor", lambda e, g=g: e.transpose(
            pbT[:, g * 128:(g + 1) * 128],
            b["ocb"][:, g * 128:(g + 1) * 128], idbfs[:]),
            sT4 if g == 3 else None)
    kb.wait("scalar", sT4, sT4.n)
    for g in range(4):
        kb.run("scalar", lambda e, g=g, n=n: e.copy(
            AP(b["outfm"], (4 * g) * 128 + n * 16,
               [[16 * 128, 128], [128, 4], [1, 16]]),
            AP(pbT, g * 128, [[512, 128], [32, 4], [1, 16]])), sA4)

    kb.run("vector", lambda e: e.tensor_copy(
        AP(s["W1bd"], 0, [[2048, 64], [128, 16], [1, 64]]),
        AP(s["W1sb"], 0, [[1024, 64], [64, 16], [1, 64]])))
    kb.run("vector", lambda e: e.tensor_copy(
        AP(s["W1bd"], 64 * 2048 + 64, [[2048, 64], [128, 16], [1, 64]]),
        AP(s["W1sb"], 64 * 1024, [[1024, 64], [64, 16], [1, 64]])), sV4)
    kb.tiny(False)
    if dbgme:
        _p4tap(ctx, "W1bdn", s["W1bd"][:], (128, 2048), eng="vector")
    if k == 0 and n == 7:
        _p4tap(ctx, "outfm", b["outfm"][:].rearrange("p a b -> p (a b)"),
               (128, 2048))
        _p4tap(ctx, "postmu", b["postmu"][:], (128, 1), eng="vector")
        _p4tap(ctx, "postr", b["postr"][:], (128, 1), eng="vector")


def _emit_p4_tail(ctx, k, bufs, sems, stat):
    nc, kb, dr = ctx["nc"], ctx["kb"], ctx["dr"]
    pbS, pbT, pbF, pbGU, pbX = (ctx["pbS"], ctx["pbT"], ctx["pbF"],
                                ctx["pbGU"], ctx["pbX"])
    idbfs, mrstd = ctx["idbfs"], ctx["mrstd"]
    c = ctx
    b = bufs
    s = stat
    sW4, sX4, sT4, sA4 = sems["sW4"], sems["sX4"], sems["sT4"], sems["sA4"]
    sV4, sG4, sO4 = sems["sV4"], sems["sG4"], sems["sO4"]
    ZW, BW = 1024, 1536
    QKV_BANKS = [(pbGU, ZW, 2048), (pbGU, BW, 2048), (pbF, 0, 512), (pbX, 0, 512)]

    def bank_sl(w):
        t, off, _ = QKV_BANKS[w]
        return t[:, off:off + 512]

    t8 = b["tin4"]

    kb.tiny(True)
    kb.run("vector", lambda e: e.tensor_scalar_mul(
        b["ocb"][:, 0:1], b["postmu"][:], -1.0))
    tv = kb.run("vector", lambda e: e.engine_nop(), sV4)
    kb.wait("tensor", sV4, tv)
    kb.wait("tensor", sA4, sA4.n)
    tt = kb.run("tensor", lambda e: e.transpose(
        pbT[0:1, 0:128], b["ocb"][:, 0:1], idbfs[:]), sT4)
    kb.wait("scalar", sT4, tt)
    tmu = kb.run("scalar", lambda e: e.copy(s["murow"][:], pbT[0:1, 0:128]), sA4)
    kb.wait("tensor", sA4, tmu)

    kb.tiny(False)
    kb.wait("tensor", sV4, sV4.n)
    for q in range(16):
        wb = b["wv4"][q % 2]
        key = ("ow", k, q - 2)
        if key in ctx["p4_wv"]:
            kb.wait("scalar", sT4, ctx["p4_wv"][key])
        tw = kb.dma("scalar", wb[:],
                    AP(c["ow_st_d"], q * 128 * C, [[C, 128], [1, C]]), sW4)
        kb.wait("tensor", sW4, tw)
        for w in range(4):
            kb.run("tensor", lambda e, q=q, w=w, wb=wb: e.matmul(
                bank_sl(w), b["outfm"][:, q, :],
                wb[:, w * 512:(w + 1) * 512],
                start=(q == 0), stop=False),
                sT4 if w == 3 else None)
        ctx["p4_wv"][("ow", k, q)] = sT4.n
    for w in range(4):
        kb.run("tensor", lambda e, w=w: e.matmul(
            bank_sl(w), s["murow"][:], s["ncsrow"][0:1, w * 512:(w + 1) * 512],
            start=False, stop=True), sT4 if w == 3 else None)
    towmm = sT4.n

    tx = kb.dma("sync", b["xsub"][:],
                AP(dr["x"], k * 128 * C, [[C, 128], [1, C]]), sX4)
    kb.wait("vector", sT4, towmm)
    kb.wait("vector", sX4, sX4.n)
    for w in range(4):
        kb.run("vector", lambda e, w=w: e.scalar_tensor_tensor(
            b["e1b"][:, w * 512:(w + 1) * 512], bank_sl(w), b["postr"][:],
            s["pbowB"][:, w * 512:(w + 1) * 512], ALU.mult, ALU.add))
    if k == 0:
        _p4tap(ctx, "xsubd", b["xsub"][:], (128, 2048))
        _p4tap(ctx, "e1d", b["e1b"][:], (128, 2048), eng="vector")
    kb.tiny(True)
    kb.run("vector", lambda e, k=k: e.tensor_mul(
        t8[:, 0:1], mrstd[:, k, 0:1], mrstd[:, k, 1:2]))
    kb.run("vector", lambda e: e.tensor_scalar_mul(t8[:, 0:1], t8[:, 0:1], -1.0))
    kb.run("vector", lambda e, k=k: e.tensor_scalar(
        b["y2b"][:], b["xsub"][:], mrstd[:, k, 1:2], t8[:, 0:1],
        ALU.mult, ALU.add))
    tu = kb.run("vector", lambda e: e.tensor_add(
        b["y2b"][:], b["y2b"][:], b["e1b"][:]), sV4)
    kb.run("vector", lambda e: e.tensor_reduce(
        t8[:, 1:2], b["y2b"][:], axis=AXL.X, op=ALU.add))
    kb.run("vector", lambda e: e.scalar_tensor_tensor(
        b["e1b"][:], b["y2b"][:], 1.0, b["y2b"][:], ALU.mult, ALU.mult,
        accum_out=t8[:, 2:3]))
    kb.run("vector", lambda e: e.tensor_scalar_mul(t8[:, 1:2], t8[:, 1:2],
                                                   1.0 / 2048.0))
    kb.run("vector", lambda e: e.tensor_mul(t8[:, 3:4], t8[:, 1:2], t8[:, 1:2]))
    kb.run("vector", lambda e: e.scalar_tensor_tensor(
        t8[:, 2:3], t8[:, 2:3], 1.0 / 2048.0, t8[:, 3:4],
        ALU.mult, ALU.subtract))
    tv = kb.run("vector", lambda e: e.tensor_scalar_add(
        t8[:, 2:3], t8[:, 2:3], 1e-6), sV4)
    kb.wait("scalar", sV4, tv)
    ta = kb.run("scalar", lambda e: e.activation(
        t8[:, 3:4], t8[:, 2:3], AF.Sqrt), sA4)
    kb.wait("vector", sA4, ta)
    kb.run("vector", lambda e: e.reciprocal(t8[:, 4:5], t8[:, 3:4]))
    tz = kb.run("vector", lambda e: e.tensor_scalar(
        b["ztm"][:], b["y2b"][:], t8[:, 1:2], t8[:, 4:5],
        ALU.subtract, ALU.mult), sV4)
    if k == 0:
        _p4tap(ctx, "y2d", b["y2b"][:], (128, 2048))
        _p4tap(ctx, "ztmd", b["ztm"][:], (128, 2048))

    kb.tiny(False)
    kb.wait("tensor", sV4, tz)
    for g in range(4):
        kb.wait("tensor", sA4, sA4.n)
        for i in range(4):
            kb.run("tensor", lambda e, g=g, i=i: e.transpose(
                pbT[:, i * 128:(i + 1) * 128],
                b["ztm"][:, (g * 4 + i) * 128:(g * 4 + i + 1) * 128],
                idbfs[:]), sT4 if i == 3 else None)
        kb.wait("scalar", sT4, sT4.n)
        kb.run("scalar", lambda e, g=g: e.copy(
            b["zfm"][:, g * 4:(g + 1) * 4, :], pbT[:]), sA4)
    kb.wait("sync", sT4, sT4.n)
    tg = kb.dma("sync", b["gsub"][:],
                AP(c["g_st_d"], k * 128 * C, [[C, 128], [1, C]]), sX4)

    kb.wait("tensor", sA4, sA4.n)
    for kc in range(16):
        wb = b["wv4"][kc % 2]
        key = ("w1f", k, kc - 2)
        if key in ctx["p4_wv"]:
            kb.wait("scalar", sT4, ctx["p4_wv"][key])
        tw = kb.dma("scalar", wb[:, 0:1024],
                    AP(c["w1f_st_d"], kc * 128 * DH, [[DH, 128], [1, DH]]), sW4)
        kb.wait("tensor", sW4, tw)
        for w in range(2):
            kb.run("tensor", lambda e, kc=kc, w=w, wb=wb: e.matmul(
                pbGU[:, ZW + w * 512:ZW + (w + 1) * 512],
                b["zfm"][:, kc, :], wb[:, w * 512:(w + 1) * 512],
                start=(kc == 0), stop=(kc == 15)),
                sT4 if w == 1 else None)
        ctx["p4_wv"][("w1f", k, kc)] = sT4.n
    ctx["p4_x1_free"] = sT4.n
    kb.wait("vector", sT4, sT4.n)
    for w in range(2):
        kb.run("vector", lambda e, w=w: e.tensor_tensor(
            b["rp1"][:], pbGU[:, ZW + w * 512:ZW + (w + 1) * 512],
            s["b1frepl"][:, w * 512:(w + 1) * 512], ALU.add))
        kb.run("vector", lambda e, w=w: e.tensor_relu(
            b["hftm"][:, w * 512:(w + 1) * 512], b["rp1"][:]), sV4)
    if k == 0:
        _p4tap(ctx, "hftd", b["hftm"][:], (128, 1024), eng="vector")
    kb.wait("tensor", sV4, sV4.n)
    for g in range(2):
        kb.wait("tensor", sA4, sA4.n)
        for i in range(4):
            kb.run("tensor", lambda e, g=g, i=i: e.transpose(
                pbT[:, i * 128:(i + 1) * 128],
                b["hftm"][:, (g * 4 + i) * 128:(g * 4 + i + 1) * 128],
                idbfs[:]), sT4 if i == 3 else None)
        kb.wait("scalar", sT4, sT4.n)
        kb.run("scalar", lambda e, g=g: e.copy(
            b["hffm"][:, g * 4:(g + 1) * 4, :], pbT[:]), sA4)

    kb.wait("tensor", sA4, sA4.n)
    for i in range(8):
        wb = b["wv4"][i % 2]
        key = ("w2f", k, i - 2)
        if key in ctx["p4_wv"]:
            kb.wait("scalar", sT4, ctx["p4_wv"][key])
        tw = kb.dma("scalar", wb[:],
                    AP(c["w2f_st_d"], i * 128 * C, [[C, 128], [1, C]]), sW4)
        kb.wait("tensor", sW4, tw)
        for w in range(4):
            kb.run("tensor", lambda e, i=i, w=w, wb=wb: e.matmul(
                bank_sl(w), b["hffm"][:, i, :], wb[:, w * 512:(w + 1) * 512],
                start=(i == 0), stop=False), sT4 if w == 3 else None)
        ctx["p4_wv"][("w2f", k, i)] = sT4.n
    for w in range(4):
        kb.run("tensor", lambda e, w=w: e.matmul(
            bank_sl(w), s["onesrow_bf"][:],
            s["b2frow"][0:1, w * 512:(w + 1) * 512],
            start=False, stop=True), sT4 if w == 3 else None)
    kb.wait("vector", sT4, sT4.n)
    for w in range(4):
        kb.run("vector", lambda e, w=w: e.tensor_add(
            b["e1b"][:, w * 512:(w + 1) * 512], bank_sl(w),
            b["y2b"][:, w * 512:(w + 1) * 512]))
    if k == 0:
        _p4tap(ctx, "sfin", b["e1b"][:], (128, 2048), eng="vector")
    kb.wait("vector", sX4, tg)
    if k == 0:
        _p4tap(ctx, "gsubd", b["gsub"][:], (128, 2048))
    kb.run("vector", lambda e: e.tensor_mul(b["y2b"][:], b["e1b"][:],
                                            b["gsub"][:]))
    txo = kb.run("vector", lambda e: e.tensor_add(
        b["xob"][:], b["y2b"][:], b["xsub"][:]), sV4)
    kb.wait("sync", sV4, txo)
    kb.dma("sync", AP(c["out_d"], k * 128 * C, [[C, 128], [1, C]]),
           b["xob"][:], sO4)



_IN_NAMES = [
    ("x", "L0_attn_enc_input"), ("q_w", "L0_attn_q_w"), ("k_w", "L0_attn_k_w"),
    ("v_w", "L0_attn_v_w"), ("o_w", "L0_attn_o_w"),
    ("ln1_w", "L0_attn_ln1_w"), ("ln1_b", "L0_attn_ln1_b"),
    ("ttt_lr_w", "L0_attn_ttt_lr_w"), ("ttt_lr_b", "L0_attn_ttt_lr_b"),
    ("token_idx_off", "L0_attn_token_idx_off"),
    ("ttt_norm_w", "L0_attn_ttt_norm_w"), ("ttt_norm_b", "L0_attn_ttt_norm_b"),
    ("W1", "L0_attn_W1"), ("b1", "L0_attn_b1"),
    ("post_w", "L0_attn_post_w"), ("post_b", "L0_attn_post_b"),
    ("ffn_ln_w", "L0_attn_ffn_ln_w"), ("ffn_ln_b", "L0_attn_ffn_ln_b"),
    ("w1f", "L0_attn_w1f"), ("b1f", "L0_attn_b1f"),
    ("w2f", "L0_attn_w2f"), ("b2f", "L0_attn_b2f"),
    ("w_gate", "L0_ffn_w_gate"), ("w_up", "L0_ffn_w_up"),
    ("w_down", "L0_ffn_w_down"),
]


def kernel(**inputs):
    nc, _ = build_program()
    x = np.ascontiguousarray(np.asarray(inputs["L0_attn_enc_input"], np.float32))
    base = {k: np.ascontiguousarray(np.asarray(inputs[v], np.float32))
            for k, v in _IN_NAMES if k != "x"}
    in_maps = []
    for c in range(8):
        m = dict(base)
        m["x"] = x[c]
        in_maps.append(m)
    res = run_bass_kernel_spmd(nc, in_maps, core_ids=list(range(8)))
    return np.stack([res.results[c]["out"] for c in range(8)], axis=0)
